# version 06
# baseline (speedup 1.0000x reference)
import contextlib

import numpy as np

import concourse.bass as bass
import concourse.mybir as mybir
from concourse.bass import IndirectOffsetOnAxis
from concourse.bass_utils import run_bass_kernel_spmd

F32 = mybir.dt.float32
BF16 = mybir.dt.bfloat16
U16 = mybir.dt.uint16
U32 = mybir.dt.uint32
ALU = mybir.AluOpType
AF = mybir.ActivationFunctionType

B, S, D, R, NC, NK, SD, KK = 8, 4096, 1024, 128, 64, 32768, 64, 8
TT = S // 128
DCH = D // 128
EPS_LN = 1e-5
EPS_W = 1e-5
EPS_NW = 1e-8
RSQRT_R = float(1.0 / np.sqrt(np.float64(R)))

_DEBUG_TAPS = []
N_P4 = [TT]


class Sem:
    def __init__(self, ctx, nc, name):
        self.h = ctx.enter_context(nc.semaphore(name))
        self.n = 0

    def inc(self, inst, k=1):
        inst.then_inc(self.h, k)
        self.n += k
        return self.n

    def incd(self, inst):
        return self.inc(inst, 16)

    def wait(self, eng, n=None):
        eng.wait_ge(self.h, self.n if n is None else n)


def build_program():
    nc = bass.Bass("TRN2", target_bir_lowering=False, debug=False)

    X = nc.dram_tensor("x", [S, D], F32, kind="ExternalInput").ap()
    A_ = nc.dram_tensor("A", [SD, SD], F32, kind="ExternalInput").ap()
    BM = nc.dram_tensor("Bm", [D, SD], F32, kind="ExternalInput").ap()
    WI = nc.dram_tensor("Wimp", [D, SD], F32, kind="ExternalInput").ap()
    WR = nc.dram_tensor("Wrout", [NC, D], F32, kind="ExternalInput").ap()
    CP = nc.dram_tensor("comp", [NC, D, R], F32, kind="ExternalInput").ap()
    KD = nc.dram_tensor("K", [NK, R], F32, kind="ExternalInput").ap()
    VD = nc.dram_tensor("V", [NK, D], F32, kind="ExternalInput").ap()
    WF = nc.dram_tensor("Wf", [D, D], F32, kind="ExternalInput").ap()
    OUT = nc.dram_tensor("out", [S, D], F32, kind="ExternalOutput").ap()
    BASE = nc.dram_tensor("base", [S, D], F32, kind="Internal").ap()
    LDR = nc.dram_tensor("Ldr", [S], F32, kind="Internal").ap()
    ZDR = nc.dram_tensor("Zdr", [S], F32, kind="Internal").ap()
    IMD = nc.dram_tensor("imd", [S], BF16, kind="Internal").ap()

    taps = {}
    for name in _DEBUG_TAPS:
        shape, dt = _TAP_SHAPES[name]
        taps[name] = nc.dram_tensor("tap_" + name, shape, dt, kind="ExternalOutput").ap()

    ctx = contextlib.ExitStack()

    at = nc.alloc_sbuf_tensor_at
    B0 = 16512
    KT = at("KT", [128, NK], BF16, offset=B0).ap()
    wstg = at("wstg", [128, 4096], F32, offset=B0).ap()
    negt = at("negt", [128, 8192], BF16, offset=B0 + 16384).ap()
    posx = at("posx", [128, 4096], BF16, offset=B0 + 32768).ap()
    uT = at("uT", [64, S], BF16, offset=B0 + 40960).ap()
    WfT = at("WfT", [128, DCH * D], BF16, offset=B0 + 49152).ap()
    big = at("big", [128, NK], BF16, offset=82048).ap()
    scq = at("scq", [128, 2 * 8192], BF16, offset=82048).ap()
    Vg = at("Vg", [128, KK * D], F32, offset=82048 + 32768).ap()
    QT = at("QT", [128, S], BF16, offset=147584).ap()
    kst = at("kst", [128, 256], F32, offset=147584).ap()
    junk = at("junk", [128, D], F32, offset=B0).ap()
    imr = at("imr", [64, S], BF16, offset=B0).ap()
    UB = 155776
    bsb = at("bsb", [128, 2 * D], F32, offset=UB).ap()
    Apow = at("Apow", [64, 64 * SD], BF16, offset=UB + 8192).ap()
    e_tp = at("e_tp", [64, S], BF16, offset=UB + 8192).ap()
    cbuf = at("cbuf", [128, 2048], F32, offset=UB + 8192).ap()
    A64p = at("A64p", [64, 64 * SD], BF16, offset=UB + 16384).ap()
    Lst = at("Lst", [1, 512], F32, offset=UB + 16384).ap()
    WiT = at("WiT", [64, D], F32, offset=UB + 18432).ap()
    SCm = at("SCm", [128, DCH * 128], F32, offset=UB + 16384).ap()
    SCb = at("SCb", [128, DCH * 128], BF16, offset=UB + 20480).ap()
    o = 180352

    def tail(name, shape, dt):
        nonlocal o
        t = at(name, shape, dt, offset=o)
        per = int(np.prod(shape[1:])) * mybir.dt.size(dt)
        o += (per + 31) // 32 * 32
        return t.ap()

    xb = tail("xb", [128, 2 * D], F32)
    xnt = tail("xnt", [128, D], BF16)
    Abf = tail("Abf", [64, SD], BF16)
    Bmb = tail("Bmb", [128, DCH * SD], BF16)
    WrT = tail("WrT", [128, DCH * NC], BF16)
    idb = tail("idb", [128, 128], BF16)
    idf = tail("idf", [128, 128], F32)
    onesf = tail("onesf", [1, 128], F32)
    ones128 = tail("ones128", [128, 1], F32)
    ones64 = tail("ones64", [64, 1], BF16)
    stats = tail("stats", [128, 16], F32)
    scal = tail("scal", [1, 8], F32)
    srep = tail("srep", [128, 2], F32)
    hp = tail("hp", [128, DCH], F32)
    hpb = tail("hpb", [128, DCH], BF16)
    Tm = tail("Tm", [64, SD], BF16)
    inT = tail("inT", [64, SD + 2], BF16)
    hvec = tail("hvec", [64, 1], F32)
    nwacc = tail("nwacc", [64, 1], F32)
    nw64 = tail("nw64", [64, 1], F32)
    nwrep = tail("nwrep", [128, NC], F32)
    L2d = tail("L2d", [128, 32], F32)
    eL2d = tail("eL2d", [128, 32], F32)
    Z2d = tail("Z2d", [128, 32], F32)
    rz2d = tail("rz2d", [128, 32], F32)
    im2d = tail("im2d", [128, 32], BF16)
    rsum = tail("rsum", [128, 1], F32)
    io32 = tail("io32", [128, 32], F32)
    hb32 = tail("hb32", [128, 32], F32)
    cman = tail("cman", [128, 32], BF16)
    cmi = tail("cmi", [128, 32], U16)
    cgi = tail("cgi", [128, 32], F32)
    gv = tail("gv", [128, 8], BF16)
    gvf = tail("gvf", [128, 8], F32)
    pos = tail("pos", [128, 8], U16)
    posf = tail("posf", [128, 8], F32)
    gixf = tail("gixf", [128, 8], F32)
    gixu = tail("gixu", [128, 8], U32)
    e8 = tail("e8", [128, 8], F32)
    w8 = tail("w8", [128, 8], F32)
    sm1 = tail("sm1", [128, 2], F32)
    nmax = tail("nmax", [128, 1], F32)
    d32 = tail("d32", [128, 32], F32)
    ckon = tail("ckon", [128, 2], F32)
    assert o <= 196576 - 32, f"sbuf tail overflow: {o}"

    psA = ctx.enter_context(nc.psum_tensor("psA", [128, 3072], F32))
    psB = ctx.enter_context(nc.psum_tensor("psB", [128, 1024], BF16))
    psD = ctx.enter_context(nc.psum_tensor("psD", [128, 512], F32))
    bank = lambda b: psA[:, 512 * b:512 * (b + 1)]

    s_pe = Sem(ctx, nc, "s_pe")
    s_dve = Sem(ctx, nc, "s_dve")
    s_act = Sem(ctx, nc, "s_act")
    s_ld = Sem(ctx, nc, "s_ld")
    s_pl = Sem(ctx, nc, "s_pl")
    s_out = Sem(ctx, nc, "s_out")
    s_x2 = [Sem(ctx, nc, "s_x0"), Sem(ctx, nc, "s_x1")]
    s_k2 = [Sem(ctx, nc, "s_k0"), Sem(ctx, nc, "s_k1")]
    s_c2 = [Sem(ctx, nc, "s_c0"), Sem(ctx, nc, "s_c1")]
    s_ctl = Sem(ctx, nc, "s_ctl")

    blk = ctx.enter_context(nc.Block())

    def ctl_dma(out_ap, in_ap, waits=()):
        pre = s_ctl.n

        def _c(g, out_ap=out_ap, in_ap=in_ap, waits=tuple(waits), pre=pre):
            g.wait_ge(s_ctl.h, pre)
            for sem, cnt in waits:
                g.wait_ge(sem.h, cnt)
            i = g.dma_start(out=out_ap, in_=in_ap)
            s_ctl.incd(i)
        blk.gpsimd(_c)
        return s_ctl.n

    def xnT(c):
        return big[:, c * S:(c + 1) * S]

    def _maybe_tap(name, src_ap):
        if name not in taps:
            return
        dn = s_dve.n
        an = s_act.n
        pn = s_pe.n

        def t_dma(g, name=name, src_ap=src_ap, dn=dn, an=an, pn=pn):
            g.wait_ge(s_dve.h, dn)
            g.wait_ge(s_act.h, an)
            g.wait_ge(s_pe.h, pn)
            i = g.dma_start(out=taps[name][:], in_=src_ap)
            s_pl.incd(i)
        blk.gpsimd(t_dma)

    def c_pool(g):
        g.memset(idb[:], 0.0)
        g.affine_select(out=idb[:], in_=idb[:], compare_op=ALU.not_equal,
                        fill=1.0, base=0, pattern=[[-1, 128]], channel_multiplier=1)
        g.memset(idf[:], 0.0)
        g.affine_select(out=idf[:], in_=idf[:], compare_op=ALU.not_equal,
                        fill=1.0, base=0, pattern=[[-1, 128]], channel_multiplier=1)
        g.iota(out=io32[:], pattern=[[1, 32]], base=0, channel_multiplier=0,
               allow_small_or_imprecise_dtypes=True)
        g.iota(out=hb32[:], pattern=[[8192, 4], [0, 8]], base=0,
               channel_multiplier=0, allow_small_or_imprecise_dtypes=True)
        g.memset(ones64[:], 1.0)
        g.memset(onesf[:], 1.0)
        g.memset(ones128[:], 1.0)
        g.memset(ckon[:, 0:1], EPS_LN)
        i = g.memset(ckon[:, 1:2], RSQRT_R)
        s_pl.inc(i)
    blk.gpsimd(c_pool)
    pl_consts = s_pl.n

    ctl_dma(wstg[0:64, 0:64], A_[:, :])
    ld_small = ctl_dma(wstg[:, 64:64 + DCH * SD].rearrange("p (c e) -> p c e", c=DCH),
                       BM[:, :].rearrange("(c p) e -> p c e", c=DCH))

    def c_cast(v, ld_small=ld_small):
        v.wait_ge(s_ctl.h, ld_small)
        v.wait_ge(s_pl.h, pl_consts)
        v.tensor_copy(out=Abf[:], in_=wstg[0:64, 0:64])
        i = v.tensor_copy(out=Bmb[:], in_=wstg[:, 64:64 + DCH * SD])
        s_dve.inc(i)
    blk.vector(c_cast)
    cast_done = s_dve.n

    def act_preload(s):
        from concourse.hw_specs import get_activation_tables
        need = {AF.Ln, AF.Exp, AF.Identity, AF.Copy}
        for idx, funcs in enumerate(get_activation_tables(nc.m.arch).values()):
            if need.issubset(funcs):
                s.add_instruction(mybir.InstLoadActFuncSet(
                    name=nc.get_next_instruction_name(),
                    act_func_set_id=idx, ins=[], outs=[]))
                return
        raise ValueError("no table set covers all activation funcs")
    blk.scalar(act_preload)

    habs = []
    for h in range(2):
        def wfa_ld(sy, h=h):
            if h == 0:
                sy.wait_ge(s_dve.h, cast_done)
            else:
                sy.wait_ge(s_dve.h, s_dve.n)
            i = sy.dma_start(out=wstg[:].rearrange("p (c i) -> p c i", c=4),
                             in_=WF[:, :].rearrange("(c p) i -> p c i", c=DCH)
                             [:, 4 * h:4 * (h + 1), :])
            s_ld.incd(i)
        blk.sync(wfa_ld)
        ldh = s_ld.n

        def wf_abs(v, h=h, ldh=ldh):
            v.wait_ge(s_ld.h, ldh)
            i = v.reduce_sum(out=stats[:, h:h + 1], in_=wstg[:],
                             axis=mybir.AxisListType.X, apply_absolute_value=True)
            s_dve.inc(i)
        blk.vector(wf_abs)
        habs.append(s_dve.n)

    def wf_sc(v):
        v.wait_ge(s_dve.h, habs[1])
        i = v.tensor_tensor(out=stats[:, 2:3], in0=stats[:, 0:1],
                            in1=stats[:, 1:2], op=ALU.add)
        s_dve.inc(i)
    blk.vector(wf_sc)
    dv1 = s_dve.n

    def wfs_mm(t, dv1=dv1):
        t.wait_ge(s_dve.h, dv1)
        i = t.matmul(out=psD[0:1, 432:433], lhsT=stats[:, 2:3],
                     rhs=ones128[:], start=True, stop=True)
        s_pe.inc(i)
    blk.tensor(wfs_mm)
    pe1 = s_pe.n

    def wfs_fin(v, pe1=pe1):
        v.wait_ge(s_pe.h, pe1)
        i = v.tensor_scalar(out=scal[0:1, 5:6], in0=psD[0:1, 432:433],
                            scalar1=1.0 / (D * D), scalar2=EPS_W,
                            op0=ALU.mult, op1=ALU.max)
        n1 = s_dve.inc(i)
        v.wait_ge(s_dve.h, n1)
        i = v.reciprocal(out=scal[0:1, 6:7], in_=scal[0:1, 5:6])
        s_dve.inc(i)
    blk.vector(wfs_fin)
    dv1 = s_dve.n

    def srep_mm(t, dv1=dv1):
        t.wait_ge(s_dve.h, dv1)
        t.matmul(out=psD[:, 433:434], lhsT=onesf[:], rhs=scal[0:1, 5:6],
                 start=True, stop=True)
        i = t.matmul(out=psD[:, 434:435], lhsT=onesf[:], rhs=scal[0:1, 6:7],
                     start=True, stop=True)
        s_pe.inc(i)
    blk.tensor(srep_mm)
    pe1 = s_pe.n

    def srep_ev(v, pe1=pe1):
        v.wait_ge(s_pe.h, pe1)
        v.tensor_scalar(out=srep[:, 0:1], in0=psD[:, 433:434], scalar1=-1.0,
                        scalar2=None, op0=ALU.mult)
        i = v.tensor_copy(out=srep[:, 1:2], in_=psD[:, 434:435])
        s_dve.inc(i)
    blk.vector(srep_ev)
    srep_ready = s_dve.n

    for h in range(2):
        def wft_ld(sy, h=h):
            sy.wait_ge(s_dve.h, s_dve.n)
            i = sy.dma_start(out=wstg[:].rearrange("p (c i) -> p c i", c=4),
                             in_=WF[:, :].rearrange("(c p) i -> p c i", c=DCH)
                             [:, 4 * h:4 * (h + 1), :])
            s_ld.incd(i)
        blk.sync(wft_ld)
        ldh = s_ld.n

        def wf_tern(v, h=h, ldh=ldh):
            v.wait_ge(s_ld.h, ldh)
            v.wait_ge(s_dve.h, srep_ready)
            i = v.tensor_scalar(out=wstg[:], in0=wstg[:], scalar1=srep[:, 1:2],
                                scalar2=None, op0=ALU.mult)
            n1 = s_dve.inc(i)
            v.wait_ge(s_dve.h, n1)
            i = v.tensor_scalar(out=posx[:], in0=wstg[:], scalar1=0.5,
                                scalar2=None, op0=ALU.is_gt)
            n2 = s_dve.inc(i)
            v.wait_ge(s_dve.h, n2)
            i = v.scalar_tensor_tensor(out=negt[:, h * 4096:(h + 1) * 4096],
                                       in0=wstg[:], scalar=-0.5, in1=posx[:],
                                       op0=ALU.is_lt, op1=ALU.subtract)
            s_dve.inc(i)
        blk.vector(wf_tern)
    tern_ready = s_dve.n

    for ci in range(DCH):
        def wft_tr(t, ci=ci):
            t.wait_ge(s_dve.h, tern_ready)
            if ci > 0:
                t.wait_ge(s_act.h, s_act.n)
            for co in range(DCH):
                i = t.transpose(out=psB[:, co * 128:(co + 1) * 128],
                                in_=negt[:, co * 1024 + ci * 128:co * 1024 + ci * 128 + 128],
                                identity=idb[:])
            s_pe.inc(i)
        blk.tensor(wft_tr)
        pe1 = s_pe.n

        def wft_ev(s, ci=ci, pe1=pe1):
            s.wait_ge(s_pe.h, pe1)
            i = s.activation(out=WfT[:, ci * 1024:(ci + 1) * 1024], in_=psB[:],
                             func=AF.Copy)
            s_act.inc(i)
        blk.scalar(wft_ev)
    wfT_ready = s_act.n


    kev_bank = {4: 0, 5: 0}
    ffn_bank_free = {0: 0, 1: 0, 2: 0, 3: 0}
    base_done = {}
    ktr_pe = {}
    kt_emitted = []

    def emit_ktile(kt, ev_waits=()):
        kb = 4 + (kt // 4) % 2
        slot = kt % 4
        kpar = len(kt_emitted) % 2
        kt_emitted.append(kt)

        def k_ld(sy, kt=kt, kpar=kpar,
                 slot_gate=(ktr_pe[kt_emitted[-3]] if len(kt_emitted) >= 3 else 0)):
            sy.wait_ge(s_dve.h, tern_ready)
            if slot_gate:
                sy.wait_ge(s_pe.h, slot_gate)
            i = sy.dma_start(out=kst[:, kpar * 128:(kpar + 1) * 128],
                             in_=KD[kt * 128:(kt + 1) * 128, :])
            s_k2[kpar].incd(i)
        blk.sync(k_ld)
        ld_k = s_k2[kpar].n

        def k_tr(t, kb=kb, slot=slot, kpar=kpar, ld_k=ld_k,
                 gate=kev_bank[4 + (kt // 4) % 2]):
            t.wait_ge(s_k2[kpar].h, ld_k)
            if slot == 0 and gate:
                t.wait_ge(s_act.h, gate)
            i = t.transpose(out=bank(kb)[:, slot * 128:(slot + 1) * 128],
                            in_=kst[:, kpar * 128:(kpar + 1) * 128],
                            identity=idf[:])
            s_pe.inc(i)
        blk.tensor(k_tr)
        ktr_pe[kt] = s_pe.n

        if slot == 3:
            pe_k = s_pe.n

            def k_ev(s, kt=kt, kb=kb, pe_k=pe_k, ev_waits=tuple(ev_waits)):
                s.wait_ge(s_pe.h, pe_k)
                for sem, cnt in ev_waits:
                    s.wait_ge(sem.h, cnt)
                base_n = (kt - 3) * 128
                i = s.activation(out=KT[:, base_n:base_n + 512],
                                 in_=bank(kb), func=AF.Copy)
                s_act.inc(i)
            blk.scalar(k_ev)
            kev_bank[kb] = s_act.n

    prev_evac = 0
    norm_done = {}
    base_done = {}

    def p1_A(tt):
        par = tt % 2
        sb = 8 * par

        def p1_ld(sy, tt=tt, par=par):
            if tt == 0:
                sy.wait_ge(s_pl.h, pl_consts)
            if tt >= 2:
                sy.wait_ge(s_dve.h, base_done[tt - 2])
            i = sy.dma_start(out=xb[:, par * D:(par + 1) * D],
                             in_=X[tt * 128:(tt + 1) * 128, :])
            s_x2[par].incd(i)
        blk.sync(p1_ld)
        ld_n = s_x2[par].n

        def p1_stats(v, par=par, sb=sb, ld_n=ld_n):
            v.wait_ge(s_x2[par].h, ld_n)
            xt = xb[:, par * D:(par + 1) * D]
            i = v.tensor_scalar(out=junk[:], in0=xt, scalar1=1.0 / D, scalar2=None,
                                op0=ALU.mult, op1=ALU.add, accum_out=stats[:, sb:sb + 1])
            i = v.scalar_tensor_tensor(out=junk[:], in0=xt, scalar=1.0 / D, in1=xt,
                                       op0=ALU.mult, op1=ALU.mult,
                                       accum_out=stats[:, sb + 2:sb + 3])
            n2 = s_dve.inc(i)
            v.wait_ge(s_dve.h, n2)
            i = v.tensor_scalar(out=stats[:, sb + 1:sb + 2], in0=stats[:, sb:sb + 1],
                                scalar1=-1.0, scalar2=None, op0=ALU.mult)
            i = v.scalar_tensor_tensor(out=stats[:, sb + 3:sb + 4],
                                       in0=stats[:, sb:sb + 1], scalar=1.0,
                                       in1=stats[:, sb:sb + 1],
                                       op0=ALU.mult, op1=ALU.mult)
            n3 = s_dve.inc(i)
            v.wait_ge(s_dve.h, n3)
            i = v.tensor_tensor(out=stats[:, sb + 4:sb + 5], in0=stats[:, sb + 2:sb + 3],
                                in1=stats[:, sb + 3:sb + 4], op=ALU.subtract)
            s_dve.inc(i)
        blk.vector(p1_stats)
        return s_dve.n

    def p1_B(tt, dve_var):
        par = tt % 2
        sb = 8 * par

        def p1_ln(s, sb=sb, dve_var=dve_var):
            s.wait_ge(s_dve.h, dve_var)
            i = s.activation(out=stats[:, sb + 5:sb + 6], in_=stats[:, sb + 4:sb + 5],
                             func=AF.Ln, bias=ckon[:, 0:1])
            s_act.inc(i)
        blk.scalar(p1_ln)
        act_ln = s_act.n

        def p1_half(v, sb=sb, act_ln=act_ln):
            v.wait_ge(s_act.h, act_ln)
            i = v.tensor_scalar(out=stats[:, sb + 6:sb + 7], in0=stats[:, sb + 5:sb + 6],
                                scalar1=-0.5, scalar2=None, op0=ALU.mult)
            s_dve.inc(i)
        blk.vector(p1_half)
        dve_half = s_dve.n

        def p1_exp(s, sb=sb, dve_half=dve_half):
            s.wait_ge(s_dve.h, dve_half)
            i = s.activation(out=stats[:, sb + 5:sb + 6], in_=stats[:, sb + 6:sb + 7],
                             func=AF.Exp)
            s_act.inc(i)
        blk.scalar(p1_exp)
        act_rstd = s_act.n

        def p1_rstd(v, sb=sb, act_rstd=act_rstd):
            v.wait_ge(s_act.h, act_rstd)
            i = v.scalar_tensor_tensor(out=stats[:, sb + 7:sb + 8],
                                       in0=stats[:, sb + 1:sb + 2], scalar=1.0,
                                       in1=stats[:, sb + 5:sb + 6],
                                       op0=ALU.mult, op1=ALU.mult)
            s_dve.inc(i)
        blk.vector(p1_rstd)
        dve_rstd = s_dve.n

        def p1_norm(s, par=par, sb=sb, dve_rstd=dve_rstd, prev=prev_evac):
            s.wait_ge(s_dve.h, dve_rstd)
            if prev:
                s.wait_ge(s_pe.h, prev)
            xt = xb[:, par * D:(par + 1) * D]
            i = s.activation(out=xnt[:], in_=xt, func=AF.Identity,
                             bias=stats[:, sb + 7:sb + 8], scale=stats[:, sb + 5:sb + 6])
            s_act.inc(i)
        blk.scalar(p1_norm)
        norm_done[tt] = s_act.n
        return s_act.n

    def p1_C(tt, act_norm):
        par = tt % 2

        def p1_tr(t, act_norm=act_norm):
            t.wait_ge(s_act.h, act_norm)
            t.wait_ge(s_dve.h, s_dve.n)
            for c in range(DCH):
                i = t.transpose(out=psB[:, c * 128:(c + 1) * 128],
                                in_=xnt[:, c * 128:(c + 1) * 128], identity=idb[:])
            s_pe.inc(i)
        blk.tensor(p1_tr)
        pe_tr = s_pe.n

        def p1_ev(v, tt=tt, pe_tr=pe_tr):
            v.wait_ge(s_pe.h, pe_tr)
            dst = big[:].rearrange("p (c s) -> p c s", c=DCH)[:, :, tt * 128:(tt + 1) * 128]
            i = v.tensor_copy(out=dst, in_=psB[:].rearrange("p (c s) -> p c s", c=DCH))
            s_dve.inc(i)
        blk.vector(p1_ev)
        ev_n = s_dve.n

        for half in range(2):
            b = 2 * par + half

            def f_mm(t, tt=tt, half=half, b=b, ev_n=ev_n,
                     gate=ffn_bank_free[2 * par + half]):
                t.wait_ge(s_act.h, wfT_ready)
                t.wait_ge(s_dve.h, ev_n)
                if gate:
                    t.wait_ge(s_dve.h, gate)
                for c in range(DCH):
                    i = t.matmul(out=bank(b),
                                 lhsT=xnT(c)[:, tt * 128:(tt + 1) * 128],
                                 rhs=WfT[:, c * 1024 + half * 512:c * 1024 + half * 512 + 512],
                                 start=(c == 0), stop=(c == DCH - 1))
                s_pe.inc(i)
            blk.tensor(f_mm)
        pe_f = s_pe.n

        def f_base(v, tt=tt, par=par, pe_f=pe_f):
            v.wait_ge(s_pe.h, pe_f)
            for half in range(2):
                b = 2 * par + half
                i = v.scalar_tensor_tensor(
                    out=bsb[:, par * D + half * 512:par * D + half * 512 + 512],
                    in0=bank(b), scalar=srep[:, 0:1],
                    in1=xb[:, par * D + half * 512:par * D + half * 512 + 512],
                    op0=ALU.mult, op1=ALU.add)
            s_dve.inc(i)
        blk.vector(f_base)
        base_done[tt] = s_dve.n
        ffn_bank_free[2 * par] = s_dve.n
        ffn_bank_free[2 * par + 1] = s_dve.n

        def f_st(sy, tt=tt, par=par, dn=base_done[tt]):
            sy.wait_ge(s_dve.h, dn)
            i = sy.dma_start(out=BASE[tt * 128:(tt + 1) * 128, :],
                             in_=bsb[:, par * D:(par + 1) * D])
            s_out.incd(i)
        blk.sync(f_st)

        for q in range(3):
            emit_ktile(64 + tt * 3 + q)
        return pe_tr

    stageA = {}
    stageB = {}
    for it in range(TT + 2):
        if it >= 2:
            prev_evac = p1_C(it - 2, stageB[it - 2])
        if it < TT:
            stageA[it] = p1_A(it)
        if 1 <= it <= TT:
            stageB[it - 1] = p1_B(it - 1, stageA[it - 1])
    xnT_ready = s_dve.n
    ffn_pe_done = s_pe.n

    def p2_seed(v):
        v.wait_ge(s_dve.h, cast_done)
        v.tensor_copy(out=Apow[:, 0:64], in_=idb[0:64, 0:64])
        v.tensor_copy(out=A64p[:, 0:64], in_=idb[0:64, 0:64])
        i = v.tensor_copy(out=Apow[:, 64:128], in_=Abf[:])
        s_dve.inc(i)
    blk.vector(p2_seed)
    def emit_dbl(pow_ap, rounds):
        for m, lo, hi in rounds:
            w = hi - lo

            def r_tr(t, pow_ap=pow_ap, m=m):
                t.wait_ge(s_dve.h, s_dve.n)
                i = t.transpose(out=psB[0:64, 0:64],
                                in_=pow_ap[:, 64 * m:64 * (m + 1)],
                                identity=idb[0:64, 0:64])
                s_pe.inc(i)
            blk.tensor(r_tr)
            pe1 = s_pe.n

            def r_ev(v, pe1=pe1):
                v.wait_ge(s_pe.h, pe1)
                i = v.tensor_copy(out=Tm[:], in_=psB[0:64, 0:64])
                s_dve.inc(i)
            blk.vector(r_ev)
            dv1 = s_dve.n

            def r_mm(t, pow_ap=pow_ap, m=m, lo=lo, w=w, dv1=dv1):
                t.wait_ge(s_dve.h, dv1)
                i = t.matmul(out=psD[0:64, 0:64 * w], lhsT=Tm[:],
                             rhs=pow_ap[:, 64 * lo:64 * (lo + w)],
                             start=True, stop=True)
                s_pe.inc(i)
            blk.tensor(r_mm)
            pe2 = s_pe.n

            def r_ev2(v, pow_ap=pow_ap, m=m, lo=lo, w=w, pe2=pe2):
                v.wait_ge(s_pe.h, pe2)
                i = v.tensor_copy(out=pow_ap[:, 64 * (m + lo):64 * (m + lo + w)],
                                  in_=psD[0:64, 0:64 * w])
                s_dve.inc(i)
            blk.vector(r_ev2)

    ROUNDS = [(1, 1, 2), (2, 1, 3), (4, 1, 5), (8, 1, 8), (8, 8, 9),
              (16, 1, 9), (16, 9, 16), (16, 16, 17),
              (32, 1, 9), (32, 9, 17), (32, 17, 25), (32, 25, 32)]
    emit_dbl(Apow, ROUNDS)

    def a64_mm(t):
        t.wait_ge(s_dve.h, s_dve.n)
        i = t.matmul(out=psD[0:64, 0:64], lhsT=Tm[:], rhs=Apow[:, 64 * 32:64 * 33],
                     start=True, stop=True)
        s_pe.inc(i)
    blk.tensor(a64_mm)
    pe1 = s_pe.n

    def a64_ev(v, pe1=pe1):
        v.wait_ge(s_pe.h, pe1)
        i = v.tensor_copy(out=A64p[:, 64:128], in_=psD[0:64, 0:64])
        s_dve.inc(i)
    blk.vector(a64_ev)
    emit_dbl(A64p, ROUNDS)
    _maybe_tap("Apow", Apow[:, :])

    for st in range(8):
        bk = st % 2

        def u_mm(t, st=st, bk=bk):
            t.wait_ge(s_dve.h, xnT_ready)
            if st >= 2:
                t.wait_ge(s_act.h, s_act.n)
            for c in range(DCH):
                i = t.matmul(out=bank(bk)[0:64, :],
                             lhsT=Bmb[:, c * SD:(c + 1) * SD],
                             rhs=xnT(c)[:, st * 512:(st + 1) * 512],
                             start=(c == 0), stop=(c == DCH - 1))
            s_pe.inc(i)
        blk.tensor(u_mm)
        pe1 = s_pe.n

        def u_ev(s, st=st, bk=bk, pe1=pe1):
            s.wait_ge(s_pe.h, pe1)
            i = s.activation(out=uT[:, st * 512:(st + 1) * 512],
                             in_=bank(bk)[0:64, :], func=AF.Copy)
            s_act.inc(i)
        blk.scalar(u_ev)
    uT_ready = s_act.n
    _maybe_tap("uT", uT[:, :])

    def in_mm(t):
        t.wait_ge(s_act.h, uT_ready)
        t.wait_ge(s_dve.h, s_dve.n)
        uv = uT[:].rearrange("p (c i) -> p i c", i=64)
        for i_ in range(64):
            i = t.matmul(out=psD[0:64, 0:64],
                         lhsT=Apow[:, 64 * (63 - i_):64 * (64 - i_)],
                         rhs=uv[:, i_, :], start=(i_ == 0), stop=(i_ == 63))
        s_pe.inc(i)
    blk.tensor(in_mm)
    pe1 = s_pe.n

    def in_ev(v, pe1=pe1):
        v.wait_ge(s_pe.h, pe1)
        i = v.tensor_copy(out=inT[:, 0:64], in_=psD[0:64, 0:64])
        s_dve.inc(i)
    blk.vector(in_ev)
    dv1 = s_dve.n

    def out_mm(t, dv1=dv1):
        t.wait_ge(s_dve.h, dv1)
        for c in range(64):
            i = t.matmul(out=psD[0:64, 64:65],
                         lhsT=A64p[:, 64 * (63 - c):64 * (64 - c)],
                         rhs=inT[:, c:c + 1], start=(c == 0), stop=(c == 63))
        s_pe.inc(i)
    blk.tensor(out_mm)
    pe_outer = s_pe.n

    def h_ev(v, pe_outer=pe_outer):
        v.wait_ge(s_pe.h, pe_outer)
        i = v.tensor_copy(out=hvec[:], in_=psD[0:64, 64:65])
        s_dve.inc(i)
    blk.vector(h_ev)
    _maybe_tap("hvec", hvec[:, :])

    ld_wi = ctl_dma(wstg[:, 0:DCH * SD].rearrange("p (c e) -> p c e", c=DCH),
                    WI[:, :].rearrange("(c p) e -> p c e", c=DCH),
                    waits=[(s_dve, xnT_ready)])
    for c in range(DCH):
        def wi_tr(t, c=c, ld_wi=ld_wi):
            t.wait_ge(s_ctl.h, ld_wi)
            t.wait_ge(s_dve.h, s_dve.n)
            i = t.transpose(out=psD[0:64, 128:256],
                            in_=wstg[:, c * SD:(c + 1) * SD],
                            identity=idf[:])
            s_pe.inc(i)
        blk.tensor(wi_tr)
        pe1 = s_pe.n

        def wi_ev(v, c=c, pe1=pe1):
            v.wait_ge(s_pe.h, pe1)
            i = v.tensor_copy(out=WiT[:, c * 128:(c + 1) * 128],
                              in_=psD[0:64, 128:256])
            s_dve.inc(i)
        blk.vector(wi_ev)
    dv1 = s_dve.n

    def hp_mm(t, dv1=dv1):
        t.wait_ge(s_dve.h, dv1)
        for c in range(DCH):
            i = t.matmul(out=psD[:, 256 + c:257 + c],
                         lhsT=WiT[:, c * 128:(c + 1) * 128], rhs=hvec[:],
                         start=True, stop=True)
        s_pe.inc(i)
    blk.tensor(hp_mm)
    pe1 = s_pe.n

    def hp_ev(v, pe1=pe1):
        v.wait_ge(s_pe.h, pe1)
        v.tensor_copy(out=hp[:], in_=psD[:, 256:256 + DCH])
        i = v.tensor_copy(out=hpb[:], in_=psD[:, 256:256 + DCH])
        s_dve.inc(i)
    blk.vector(hp_ev)
    hp_ready = s_dve.n

    lst_free = 0
    for st in range(8):
        def L_mm(t, st=st):
            t.wait_ge(s_dve.h, s_dve.n if st else hp_ready)
            for c in range(DCH):
                i = t.matmul(out=psD[0:1, :], lhsT=hpb[:, c:c + 1],
                             rhs=xnT(c)[:, st * 512:(st + 1) * 512],
                             start=(c == 0), stop=(c == DCH - 1))
            s_pe.inc(i)
        blk.tensor(L_mm)
        pe1 = s_pe.n

        def L_ev(v, pe1=pe1, lst_free=lst_free):
            v.wait_ge(s_pe.h, pe1)
            v.wait_ge(s_ctl.h, lst_free)
            i = v.tensor_copy(out=Lst[:], in_=psD[0:1, :])
            s_dve.inc(i)
        blk.vector(L_ev)
        dv1 = s_dve.n
        lst_free = ctl_dma(LDR[st * 512:(st + 1) * 512], Lst[0:1, :],
                           waits=[(s_dve, dv1)])

    ld_L2d = ctl_dma(L2d[:], LDR[:].rearrange("(p q) -> p q", p=128))

    def eL_a(s, ld_L2d=ld_L2d):
        s.wait_ge(s_ctl.h, ld_L2d)
        i = s.activation(out=eL2d[:], in_=L2d[:], func=AF.Exp)
        s_act.inc(i)
    blk.scalar(eL_a)
    act1 = s_act.n

    def eL_sum(v, act1=act1):
        v.wait_ge(s_act.h, act1)
        i = v.tensor_scalar(out=d32[:], in0=eL2d[:], scalar1=1.0,
                            scalar2=None, op0=ALU.mult, op1=ALU.add,
                            accum_out=stats[:, 0:1])
        s_dve.inc(i)
    blk.vector(eL_sum)
    dv1 = s_dve.n

    def els_mm(t, dv1=dv1):
        t.wait_ge(s_dve.h, dv1)
        i = t.matmul(out=psD[0:1, 300:301], lhsT=stats[:, 0:1],
                     rhs=ones128[:], start=True, stop=True)
        s_pe.inc(i)
    blk.tensor(els_mm)
    pe1 = s_pe.n

    def els_r(v, pe1=pe1):
        v.wait_ge(s_pe.h, pe1)
        i = v.reciprocal(out=scal[0:1, 0:1], in_=psD[0:1, 300:301])
        s_dve.inc(i)
    blk.vector(els_r)
    dv1 = s_dve.n

    def els_rep(t, dv1=dv1):
        t.wait_ge(s_dve.h, dv1)
        i = t.matmul(out=psD[:, 301:302], lhsT=onesf[:], rhs=scal[0:1, 0:1],
                     start=True, stop=True)
        s_pe.inc(i)
    blk.tensor(els_rep)
    pe1 = s_pe.n

    def els_rep_ev(v, pe1=pe1):
        v.wait_ge(s_pe.h, pe1)
        i = v.tensor_copy(out=rsum[:], in_=psD[:, 301:302])
        s_dve.inc(i)
    blk.vector(els_rep_ev)
    rsum_ready = s_dve.n

    ld_wr = ctl_dma(wstg[0:64, 512:512 + D], WR[:, :])

    def wr_cast(s, ld_wr=ld_wr):
        s.wait_ge(s_ctl.h, ld_wr)
        s.wait_ge(s_pe.h, s_pe.n)
        i = s.activation(out=e_tp[:, 0:1024], in_=wstg[0:64, 512:512 + D],
                         func=AF.Copy)
        s_act.inc(i)
    blk.scalar(wr_cast)
    act1 = s_act.n

    for c in range(DCH):
        def wr_tr(t, c=c, act1=act1):
            t.wait_ge(s_act.h, act1)
            t.wait_ge(s_dve.h, s_dve.n)
            i = t.transpose(out=psB[:, 0:64], in_=e_tp[:, c * 128:(c + 1) * 128],
                            identity=idb[0:64, 0:64])
            s_pe.inc(i)
        blk.tensor(wr_tr)
        pe1 = s_pe.n

        def wr_ev(v, c=c, pe1=pe1):
            v.wait_ge(s_pe.h, pe1)
            i = v.tensor_copy(out=WrT[:, c * NC:(c + 1) * NC], in_=psB[:, 0:64])
            s_dve.inc(i)
        blk.vector(wr_ev)
    wrT_ready = s_dve.n

    for st in range(8):
        bk = 2 + st % 2

        def tp_mm(t, st=st, bk=bk):
            t.wait_ge(s_dve.h, wrT_ready)
            if st >= 2:
                t.wait_ge(s_act.h, s_act.n)
            for c in range(DCH):
                i = t.matmul(out=bank(bk)[0:64, :], lhsT=WrT[:, c * NC:(c + 1) * NC],
                             rhs=xnT(c)[:, st * 512:(st + 1) * 512],
                             start=(c == 0), stop=(c == DCH - 1))
            s_pe.inc(i)
        blk.tensor(tp_mm)
        pe1 = s_pe.n

        def tp_ev(s, st=st, bk=bk, pe1=pe1):
            s.wait_ge(s_pe.h, pe1)
            s.wait_ge(s_dve.h, s_dve.n)
            i = s.activation(out=e_tp[:, st * 512:(st + 1) * 512],
                             in_=bank(bk)[0:64, :], func=AF.Exp)
            s_act.inc(i)
        blk.scalar(tp_ev)
    etp_ready = s_act.n

    for st in range(8):
        def z_mm(t, st=st):
            t.wait_ge(s_act.h, etp_ready)
            t.wait_ge(s_dve.h, s_dve.n)
            i = t.matmul(out=psD[0:1, :], lhsT=ones64[:],
                         rhs=e_tp[:, st * 512:(st + 1) * 512], start=True, stop=True)
            s_pe.inc(i)
        blk.tensor(z_mm)
        pe1 = s_pe.n

        def z_ev(v, pe1=pe1, lst_free=lst_free):
            v.wait_ge(s_pe.h, pe1)
            v.wait_ge(s_ctl.h, lst_free)
            i = v.tensor_copy(out=Lst[:], in_=psD[0:1, :])
            s_dve.inc(i)
        blk.vector(z_ev)
        dv1 = s_dve.n
        lst_free = ctl_dma(ZDR[st * 512:(st + 1) * 512], Lst[0:1, :],
                           waits=[(s_dve, dv1)])

    ld_z = ctl_dma(Z2d[:], ZDR[:].rearrange("(p q) -> p q", p=128))

    def z_fin(v, ld_z=ld_z):
        v.wait_ge(s_ctl.h, ld_z)
        v.wait_ge(s_dve.h, rsum_ready)
        i = v.reciprocal(out=rz2d[:], in_=Z2d[:])
        n1 = s_dve.inc(i)
        v.wait_ge(s_dve.h, n1)
        i = v.tensor_scalar(out=im2d[:],
                            in0=eL2d[:], scalar1=rsum[:, 0:1], scalar2=None,
                            op0=ALU.mult)
        n2 = s_dve.inc(i)
        v.wait_ge(s_dve.h, n2)
        i = v.tensor_tensor(out=im2d[:], in0=im2d[:], in1=rz2d[:], op=ALU.mult)
        s_dve.inc(i)
    blk.vector(z_fin)
    dv1 = s_dve.n

    ctl_dma(IMD[:].rearrange("(p q) -> p q", p=128), im2d[:],
            waits=[(s_dve, dv1)])
    ld_imr = ctl_dma(imr[:],
                     IMD[:].rearrange("(a s) -> a s", a=1).to_broadcast([64, S]),
                     waits=[(s_pe, pe_outer)])
    _maybe_tap("im2d", im2d[:, :])

    def nw_acc(v, ld_imr=ld_imr):
        v.wait_ge(s_ctl.h, ld_imr)
        v.wait_ge(s_act.h, etp_ready)
        i = v.scalar_tensor_tensor(out=e_tp[:], in0=e_tp[:], scalar=1.0, in1=imr[:],
                                   op0=ALU.mult, op1=ALU.mult, accum_out=nwacc[:])
        n1 = s_dve.inc(i)
        v.wait_ge(s_dve.h, n1)
        i = v.tensor_copy(out=inT[:, 64:65], in_=nwacc[:])
        s_dve.inc(i)
    blk.vector(nw_acc)
    dv1 = s_dve.n

    def nws_mm(t, dv1=dv1):
        t.wait_ge(s_dve.h, dv1)
        i = t.matmul(out=psD[0:1, 302:303], lhsT=inT[:, 64:65], rhs=ones64[:],
                     start=True, stop=True)
        s_pe.inc(i)
    blk.tensor(nws_mm)
    pe1 = s_pe.n

    def nws_fin(v, pe1=pe1):
        v.wait_ge(s_pe.h, pe1)
        i = v.tensor_scalar(out=scal[0:1, 2:3], in0=psD[0:1, 302:303],
                            scalar1=EPS_NW, scalar2=None, op0=ALU.add)
        n1 = s_dve.inc(i)
        v.wait_ge(s_dve.h, n1)
        i = v.reciprocal(out=scal[0:1, 3:4], in_=scal[0:1, 2:3])
        s_dve.inc(i)
    blk.vector(nws_fin)
    dv1 = s_dve.n

    def nwr_rep(t, dv1=dv1):
        t.wait_ge(s_dve.h, dv1)
        i = t.matmul(out=psD[0:64, 303:304], lhsT=onesf[:, 0:64],
                     rhs=scal[0:1, 3:4], start=True, stop=True)
        s_pe.inc(i)
    blk.tensor(nwr_rep)
    pe1 = s_pe.n

    def nw_fin(v, pe1=pe1):
        v.wait_ge(s_pe.h, pe1)
        i = v.scalar_tensor_tensor(out=nw64[:], in0=nwacc[:], scalar=1.0,
                                   in1=psD[0:64, 303:304], op0=ALU.mult, op1=ALU.mult)
        s_dve.inc(i)
    blk.vector(nw_fin)
    dv1 = s_dve.n
    _maybe_tap("nw64", nw64[:, :])

    def nwt_tr(t, dv1=dv1):
        t.wait_ge(s_dve.h, dv1)
        i = t.transpose(out=psD[0:1, 304:368], in_=nw64[:], identity=idf[0:64, 0:64])
        s_pe.inc(i)
    blk.tensor(nwt_tr)
    pe1 = s_pe.n

    def nwrow_ev(v, pe1=pe1, lst_free=lst_free):
        v.wait_ge(s_pe.h, pe1)
        v.wait_ge(s_ctl.h, lst_free)
        i = v.tensor_copy(out=Lst[0:1, 0:64], in_=psD[0:1, 304:368])
        s_dve.inc(i)
    blk.vector(nwrow_ev)
    dv1 = s_dve.n

    def nwrep_mm(t, dv1=dv1):
        t.wait_ge(s_dve.h, dv1)
        i = t.matmul(out=psD[:, 368:432], lhsT=onesf[:], rhs=Lst[0:1, 0:64],
                     start=True, stop=True)
        s_pe.inc(i)
    blk.tensor(nwrep_mm)
    pe1 = s_pe.n

    def nwrep_ev(v, pe1=pe1):
        v.wait_ge(s_pe.h, pe1)
        i = v.tensor_copy(out=nwrep[:], in_=psD[:, 368:432])
        s_dve.inc(i)
    blk.vector(nwrep_ev)
    nwrep_ready = s_dve.n


    ffn_pe_done = s_pe.n

    kdef = list(range(0, 64)) + list(range(160, 256))
    kdef_waits = [(s_pe, s_pe.n), (s_dve, s_dve.n)]
    sc_cons = {}
    for n in range(NC):
        slot = n % 2

        def sc_ld(sy, n=n, slot=slot):
            if n < 2:
                sy.wait_ge(s_dve.h, nwrep_ready)
                sy.wait_ge(s_ctl.h, ld_imr)
                sy.wait_ge(s_act.h, etp_ready)
            else:
                sy.wait_ge(s_dve.h, sc_cons[n - 2])
            i = sy.dma_start(out=cbuf[:, slot * 1024:(slot + 1) * 1024]
                             .rearrange("p (q r) -> p q r", q=DCH),
                             in_=CP[n, :, :].rearrange("(q p) r -> p q r", q=DCH))
            s_c2[slot].incd(i)
        blk.sync(sc_ld)
        ld_c = s_c2[slot].n

        def sc_acc(v, n=n, slot=slot, ld_c=ld_c):
            v.wait_ge(s_c2[slot].h, ld_c)
            if n:
                v.wait_ge(s_dve.h, s_dve.n)
            for c in range(DCH):
                src = cbuf[:, slot * 1024 + c * 128:slot * 1024 + (c + 1) * 128]
                dst = SCm[:, c * 128:(c + 1) * 128]
                if n == 0:
                    i = v.tensor_scalar(out=dst, in0=src, scalar1=nwrep[:, n:n + 1],
                                        scalar2=None, op0=ALU.mult)
                else:
                    i = v.scalar_tensor_tensor(out=dst, in0=src,
                                               scalar=nwrep[:, n:n + 1], in1=dst,
                                               op0=ALU.mult, op1=ALU.add)
            s_dve.inc(i)
        blk.vector(sc_acc)
        sc_cons[n] = s_dve.n
        emit_ktile(kdef[2 * n], kdef_waits)
        emit_ktile(kdef[2 * n + 1], kdef_waits)

    for kt in kdef[2 * NC:]:
        emit_ktile(kt, kdef_waits)
    kt_ready = s_act.n

    def sc_cast(v):
        v.wait_ge(s_dve.h, s_dve.n)
        i = v.tensor_copy(out=SCb[:], in_=SCm[:])
        s_dve.inc(i)
    blk.vector(sc_cast)
    scb_ready = s_dve.n
    _maybe_tap("SCm", SCm[:, :])

    qt_bank_free = {0: 0, 1: 0}
    for st in range(8):
        bk = st % 2

        def q_mm(t, st=st, bk=bk, gate=qt_bank_free[st % 2]):
            t.wait_ge(s_dve.h, scb_ready)
            if gate:
                t.wait_ge(s_act.h, gate)
            for c in range(DCH):
                i = t.matmul(out=bank(bk), lhsT=SCb[:, c * 128:(c + 1) * 128],
                             rhs=xnT(c)[:, st * 512:(st + 1) * 512],
                             start=(c == 0), stop=(c == DCH - 1))
            s_pe.inc(i)
        blk.tensor(q_mm)
        pe1 = s_pe.n

        def q_ev(s, st=st, bk=bk, pe1=pe1):
            s.wait_ge(s_pe.h, pe1)
            i = s.activation(out=QT[:, st * 512:(st + 1) * 512], in_=bank(bk),
                             func=AF.Copy, scale=ckon[:, 1:2])
            s_act.inc(i)
        blk.scalar(q_ev)
        qt_bank_free[bk] = s_act.n
    qt_ready = s_act.n
    _maybe_tap("QT", QT[:, :])

    s_oe = Sem(ctx, nc, "s_oe")
    s_oo = Sem(ctx, nc, "s_oo")
    opar = [s_oe, s_oo]
    sc_bank_free = [0] * 6
    half_read = {}
    wsum_done = {}
    w8_ready = {}
    gidx_ready = {}
    nmax_n = {}
    exp_n = {}
    pool_ready = {}
    base_all = s_out.n

    def emit_wsum(tt):
        def d_ws(v, tt=tt):
            par = tt % 2
            acc = xb[:, par * D:(par + 1) * D]
            v.wait_ge(s_act.h, exp_n[tt])
            i = v.tensor_scalar(out=d32[:, 0:8], in0=e8[:], scalar1=1.0,
                                scalar2=None, op0=ALU.mult, op1=ALU.add,
                                accum_out=sm1[:, 0:1])
            n1 = s_dve.inc(i)
            v.wait_ge(s_dve.h, n1)
            i = v.reciprocal(out=sm1[:, 1:2], in_=sm1[:, 0:1])
            n2 = s_dve.inc(i)
            v.wait_ge(s_dve.h, n2)
            i = v.tensor_scalar(out=w8[:], in0=e8[:], scalar1=sm1[:, 1:2],
                                scalar2=None, op0=ALU.mult)
            w8_ready[tt] = s_dve.inc(i)
            v.wait_ge(s_dve.h, w8_ready[tt])
            v.wait_ge(s_pl.h, pool_ready[tt])
            v.wait_ge(opar[tt % 2].h, opar[tt % 2].n)
            for k in range(KK):
                if k:
                    v.wait_ge(s_dve.h, s_dve.n)
                i = v.scalar_tensor_tensor(
                    out=acc, in0=Vg[:].rearrange("p (k d) -> p k d", k=KK)[:, k, :],
                    scalar=w8[:, k:k + 1],
                    in1=(bsb[:, par * D:(par + 1) * D] if k == 0 else acc),
                    op0=ALU.mult, op1=ALU.add)
                s_dve.inc(i)
        blk.vector(d_ws)
        wsum_done[tt] = s_dve.n

    def emit_out(tt):
        def p_out(g, tt=tt, wn=wsum_done[tt]):
            g.wait_ge(s_dve.h, wn)
            par = tt % 2
            i = g.dma_start(out=OUT[tt * 128:(tt + 1) * 128, :],
                            in_=xb[:, par * D:(par + 1) * D])
            opar[par].incd(i)
        blk.gpsimd(p_out)

    NP4 = N_P4[0]
    for tt in range(NP4):

        for j in range(64):
            b = j % 6
            q = j // 16
            qs = q % 2

            def s_mm(t, tt=tt, j=j, b=b, gate=sc_bank_free[j % 6]):
                if tt == 0 and j == 0:
                    t.wait_ge(s_act.h, qt_ready)
                    t.wait_ge(s_act.h, kt_ready)
                if gate:
                    t.wait_ge(s_act.h, gate)
                i = t.matmul(out=bank(b), lhsT=QT[:, tt * 128:(tt + 1) * 128],
                             rhs=KT[:, j * 512:(j + 1) * 512], start=True, stop=True)
                s_pe.inc(i)
            blk.tensor(s_mm)
            mm_n = s_pe.n

            war = (half_read.get((tt, q - 2), 0) if q >= 2
                   else half_read.get((tt - 1, q + 2), 0))

            def s_ev(s, tt=tt, j=j, b=b, q=q, qs=qs, mm_n=mm_n, war=war):
                s.wait_ge(s_pe.h, mm_n)
                if war:
                    s.wait_ge(s_dve.h, war)
                i = s.activation(out=scq[:, qs * 8192 + (j % 16) * 512:
                                         qs * 8192 + (j % 16) * 512 + 512],
                                 in_=bank(b), func=AF.Copy)
                s_act.inc(i)
            blk.scalar(s_ev)
            sc_bank_free[b] = s_act.n

            if j == 15 and tt > 0:
                def a_exp(s, p=tt - 1):
                    s.wait_ge(s_dve.h, nmax_n[p])
                    i = s.activation(out=e8[:], in_=gvf[:], func=AF.Exp,
                                     bias=nmax[:, 0:1])
                    s_act.inc(i)
                blk.scalar(a_exp)
                exp_n[tt - 1] = s_act.n

            if j % 16 == 15:
                qe = s_act.n

                def d_q(v, tt=tt, q=q, qs=qs, qe=qe):
                    v.wait_ge(s_act.h, qe)
                    i = v.max(out=cman[:, 8 * q:8 * q + 8],
                              in_=scq[:, qs * 8192:(qs + 1) * 8192])
                    n1 = s_dve.inc(i)
                    v.wait_ge(s_dve.h, n1)
                    i = v.max_index(out=cmi[:, 8 * q:8 * q + 8],
                                    in_max=cman[:, 8 * q:8 * q + 8],
                                    in_values=scq[:, qs * 8192:(qs + 1) * 8192])
                    half_read[(tt, q)] = s_dve.inc(i)
                blk.vector(d_q)
                if q == 1 and tt > 0:
                    emit_wsum(tt - 1)

        def d_top(v, tt=tt):
            v.wait_ge(s_dve.h, s_dve.n)
            i = v.max(out=gv[:], in_=cman[:])
            n4 = s_dve.inc(i)
            v.wait_ge(s_dve.h, n4)
            i = v.max_index(out=pos[:], in_max=gv[:], in_values=cman[:])
            n5 = s_dve.inc(i)
            v.wait_ge(s_dve.h, n5)
            v.tensor_scalar(out=posf[:], in0=pos[:], scalar1=0.0, scalar2=None,
                            op0=ALU.add)
            i = v.tensor_scalar(out=cgi[:], in0=cmi[:], scalar1=0.0, scalar2=None,
                                op0=ALU.add)
            n6 = s_dve.inc(i)
            v.wait_ge(s_dve.h, n6)
            i = v.tensor_tensor(out=cgi[:], in0=cgi[:], in1=hb32[:], op=ALU.add)
            n7 = s_dve.inc(i)
            v.wait_ge(s_dve.h, n7)
            for k in range(KK):
                i = v.scalar_tensor_tensor(out=d32[:], in0=io32[:],
                                           scalar=posf[:, k:k + 1], in1=cgi[:],
                                           op0=ALU.is_equal, op1=ALU.mult,
                                           accum_out=gixf[:, k:k + 1])
            n8 = s_dve.inc(i)
            v.wait_ge(s_dve.h, n8)
            i = v.tensor_scalar(out=gixu[:], in0=gixf[:], scalar1=0.0,
                                scalar2=None, op0=ALU.add)
            gidx_ready[tt] = s_dve.inc(i)
            i = v.tensor_scalar(out=gvf[:], in0=gv[:], scalar1=0.0, scalar2=None,
                                op0=ALU.add)
            n9 = s_dve.inc(i)
            v.wait_ge(s_dve.h, n9)
            i = v.tensor_scalar(out=nmax[:], in0=gvf[:, 0:1], scalar1=-1.0,
                                scalar2=None, op0=ALU.mult)
            nmax_n[tt] = s_dve.inc(i)
        blk.vector(d_top)

        for k in range(KK):
            def p_g(g, tt=tt, k=k):
                if k == 0:
                    g.wait_ge(s_dve.h, gidx_ready[tt])
                    if tt > 0:
                        g.wait_ge(s_dve.h, wsum_done[tt - 1])
                i = g.indirect_dma_start(
                    out=Vg[:].rearrange("p (k d) -> p k d", k=KK)[:, k, :],
                    out_offset=None, in_=VD[:, :],
                    in_offset=IndirectOffsetOnAxis(ap=gixu[:, k:k + 1], axis=0))
                s_pl.incd(i)
            blk.gpsimd(p_g)

        def p_b(g, tt=tt, base_all=base_all):
            g.wait_ge(s_out.h, base_all)
            if tt >= 2:
                g.wait_ge(s_dve.h, wsum_done[tt - 2])
            i = g.dma_start(out=bsb[:, (tt % 2) * D:(tt % 2 + 1) * D],
                            in_=BASE[tt * 128:(tt + 1) * 128, :])
            s_pl.incd(i)
        blk.gpsimd(p_b)
        pool_ready[tt] = s_pl.n

        if tt > 0:
            emit_out(tt - 1)

    def a_exp_last(s):
        s.wait_ge(s_dve.h, nmax_n[NP4 - 1])
        i = s.activation(out=e8[:], in_=gvf[:], func=AF.Exp, bias=nmax[:, 0:1])
        s_act.inc(i)
    blk.scalar(a_exp_last)
    exp_n[NP4 - 1] = s_act.n
    emit_wsum(NP4 - 1)
    emit_out(NP4 - 1)
    _maybe_tap("gv", gv[:, :])
    _maybe_tap("cmi", cmi[:, :])
    _maybe_tap("cman", cman[:, :])
    _maybe_tap("pos", pos[:, :])
    _maybe_tap("gixf", gixf[:, :])
    _maybe_tap("gixu", gixu[:, :])
    _maybe_tap("e8", e8[:, :])
    _maybe_tap("w8", w8[:, :])
    _maybe_tap("bsb", bsb[:, :])
    _maybe_tap("Vg", Vg[:, :])
    _maybe_tap("cgi", cgi[:, :])
    _maybe_tap("KTt", KT[:, :])
    _maybe_tap("scq", scq[:, :])

    def fin(g):
        g.wait_ge(s_oe.h, s_oe.n)
        g.wait_ge(s_oo.h, s_oo.n)
        g.wait_ge(s_out.h, s_out.n)
        g.wait_ge(s_pl.h, s_pl.n)
        g.wait_ge(s_ctl.h, s_ctl.n)
    blk.gpsimd(fin)

    ctx.close()
    return nc


_TAP_SHAPES = {
    "gv": ([128, 8], BF16),
    "cmi": ([128, 32], U16),
    "cman": ([128, 32], BF16),
    "pos": ([128, 8], U16),
    "gixf": ([128, 8], F32),
    "gixu": ([128, 8], U32),
    "e8": ([128, 8], F32),
    "w8": ([128, 8], F32),
    "bsb": ([128, 2048], F32),
    "Vg": ([128, 8 * 1024], F32),
    "cgi": ([128, 32], F32),
    "KTt": ([128, NK], BF16),
    "scq": ([128, 2 * 8192], BF16),
    "Apow": ([64, 64 * SD], BF16),
    "uT": ([64, S], BF16),
    "hvec": ([64, 1], F32),
    "im2d": ([128, 32], BF16),
    "nw64": ([64, 1], F32),
    "SCm": ([128, DCH * 128], F32),
    "QT": ([128, S], BF16),
}


def kernel(L0_attn_x, L0_attn_A, L0_attn_Bm, L0_attn_W_imp, L0_attn_W_router,
           L0_attn_compress, L0_attn_K, L0_attn_V, L0_ffn_weight):
    x = np.ascontiguousarray(np.asarray(L0_attn_x, dtype=np.float32))
    com = {
        "A": np.ascontiguousarray(np.asarray(L0_attn_A, dtype=np.float32)),
        "Bm": np.ascontiguousarray(np.asarray(L0_attn_Bm, dtype=np.float32)),
        "Wimp": np.ascontiguousarray(np.asarray(L0_attn_W_imp, dtype=np.float32)),
        "Wrout": np.ascontiguousarray(np.asarray(L0_attn_W_router, dtype=np.float32)),
        "comp": np.ascontiguousarray(np.asarray(L0_attn_compress, dtype=np.float32)),
        "K": np.ascontiguousarray(np.asarray(L0_attn_K, dtype=np.float32)),
        "V": np.ascontiguousarray(np.asarray(L0_attn_V, dtype=np.float32)),
        "Wf": np.ascontiguousarray(np.asarray(L0_ffn_weight, dtype=np.float32)),
    }
    nc = build_program()
    in_maps = [dict(com, x=np.ascontiguousarray(x[b])) for b in range(B)]
    res = run_bass_kernel_spmd(nc, in_maps, core_ids=list(range(B)))
    out = np.stack([np.asarray(res.results[b]["out"]) for b in range(B)], axis=0)
    return out


# version 07
# speedup vs baseline: 1.0369x; 1.0369x over previous
import contextlib

import numpy as np

import concourse.bass as bass
import concourse.mybir as mybir
from concourse.bass import IndirectOffsetOnAxis
from concourse.bass_utils import run_bass_kernel_spmd

F32 = mybir.dt.float32
BF16 = mybir.dt.bfloat16
U16 = mybir.dt.uint16
U32 = mybir.dt.uint32
ALU = mybir.AluOpType
AF = mybir.ActivationFunctionType

B, S, D, R, NC, NK, SD, KK = 8, 4096, 1024, 128, 64, 32768, 64, 8
TT = S // 128
DCH = D // 128
EPS_LN = 1e-5
EPS_W = 1e-5
EPS_NW = 1e-8
RSQRT_R = float(1.0 / np.sqrt(np.float64(R)))

_DEBUG_TAPS = []
N_P4 = [TT]


class Sem:
    def __init__(self, ctx, nc, name):
        self.h = ctx.enter_context(nc.semaphore(name))
        self.n = 0

    def inc(self, inst, k=1):
        inst.then_inc(self.h, k)
        self.n += k
        return self.n

    def incd(self, inst):
        return self.inc(inst, 16)

    def wait(self, eng, n=None):
        eng.wait_ge(self.h, self.n if n is None else n)


def build_program():
    nc = bass.Bass("TRN2", target_bir_lowering=False, debug=False)

    X = nc.dram_tensor("x", [S, D], F32, kind="ExternalInput").ap()
    A_ = nc.dram_tensor("A", [SD, SD], F32, kind="ExternalInput").ap()
    BM = nc.dram_tensor("Bm", [D, SD], F32, kind="ExternalInput").ap()
    WI = nc.dram_tensor("Wimp", [D, SD], F32, kind="ExternalInput").ap()
    WR = nc.dram_tensor("Wrout", [NC, D], F32, kind="ExternalInput").ap()
    CP = nc.dram_tensor("comp", [NC, D, R], F32, kind="ExternalInput").ap()
    KD = nc.dram_tensor("K", [NK, R], F32, kind="ExternalInput").ap()
    VD = nc.dram_tensor("V", [NK, D], F32, kind="ExternalInput").ap()
    WF = nc.dram_tensor("Wf", [D, D], F32, kind="ExternalInput").ap()
    OUT = nc.dram_tensor("out", [S, D], F32, kind="ExternalOutput").ap()
    BASE = nc.dram_tensor("base", [S, D], F32, kind="Internal").ap()
    LDR = nc.dram_tensor("Ldr", [S], F32, kind="Internal").ap()
    ZDR = nc.dram_tensor("Zdr", [S], F32, kind="Internal").ap()
    IMD = nc.dram_tensor("imd", [S], BF16, kind="Internal").ap()

    taps = {}
    for name in _DEBUG_TAPS:
        shape, dt = _TAP_SHAPES[name]
        taps[name] = nc.dram_tensor("tap_" + name, shape, dt, kind="ExternalOutput").ap()

    ctx = contextlib.ExitStack()

    at = nc.alloc_sbuf_tensor_at
    B0 = 16512
    KT = at("KT", [128, NK], BF16, offset=B0).ap()
    wstg = at("wstg", [128, 4096], F32, offset=B0).ap()
    negt = at("negt", [128, 8192], BF16, offset=B0 + 16384).ap()
    posx = at("posx", [128, 4096], BF16, offset=B0 + 32768).ap()
    uT = at("uT", [64, S], BF16, offset=B0 + 40960).ap()
    WfT = at("WfT", [128, DCH * D], BF16, offset=B0 + 49152).ap()
    big = at("big", [128, NK], BF16, offset=82048).ap()
    scq = at("scq", [128, 2 * 8192], BF16, offset=82048).ap()
    Vg = at("Vg", [128, KK * D], F32, offset=82048 + 32768).ap()
    QT = at("QT", [128, S], BF16, offset=147584).ap()
    kst = at("kst", [128, 256], F32, offset=147584).ap()
    kst4 = at("kst4", [128, 2 * 512], F32, offset=147584).ap()
    xnt2 = at("xnt2", [128, D], BF16, offset=147584 + 4096).ap()
    junk = at("junk", [128, D], F32, offset=B0).ap()
    imr = at("imr", [64, S], BF16, offset=B0).ap()
    UB = 155776
    bsb = at("bsb", [128, 2 * D], F32, offset=UB).ap()
    Apow = at("Apow", [64, 64 * SD], BF16, offset=UB + 8192).ap()
    e_tp = at("e_tp", [64, S], BF16, offset=UB + 8192).ap()
    cbuf = at("cbuf", [128, 2048], F32, offset=UB + 8192).ap()
    A64p = at("A64p", [64, 64 * SD], BF16, offset=UB + 16384).ap()
    Lst = at("Lst", [1, 512], F32, offset=UB + 16384).ap()
    WiT = at("WiT", [64, D], F32, offset=UB + 18432).ap()
    SCm = at("SCm", [128, DCH * 128], F32, offset=UB + 16384).ap()
    SCb = at("SCb", [128, DCH * 128], BF16, offset=UB + 20480).ap()
    o = 180352

    def tail(name, shape, dt):
        nonlocal o
        t = at(name, shape, dt, offset=o)
        per = int(np.prod(shape[1:])) * mybir.dt.size(dt)
        o += (per + 31) // 32 * 32
        return t.ap()

    xb = tail("xb", [128, 2 * D], F32)
    xnt = tail("xnt", [128, D], BF16)
    Abf = tail("Abf", [64, SD], BF16)
    Bmb = tail("Bmb", [128, DCH * SD], BF16)
    WrT = tail("WrT", [128, DCH * NC], BF16)
    idb = tail("idb", [128, 128], BF16)
    idf = tail("idf", [128, 128], F32)
    onesf = tail("onesf", [1, 128], F32)
    ones128 = tail("ones128", [128, 1], F32)
    ones64 = tail("ones64", [64, 1], BF16)
    stats = tail("stats", [128, 16], F32)
    scal = tail("scal", [1, 8], F32)
    srep = tail("srep", [128, 2], F32)
    hp = tail("hp", [128, DCH], F32)
    hpb = tail("hpb", [128, DCH], BF16)
    Tm = tail("Tm", [64, SD], BF16)
    inT = tail("inT", [64, SD + 2], BF16)
    hvec = tail("hvec", [64, 1], F32)
    nwacc = tail("nwacc", [64, 1], F32)
    nw64 = tail("nw64", [64, 1], F32)
    nwrep = tail("nwrep", [128, NC], F32)
    L2d = tail("L2d", [128, 32], F32)
    eL2d = tail("eL2d", [128, 32], F32)
    Z2d = tail("Z2d", [128, 32], F32)
    rz2d = tail("rz2d", [128, 32], F32)
    im2d = tail("im2d", [128, 32], BF16)
    rsum = tail("rsum", [128, 1], F32)
    io32 = tail("io32", [128, 32], F32)
    hb32 = tail("hb32", [128, 32], F32)
    cman = tail("cman", [128, 32], BF16)
    cmi = tail("cmi", [128, 32], U16)
    cgi = tail("cgi", [128, 32], F32)
    gv = tail("gv", [128, 8], BF16)
    gvf = tail("gvf", [128, 8], F32)
    pos = tail("pos", [128, 8], U16)
    posf = tail("posf", [128, 8], F32)
    gixf = tail("gixf", [128, 8], F32)
    gixu = tail("gixu", [128, 8], U32)
    e8 = tail("e8", [128, 8], F32)
    w8 = tail("w8", [128, 8], F32)
    sm1 = tail("sm1", [128, 2], F32)
    nmax = tail("nmax", [128, 1], F32)
    d32 = tail("d32", [128, 32], F32)
    ckon = tail("ckon", [128, 2], F32)
    assert o <= 196576 - 32, f"sbuf tail overflow: {o}"

    psA = ctx.enter_context(nc.psum_tensor("psA", [128, 3072], F32))
    psB = ctx.enter_context(nc.psum_tensor("psB", [128, 1024], BF16))
    psD = ctx.enter_context(nc.psum_tensor("psD", [128, 512], F32))
    bank = lambda b: psA[:, 512 * b:512 * (b + 1)]

    s_pe = Sem(ctx, nc, "s_pe")
    s_dve = Sem(ctx, nc, "s_dve")
    s_act = Sem(ctx, nc, "s_act")
    s_ld = Sem(ctx, nc, "s_ld")
    s_pl = Sem(ctx, nc, "s_pl")
    s_out = Sem(ctx, nc, "s_out")
    s_x2 = [Sem(ctx, nc, "s_x0"), Sem(ctx, nc, "s_x1")]
    s_k2 = [Sem(ctx, nc, "s_k0"), Sem(ctx, nc, "s_k1")]
    s_c2 = [Sem(ctx, nc, "s_c0"), Sem(ctx, nc, "s_c1")]
    s_ctl = Sem(ctx, nc, "s_ctl")

    blk = ctx.enter_context(nc.Block())

    def ctl_dma(out_ap, in_ap, waits=()):
        pre = s_ctl.n

        def _c(g, out_ap=out_ap, in_ap=in_ap, waits=tuple(waits), pre=pre):
            g.wait_ge(s_ctl.h, pre)
            for sem, cnt in waits:
                g.wait_ge(sem.h, cnt)
            i = g.dma_start(out=out_ap, in_=in_ap)
            s_ctl.incd(i)
        blk.gpsimd(_c)
        return s_ctl.n

    def xnT(c):
        return big[:, c * S:(c + 1) * S]

    def _maybe_tap(name, src_ap):
        if name not in taps:
            return
        dn = s_dve.n
        an = s_act.n
        pn = s_pe.n

        def t_dma(g, name=name, src_ap=src_ap, dn=dn, an=an, pn=pn):
            g.wait_ge(s_dve.h, dn)
            g.wait_ge(s_act.h, an)
            g.wait_ge(s_pe.h, pn)
            i = g.dma_start(out=taps[name][:], in_=src_ap)
            s_pl.incd(i)
        blk.gpsimd(t_dma)

    def c_pool(g):
        g.memset(idb[:], 0.0)
        g.affine_select(out=idb[:], in_=idb[:], compare_op=ALU.not_equal,
                        fill=1.0, base=0, pattern=[[-1, 128]], channel_multiplier=1)
        g.memset(idf[:], 0.0)
        g.affine_select(out=idf[:], in_=idf[:], compare_op=ALU.not_equal,
                        fill=1.0, base=0, pattern=[[-1, 128]], channel_multiplier=1)
        g.iota(out=io32[:], pattern=[[1, 32]], base=0, channel_multiplier=0,
               allow_small_or_imprecise_dtypes=True)
        g.iota(out=hb32[:], pattern=[[8192, 4], [0, 8]], base=0,
               channel_multiplier=0, allow_small_or_imprecise_dtypes=True)
        g.memset(ones64[:], 1.0)
        g.memset(onesf[:], 1.0)
        g.memset(ones128[:], 1.0)
        g.memset(ckon[:, 0:1], EPS_LN)
        i = g.memset(ckon[:, 1:2], RSQRT_R)
        s_pl.inc(i)
    blk.gpsimd(c_pool)
    pl_consts = s_pl.n

    ctl_dma(wstg[0:64, 0:64], A_[:, :])
    ld_small = ctl_dma(wstg[:, 64:64 + DCH * SD].rearrange("p (c e) -> p c e", c=DCH),
                       BM[:, :].rearrange("(c p) e -> p c e", c=DCH))

    def c_cast(v, ld_small=ld_small):
        v.wait_ge(s_ctl.h, ld_small)
        v.wait_ge(s_pl.h, pl_consts)
        v.tensor_copy(out=Abf[:], in_=wstg[0:64, 0:64])
        i = v.tensor_copy(out=Bmb[:], in_=wstg[:, 64:64 + DCH * SD])
        s_dve.inc(i)
    blk.vector(c_cast)
    cast_done = s_dve.n

    def act_preload(s):
        from concourse.hw_specs import get_activation_tables
        need = {AF.Ln, AF.Exp, AF.Identity, AF.Copy}
        for idx, funcs in enumerate(get_activation_tables(nc.m.arch).values()):
            if need.issubset(funcs):
                s.add_instruction(mybir.InstLoadActFuncSet(
                    name=nc.get_next_instruction_name(),
                    act_func_set_id=idx, ins=[], outs=[]))
                return
        raise ValueError("no table set covers all activation funcs")
    blk.scalar(act_preload)

    habs = []
    for h in range(2):
        def wfa_ld(sy, h=h):
            if h == 0:
                sy.wait_ge(s_dve.h, cast_done)
            else:
                sy.wait_ge(s_dve.h, s_dve.n)
            i = sy.dma_start(out=wstg[:].rearrange("p (c i) -> p c i", c=4),
                             in_=WF[:, :].rearrange("(c p) i -> p c i", c=DCH)
                             [:, 4 * h:4 * (h + 1), :])
            s_ld.incd(i)
        blk.sync(wfa_ld)
        ldh = s_ld.n

        def wf_abs(v, h=h, ldh=ldh):
            v.wait_ge(s_ld.h, ldh)
            i = v.reduce_sum(out=stats[:, h:h + 1], in_=wstg[:],
                             axis=mybir.AxisListType.X, apply_absolute_value=True)
            s_dve.inc(i)
        blk.vector(wf_abs)
        habs.append(s_dve.n)

    def wf_sc(v):
        v.wait_ge(s_dve.h, habs[1])
        i = v.tensor_tensor(out=stats[:, 2:3], in0=stats[:, 0:1],
                            in1=stats[:, 1:2], op=ALU.add)
        s_dve.inc(i)
    blk.vector(wf_sc)
    dv1 = s_dve.n

    def wfs_mm(t, dv1=dv1):
        t.wait_ge(s_dve.h, dv1)
        i = t.matmul(out=psD[0:1, 432:433], lhsT=stats[:, 2:3],
                     rhs=ones128[:], start=True, stop=True)
        s_pe.inc(i)
    blk.tensor(wfs_mm)
    pe1 = s_pe.n

    def wfs_fin(v, pe1=pe1):
        v.wait_ge(s_pe.h, pe1)
        i = v.tensor_scalar(out=scal[0:1, 5:6], in0=psD[0:1, 432:433],
                            scalar1=1.0 / (D * D), scalar2=EPS_W,
                            op0=ALU.mult, op1=ALU.max)
        n1 = s_dve.inc(i)
        v.wait_ge(s_dve.h, n1)
        i = v.reciprocal(out=scal[0:1, 6:7], in_=scal[0:1, 5:6])
        s_dve.inc(i)
    blk.vector(wfs_fin)
    dv1 = s_dve.n

    def srep_mm(t, dv1=dv1):
        t.wait_ge(s_dve.h, dv1)
        t.matmul(out=psD[:, 433:434], lhsT=onesf[:], rhs=scal[0:1, 5:6],
                 start=True, stop=True)
        i = t.matmul(out=psD[:, 434:435], lhsT=onesf[:], rhs=scal[0:1, 6:7],
                     start=True, stop=True)
        s_pe.inc(i)
    blk.tensor(srep_mm)
    pe1 = s_pe.n

    def srep_ev(v, pe1=pe1):
        v.wait_ge(s_pe.h, pe1)
        v.tensor_scalar(out=srep[:, 0:1], in0=psD[:, 433:434], scalar1=-1.0,
                        scalar2=None, op0=ALU.mult)
        i = v.tensor_copy(out=srep[:, 1:2], in_=psD[:, 434:435])
        s_dve.inc(i)
    blk.vector(srep_ev)
    srep_ready = s_dve.n

    for h in range(2):
        def wft_ld(sy, h=h):
            sy.wait_ge(s_dve.h, s_dve.n)
            i = sy.dma_start(out=wstg[:].rearrange("p (c i) -> p c i", c=4),
                             in_=WF[:, :].rearrange("(c p) i -> p c i", c=DCH)
                             [:, 4 * h:4 * (h + 1), :])
            s_ld.incd(i)
        blk.sync(wft_ld)
        ldh = s_ld.n

        def wf_tern(v, h=h, ldh=ldh):
            v.wait_ge(s_ld.h, ldh)
            v.wait_ge(s_dve.h, srep_ready)
            i = v.tensor_scalar(out=wstg[:], in0=wstg[:], scalar1=srep[:, 1:2],
                                scalar2=None, op0=ALU.mult)
            n1 = s_dve.inc(i)
            v.wait_ge(s_dve.h, n1)
            i = v.tensor_scalar(out=posx[:], in0=wstg[:], scalar1=0.5,
                                scalar2=None, op0=ALU.is_gt)
            n2 = s_dve.inc(i)
            v.wait_ge(s_dve.h, n2)
            i = v.scalar_tensor_tensor(out=negt[:, h * 4096:(h + 1) * 4096],
                                       in0=wstg[:], scalar=-0.5, in1=posx[:],
                                       op0=ALU.is_lt, op1=ALU.subtract)
            s_dve.inc(i)
        blk.vector(wf_tern)
    tern_ready = s_dve.n

    for ci in range(DCH):
        def wft_tr(t, ci=ci):
            t.wait_ge(s_dve.h, tern_ready)
            if ci > 0:
                t.wait_ge(s_act.h, s_act.n)
            for co in range(DCH):
                i = t.transpose(out=psB[:, co * 128:(co + 1) * 128],
                                in_=negt[:, co * 1024 + ci * 128:co * 1024 + ci * 128 + 128],
                                identity=idb[:])
            s_pe.inc(i)
        blk.tensor(wft_tr)
        pe1 = s_pe.n

        def wft_ev(s, ci=ci, pe1=pe1):
            s.wait_ge(s_pe.h, pe1)
            i = s.activation(out=WfT[:, ci * 1024:(ci + 1) * 1024], in_=psB[:],
                             func=AF.Copy)
            s_act.inc(i)
        blk.scalar(wft_ev)
    wfT_ready = s_act.n


    kev_bank = {4: 0, 5: 0}
    ffn_bank_free = {0: 0, 1: 0, 2: 0, 3: 0}
    base_done = {}
    ktr_pe = {}
    kt_emitted = []

    def emit_ktile(kt, ev_waits=()):
        kb = 4 + (kt // 4) % 2
        slot = kt % 4
        kpar = len(kt_emitted) % 2
        kt_emitted.append(kt)

        def k_ld(sy, kt=kt, kpar=kpar,
                 slot_gate=(ktr_pe[kt_emitted[-3]] if len(kt_emitted) >= 3 else 0)):
            sy.wait_ge(s_dve.h, tern_ready)
            if slot_gate:
                sy.wait_ge(s_pe.h, slot_gate)
            i = sy.dma_start(out=kst[:, kpar * 128:(kpar + 1) * 128],
                             in_=KD[kt * 128:(kt + 1) * 128, :])
            s_k2[kpar].incd(i)
        blk.sync(k_ld)
        ld_k = s_k2[kpar].n

        def k_tr(t, kb=kb, slot=slot, kpar=kpar, ld_k=ld_k,
                 gate=kev_bank[4 + (kt // 4) % 2]):
            t.wait_ge(s_k2[kpar].h, ld_k)
            if slot == 0 and gate:
                t.wait_ge(s_act.h, gate)
            i = t.transpose(out=bank(kb)[:, slot * 128:(slot + 1) * 128],
                            in_=kst[:, kpar * 128:(kpar + 1) * 128],
                            identity=idf[:])
            s_pe.inc(i)
        blk.tensor(k_tr)
        ktr_pe[kt] = s_pe.n

        if slot == 3:
            pe_k = s_pe.n

            def k_ev(s, kt=kt, kb=kb, pe_k=pe_k, ev_waits=tuple(ev_waits)):
                s.wait_ge(s_pe.h, pe_k)
                for sem, cnt in ev_waits:
                    s.wait_ge(sem.h, cnt)
                base_n = (kt - 3) * 128
                i = s.activation(out=KT[:, base_n:base_n + 512],
                                 in_=bank(kb), func=AF.Copy)
                s_act.inc(i)
            blk.scalar(k_ev)
            kev_bank[kb] = s_act.n

    prev_evac = [0, 0]
    norm_done = {}
    base_done = {}
    xnt2_ = [None, None]

    def p1_A(tt):
        par = tt % 2
        sb = 8 * par

        def p1_ld(sy, tt=tt, par=par):
            if tt == 0:
                sy.wait_ge(s_pl.h, pl_consts)
            if tt >= 2:
                sy.wait_ge(s_dve.h, base_done[tt - 2])
            i = sy.dma_start(out=xb[:, par * D:(par + 1) * D],
                             in_=X[tt * 128:(tt + 1) * 128, :])
            s_x2[par].incd(i)
        blk.sync(p1_ld)
        ld_n = s_x2[par].n

        def p1_stats(v, par=par, sb=sb, ld_n=ld_n):
            v.wait_ge(s_x2[par].h, ld_n)
            xt = xb[:, par * D:(par + 1) * D]
            bns = d32[:, 16 * par:16 * par + 12]
            v.bn_stats(out=bns[:, 0:6], in_=xt[:, 0:512])
            i = v.bn_stats(out=bns[:, 6:12], in_=xt[:, 512:1024])
            n1 = s_dve.inc(i)
            v.wait_ge(s_dve.h, n1)
            i = v.bn_aggr(out=stats[:, sb:sb + 2], in_=bns[:])
            n2 = s_dve.inc(i)
            v.wait_ge(s_dve.h, n2)
            i = v.tensor_copy(out=stats[:, sb + 4:sb + 5], in_=stats[:, sb + 1:sb + 2])
            s_dve.inc(i)
        blk.vector(p1_stats)
        return s_dve.n

    def p1_B(tt, dve_var):
        par = tt % 2
        sb = 8 * par

        def p1_ln(s, sb=sb, dve_var=dve_var):
            s.wait_ge(s_dve.h, dve_var)
            i = s.activation(out=stats[:, sb + 5:sb + 6], in_=stats[:, sb + 4:sb + 5],
                             func=AF.Ln, bias=ckon[:, 0:1])
            s_act.inc(i)
        blk.scalar(p1_ln)
        act_ln = s_act.n

        def p1_half(v, sb=sb, act_ln=act_ln):
            v.wait_ge(s_act.h, act_ln)
            i = v.tensor_scalar(out=stats[:, sb + 6:sb + 7], in0=stats[:, sb + 5:sb + 6],
                                scalar1=-0.5, scalar2=None, op0=ALU.mult)
            s_dve.inc(i)
        blk.vector(p1_half)
        dve_half = s_dve.n

        def p1_exp(s, sb=sb, dve_half=dve_half):
            s.wait_ge(s_dve.h, dve_half)
            i = s.activation(out=stats[:, sb + 5:sb + 6], in_=stats[:, sb + 6:sb + 7],
                             func=AF.Exp)
            s_act.inc(i)
        blk.scalar(p1_exp)
        act_rstd = s_act.n

        def p1_rstd(v, sb=sb, act_rstd=act_rstd):
            v.wait_ge(s_act.h, act_rstd)
            i = v.scalar_tensor_tensor(out=stats[:, sb + 7:sb + 8],
                                       in0=stats[:, sb:sb + 1], scalar=-1.0,
                                       in1=stats[:, sb + 5:sb + 6],
                                       op0=ALU.mult, op1=ALU.mult)
            s_dve.inc(i)
        blk.vector(p1_rstd)
        dve_rstd = s_dve.n

        def p1_norm(s, par=par, sb=sb, dve_rstd=dve_rstd, prev=prev_evac[par]):
            s.wait_ge(s_dve.h, dve_rstd)
            if prev:
                s.wait_ge(s_pe.h, prev)
            xt = xb[:, par * D:(par + 1) * D]
            i = s.activation(out=(xnt if par == 0 else xnt2)[:], in_=xt,
                             func=AF.Identity,
                             bias=stats[:, sb + 7:sb + 8], scale=stats[:, sb + 5:sb + 6])
            s_act.inc(i)
        blk.scalar(p1_norm)
        norm_done[tt] = s_act.n
        return s_act.n

    def p1_C(tt, act_norm):
        par = tt % 2
        xsrc = xnt if par == 0 else xnt2

        def p1_tr(t, act_norm=act_norm, xsrc=xsrc):
            t.wait_ge(s_act.h, act_norm)
            t.wait_ge(s_dve.h, s_dve.n)
            for c in range(DCH):
                i = t.transpose(out=psB[:, c * 128:(c + 1) * 128],
                                in_=xsrc[:, c * 128:(c + 1) * 128], identity=idb[:])
            s_pe.inc(i)
        blk.tensor(p1_tr)
        pe_tr = s_pe.n
        prev_evac[par] = pe_tr

        def p1_ev(v, tt=tt, pe_tr=pe_tr):
            v.wait_ge(s_pe.h, pe_tr)
            dst = big[:].rearrange("p (c s) -> p c s", c=DCH)[:, :, tt * 128:(tt + 1) * 128]
            i = v.tensor_copy(out=dst, in_=psB[:].rearrange("p (c s) -> p c s", c=DCH))
            s_dve.inc(i)
        blk.vector(p1_ev)
        ev_n = s_dve.n

        for half in range(2):
            b = 2 * par + half

            def f_mm(t, tt=tt, half=half, b=b, ev_n=ev_n,
                     gate=ffn_bank_free[2 * par + half]):
                t.wait_ge(s_act.h, wfT_ready)
                t.wait_ge(s_dve.h, ev_n)
                if gate:
                    t.wait_ge(s_dve.h, gate)
                for c in range(DCH):
                    i = t.matmul(out=bank(b),
                                 lhsT=xnT(c)[:, tt * 128:(tt + 1) * 128],
                                 rhs=WfT[:, c * 1024 + half * 512:c * 1024 + half * 512 + 512],
                                 start=(c == 0), stop=(c == DCH - 1))
                s_pe.inc(i)
            blk.tensor(f_mm)
        pe_f = s_pe.n

        def f_base(v, tt=tt, par=par, pe_f=pe_f):
            v.wait_ge(s_pe.h, pe_f)
            for half in range(2):
                b = 2 * par + half
                i = v.scalar_tensor_tensor(
                    out=bsb[:, par * D + half * 512:par * D + half * 512 + 512],
                    in0=bank(b), scalar=srep[:, 0:1],
                    in1=xb[:, par * D + half * 512:par * D + half * 512 + 512],
                    op0=ALU.mult, op1=ALU.add)
            s_dve.inc(i)
        blk.vector(f_base)
        base_done[tt] = s_dve.n
        ffn_bank_free[2 * par] = s_dve.n
        ffn_bank_free[2 * par + 1] = s_dve.n

        def f_st(sy, tt=tt, par=par, dn=base_done[tt]):
            sy.wait_ge(s_dve.h, dn)
            i = sy.dma_start(out=BASE[tt * 128:(tt + 1) * 128, :],
                             in_=bsb[:, par * D:(par + 1) * D])
            s_out.incd(i)
        blk.sync(f_st)

        for q in range(3):
            emit_ktile(64 + tt * 3 + q)
        return pe_tr

    stageA = {}
    stageB = {}
    for it in range(TT + 2):
        if it >= 2:
            p1_C(it - 2, stageB[it - 2])
        if it < TT:
            stageA[it] = p1_A(it)
        if 1 <= it <= TT:
            stageB[it - 1] = p1_B(it - 1, stageA[it - 1])
    xnT_ready = s_dve.n
    ffn_pe_done = s_pe.n

    def p2_seed(v):
        v.wait_ge(s_dve.h, cast_done)
        v.tensor_copy(out=Apow[:, 0:64], in_=idb[0:64, 0:64])
        v.tensor_copy(out=A64p[:, 0:64], in_=idb[0:64, 0:64])
        i = v.tensor_copy(out=Apow[:, 64:128], in_=Abf[:])
        s_dve.inc(i)
    blk.vector(p2_seed)
    def emit_dbl(pow_ap, rounds):
        for m, lo, hi in rounds:
            w = hi - lo

            def r_tr(t, pow_ap=pow_ap, m=m):
                t.wait_ge(s_dve.h, s_dve.n)
                i = t.transpose(out=psB[0:64, 0:64],
                                in_=pow_ap[:, 64 * m:64 * (m + 1)],
                                identity=idb[0:64, 0:64])
                s_pe.inc(i)
            blk.tensor(r_tr)
            pe1 = s_pe.n

            def r_ev(v, pe1=pe1):
                v.wait_ge(s_pe.h, pe1)
                i = v.tensor_copy(out=Tm[:], in_=psB[0:64, 0:64])
                s_dve.inc(i)
            blk.vector(r_ev)
            dv1 = s_dve.n

            def r_mm(t, pow_ap=pow_ap, m=m, lo=lo, w=w, dv1=dv1):
                t.wait_ge(s_dve.h, dv1)
                i = t.matmul(out=psD[0:64, 0:64 * w], lhsT=Tm[:],
                             rhs=pow_ap[:, 64 * lo:64 * (lo + w)],
                             start=True, stop=True)
                s_pe.inc(i)
            blk.tensor(r_mm)
            pe2 = s_pe.n

            def r_ev2(v, pow_ap=pow_ap, m=m, lo=lo, w=w, pe2=pe2):
                v.wait_ge(s_pe.h, pe2)
                i = v.tensor_copy(out=pow_ap[:, 64 * (m + lo):64 * (m + lo + w)],
                                  in_=psD[0:64, 0:64 * w])
                s_dve.inc(i)
            blk.vector(r_ev2)

    ROUNDS = [(1, 1, 2), (2, 1, 3), (4, 1, 5), (8, 1, 8), (8, 8, 9),
              (16, 1, 9), (16, 9, 16), (16, 16, 17),
              (32, 1, 9), (32, 9, 17), (32, 17, 25), (32, 25, 32)]
    emit_dbl(Apow, ROUNDS)

    def a64_mm(t):
        t.wait_ge(s_dve.h, s_dve.n)
        i = t.matmul(out=psD[0:64, 0:64], lhsT=Tm[:], rhs=Apow[:, 64 * 32:64 * 33],
                     start=True, stop=True)
        s_pe.inc(i)
    blk.tensor(a64_mm)
    pe1 = s_pe.n

    def a64_ev(v, pe1=pe1):
        v.wait_ge(s_pe.h, pe1)
        i = v.tensor_copy(out=A64p[:, 64:128], in_=psD[0:64, 0:64])
        s_dve.inc(i)
    blk.vector(a64_ev)
    emit_dbl(A64p, ROUNDS)
    _maybe_tap("Apow", Apow[:, :])

    for st in range(8):
        bk = st % 2

        def u_mm(t, st=st, bk=bk):
            t.wait_ge(s_dve.h, xnT_ready)
            if st >= 2:
                t.wait_ge(s_act.h, s_act.n)
            for c in range(DCH):
                i = t.matmul(out=bank(bk)[0:64, :],
                             lhsT=Bmb[:, c * SD:(c + 1) * SD],
                             rhs=xnT(c)[:, st * 512:(st + 1) * 512],
                             start=(c == 0), stop=(c == DCH - 1))
            s_pe.inc(i)
        blk.tensor(u_mm)
        pe1 = s_pe.n

        def u_ev(s, st=st, bk=bk, pe1=pe1):
            s.wait_ge(s_pe.h, pe1)
            i = s.activation(out=uT[:, st * 512:(st + 1) * 512],
                             in_=bank(bk)[0:64, :], func=AF.Copy)
            s_act.inc(i)
        blk.scalar(u_ev)
    uT_ready = s_act.n
    _maybe_tap("uT", uT[:, :])

    def in_mm(t):
        t.wait_ge(s_act.h, uT_ready)
        t.wait_ge(s_dve.h, s_dve.n)
        uv = uT[:].rearrange("p (c i) -> p i c", i=64)
        for i_ in range(64):
            i = t.matmul(out=psD[0:64, 0:64],
                         lhsT=Apow[:, 64 * (63 - i_):64 * (64 - i_)],
                         rhs=uv[:, i_, :], start=(i_ == 0), stop=(i_ == 63))
        s_pe.inc(i)
    blk.tensor(in_mm)
    pe1 = s_pe.n

    def in_ev(v, pe1=pe1):
        v.wait_ge(s_pe.h, pe1)
        i = v.tensor_copy(out=inT[:, 0:64], in_=psD[0:64, 0:64])
        s_dve.inc(i)
    blk.vector(in_ev)
    dv1 = s_dve.n

    def out_mm(t, dv1=dv1):
        t.wait_ge(s_dve.h, dv1)
        for c in range(64):
            i = t.matmul(out=psD[0:64, 64:65],
                         lhsT=A64p[:, 64 * (63 - c):64 * (64 - c)],
                         rhs=inT[:, c:c + 1], start=(c == 0), stop=(c == 63))
        s_pe.inc(i)
    blk.tensor(out_mm)
    pe_outer = s_pe.n

    def h_ev(v, pe_outer=pe_outer):
        v.wait_ge(s_pe.h, pe_outer)
        i = v.tensor_copy(out=hvec[:], in_=psD[0:64, 64:65])
        s_dve.inc(i)
    blk.vector(h_ev)
    _maybe_tap("hvec", hvec[:, :])

    ld_wi = ctl_dma(wstg[:, 0:DCH * SD].rearrange("p (c e) -> p c e", c=DCH),
                    WI[:, :].rearrange("(c p) e -> p c e", c=DCH),
                    waits=[(s_dve, xnT_ready)])
    for c in range(DCH):
        def wi_tr(t, c=c, ld_wi=ld_wi):
            t.wait_ge(s_ctl.h, ld_wi)
            t.wait_ge(s_dve.h, s_dve.n)
            i = t.transpose(out=psD[0:64, 128:256],
                            in_=wstg[:, c * SD:(c + 1) * SD],
                            identity=idf[:])
            s_pe.inc(i)
        blk.tensor(wi_tr)
        pe1 = s_pe.n

        def wi_ev(v, c=c, pe1=pe1):
            v.wait_ge(s_pe.h, pe1)
            i = v.tensor_copy(out=WiT[:, c * 128:(c + 1) * 128],
                              in_=psD[0:64, 128:256])
            s_dve.inc(i)
        blk.vector(wi_ev)
    dv1 = s_dve.n

    def hp_mm(t, dv1=dv1):
        t.wait_ge(s_dve.h, dv1)
        for c in range(DCH):
            i = t.matmul(out=psD[:, 256 + c:257 + c],
                         lhsT=WiT[:, c * 128:(c + 1) * 128], rhs=hvec[:],
                         start=True, stop=True)
        s_pe.inc(i)
    blk.tensor(hp_mm)
    pe1 = s_pe.n

    def hp_ev(v, pe1=pe1):
        v.wait_ge(s_pe.h, pe1)
        v.tensor_copy(out=hp[:], in_=psD[:, 256:256 + DCH])
        i = v.tensor_copy(out=hpb[:], in_=psD[:, 256:256 + DCH])
        s_dve.inc(i)
    blk.vector(hp_ev)
    hp_ready = s_dve.n

    lst_free = 0
    for st in range(8):
        def L_mm(t, st=st):
            t.wait_ge(s_dve.h, s_dve.n if st else hp_ready)
            for c in range(DCH):
                i = t.matmul(out=psD[0:1, :], lhsT=hpb[:, c:c + 1],
                             rhs=xnT(c)[:, st * 512:(st + 1) * 512],
                             start=(c == 0), stop=(c == DCH - 1))
            s_pe.inc(i)
        blk.tensor(L_mm)
        pe1 = s_pe.n

        def L_ev(v, pe1=pe1, lst_free=lst_free):
            v.wait_ge(s_pe.h, pe1)
            v.wait_ge(s_ld.h, lst_free)
            i = v.tensor_copy(out=Lst[:], in_=psD[0:1, :])
            s_dve.inc(i)
        blk.vector(L_ev)
        dv1 = s_dve.n

        def L_st(sy, st=st, dv1=dv1):
            sy.wait_ge(s_dve.h, dv1)
            i = sy.dma_start(out=LDR[st * 512:(st + 1) * 512], in_=Lst[0:1, :])
            s_ld.incd(i)
        blk.sync(L_st)
        lst_free = s_ld.n

    ld_L2d = ctl_dma(L2d[:], LDR[:].rearrange("(p q) -> p q", p=128),
                     waits=[(s_ld, s_ld.n)])

    def eL_a(s, ld_L2d=ld_L2d):
        s.wait_ge(s_ctl.h, ld_L2d)
        i = s.activation(out=eL2d[:], in_=L2d[:], func=AF.Exp)
        s_act.inc(i)
    blk.scalar(eL_a)
    act1 = s_act.n

    def eL_sum(v, act1=act1):
        v.wait_ge(s_act.h, act1)
        i = v.tensor_scalar(out=d32[:], in0=eL2d[:], scalar1=1.0,
                            scalar2=None, op0=ALU.mult, op1=ALU.add,
                            accum_out=stats[:, 0:1])
        s_dve.inc(i)
    blk.vector(eL_sum)
    dv1 = s_dve.n

    def els_mm(t, dv1=dv1):
        t.wait_ge(s_dve.h, dv1)
        i = t.matmul(out=psD[0:1, 300:301], lhsT=stats[:, 0:1],
                     rhs=ones128[:], start=True, stop=True)
        s_pe.inc(i)
    blk.tensor(els_mm)
    pe1 = s_pe.n

    def els_r(v, pe1=pe1):
        v.wait_ge(s_pe.h, pe1)
        i = v.reciprocal(out=scal[0:1, 0:1], in_=psD[0:1, 300:301])
        s_dve.inc(i)
    blk.vector(els_r)
    dv1 = s_dve.n

    def els_rep(t, dv1=dv1):
        t.wait_ge(s_dve.h, dv1)
        i = t.matmul(out=psD[:, 301:302], lhsT=onesf[:], rhs=scal[0:1, 0:1],
                     start=True, stop=True)
        s_pe.inc(i)
    blk.tensor(els_rep)
    pe1 = s_pe.n

    def els_rep_ev(v, pe1=pe1):
        v.wait_ge(s_pe.h, pe1)
        i = v.tensor_copy(out=rsum[:], in_=psD[:, 301:302])
        s_dve.inc(i)
    blk.vector(els_rep_ev)
    rsum_ready = s_dve.n

    ld_wr = ctl_dma(wstg[0:64, 512:512 + D], WR[:, :])

    def wr_cast(s, ld_wr=ld_wr):
        s.wait_ge(s_ctl.h, ld_wr)
        s.wait_ge(s_pe.h, s_pe.n)
        i = s.activation(out=e_tp[:, 0:1024], in_=wstg[0:64, 512:512 + D],
                         func=AF.Copy)
        s_act.inc(i)
    blk.scalar(wr_cast)
    act1 = s_act.n

    for c in range(DCH):
        def wr_tr(t, c=c, act1=act1):
            t.wait_ge(s_act.h, act1)
            t.wait_ge(s_dve.h, s_dve.n)
            i = t.transpose(out=psB[:, 0:64], in_=e_tp[:, c * 128:(c + 1) * 128],
                            identity=idb[0:64, 0:64])
            s_pe.inc(i)
        blk.tensor(wr_tr)
        pe1 = s_pe.n

        def wr_ev(v, c=c, pe1=pe1):
            v.wait_ge(s_pe.h, pe1)
            i = v.tensor_copy(out=WrT[:, c * NC:(c + 1) * NC], in_=psB[:, 0:64])
            s_dve.inc(i)
        blk.vector(wr_ev)
    wrT_ready = s_dve.n

    for st in range(8):
        bk = 2 + st % 2

        def tp_mm(t, st=st, bk=bk):
            t.wait_ge(s_dve.h, wrT_ready)
            if st >= 2:
                t.wait_ge(s_act.h, s_act.n)
            for c in range(DCH):
                i = t.matmul(out=bank(bk)[0:64, :], lhsT=WrT[:, c * NC:(c + 1) * NC],
                             rhs=xnT(c)[:, st * 512:(st + 1) * 512],
                             start=(c == 0), stop=(c == DCH - 1))
            s_pe.inc(i)
        blk.tensor(tp_mm)
        pe1 = s_pe.n

        def tp_ev(s, st=st, bk=bk, pe1=pe1):
            s.wait_ge(s_pe.h, pe1)
            s.wait_ge(s_dve.h, s_dve.n)
            i = s.activation(out=e_tp[:, st * 512:(st + 1) * 512],
                             in_=bank(bk)[0:64, :], func=AF.Exp)
            s_act.inc(i)
        blk.scalar(tp_ev)
    etp_ready = s_act.n

    for st in range(8):
        def z_mm(t, st=st):
            t.wait_ge(s_act.h, etp_ready)
            t.wait_ge(s_dve.h, s_dve.n)
            i = t.matmul(out=psD[0:1, :], lhsT=ones64[:],
                         rhs=e_tp[:, st * 512:(st + 1) * 512], start=True, stop=True)
            s_pe.inc(i)
        blk.tensor(z_mm)
        pe1 = s_pe.n

        def z_ev(v, pe1=pe1, lst_free=lst_free):
            v.wait_ge(s_pe.h, pe1)
            v.wait_ge(s_ld.h, lst_free)
            i = v.tensor_copy(out=Lst[:], in_=psD[0:1, :])
            s_dve.inc(i)
        blk.vector(z_ev)
        dv1 = s_dve.n

        def z_st(sy, st=st, dv1=dv1):
            sy.wait_ge(s_dve.h, dv1)
            i = sy.dma_start(out=ZDR[st * 512:(st + 1) * 512], in_=Lst[0:1, :])
            s_ld.incd(i)
        blk.sync(z_st)
        lst_free = s_ld.n

    ld_z = ctl_dma(Z2d[:], ZDR[:].rearrange("(p q) -> p q", p=128),
                   waits=[(s_ld, s_ld.n)])

    def z_fin(v, ld_z=ld_z):
        v.wait_ge(s_ctl.h, ld_z)
        v.wait_ge(s_dve.h, rsum_ready)
        i = v.reciprocal(out=rz2d[:], in_=Z2d[:])
        n1 = s_dve.inc(i)
        v.wait_ge(s_dve.h, n1)
        i = v.tensor_scalar(out=im2d[:],
                            in0=eL2d[:], scalar1=rsum[:, 0:1], scalar2=None,
                            op0=ALU.mult)
        n2 = s_dve.inc(i)
        v.wait_ge(s_dve.h, n2)
        i = v.tensor_tensor(out=im2d[:], in0=im2d[:], in1=rz2d[:], op=ALU.mult)
        s_dve.inc(i)
    blk.vector(z_fin)
    dv1 = s_dve.n

    ctl_dma(IMD[:].rearrange("(p q) -> p q", p=128), im2d[:],
            waits=[(s_dve, dv1)])
    ld_imr = ctl_dma(imr[:],
                     IMD[:].rearrange("(a s) -> a s", a=1).to_broadcast([64, S]),
                     waits=[(s_pe, pe_outer)])
    _maybe_tap("im2d", im2d[:, :])

    def nw_acc(v, ld_imr=ld_imr):
        v.wait_ge(s_ctl.h, ld_imr)
        v.wait_ge(s_act.h, etp_ready)
        i = v.scalar_tensor_tensor(out=e_tp[:], in0=e_tp[:], scalar=1.0, in1=imr[:],
                                   op0=ALU.mult, op1=ALU.mult, accum_out=nwacc[:])
        n1 = s_dve.inc(i)
        v.wait_ge(s_dve.h, n1)
        i = v.tensor_copy(out=inT[:, 64:65], in_=nwacc[:])
        s_dve.inc(i)
    blk.vector(nw_acc)
    dv1 = s_dve.n

    def nws_mm(t, dv1=dv1):
        t.wait_ge(s_dve.h, dv1)
        i = t.matmul(out=psD[0:1, 302:303], lhsT=inT[:, 64:65], rhs=ones64[:],
                     start=True, stop=True)
        s_pe.inc(i)
    blk.tensor(nws_mm)
    pe1 = s_pe.n

    def nws_fin(v, pe1=pe1):
        v.wait_ge(s_pe.h, pe1)
        i = v.tensor_scalar(out=scal[0:1, 2:3], in0=psD[0:1, 302:303],
                            scalar1=EPS_NW, scalar2=None, op0=ALU.add)
        n1 = s_dve.inc(i)
        v.wait_ge(s_dve.h, n1)
        i = v.reciprocal(out=scal[0:1, 3:4], in_=scal[0:1, 2:3])
        s_dve.inc(i)
    blk.vector(nws_fin)
    dv1 = s_dve.n

    def nwr_rep(t, dv1=dv1):
        t.wait_ge(s_dve.h, dv1)
        i = t.matmul(out=psD[0:64, 303:304], lhsT=onesf[:, 0:64],
                     rhs=scal[0:1, 3:4], start=True, stop=True)
        s_pe.inc(i)
    blk.tensor(nwr_rep)
    pe1 = s_pe.n

    def nw_fin(v, pe1=pe1):
        v.wait_ge(s_pe.h, pe1)
        i = v.scalar_tensor_tensor(out=nw64[:], in0=nwacc[:], scalar=1.0,
                                   in1=psD[0:64, 303:304], op0=ALU.mult, op1=ALU.mult)
        s_dve.inc(i)
    blk.vector(nw_fin)
    dv1 = s_dve.n
    _maybe_tap("nw64", nw64[:, :])

    def nwt_tr(t, dv1=dv1):
        t.wait_ge(s_dve.h, dv1)
        i = t.transpose(out=psD[0:1, 304:368], in_=nw64[:], identity=idf[0:64, 0:64])
        s_pe.inc(i)
    blk.tensor(nwt_tr)
    pe1 = s_pe.n

    def nwrow_ev(v, pe1=pe1, lst_free=lst_free):
        v.wait_ge(s_pe.h, pe1)
        v.wait_ge(s_ld.h, lst_free)
        i = v.tensor_copy(out=Lst[0:1, 0:64], in_=psD[0:1, 304:368])
        s_dve.inc(i)
    blk.vector(nwrow_ev)
    dv1 = s_dve.n

    def nwrep_mm(t, dv1=dv1):
        t.wait_ge(s_dve.h, dv1)
        i = t.matmul(out=psD[:, 368:432], lhsT=onesf[:], rhs=Lst[0:1, 0:64],
                     start=True, stop=True)
        s_pe.inc(i)
    blk.tensor(nwrep_mm)
    pe1 = s_pe.n

    def nwrep_ev(v, pe1=pe1):
        v.wait_ge(s_pe.h, pe1)
        i = v.tensor_copy(out=nwrep[:], in_=psD[:, 368:432])
        s_dve.inc(i)
    blk.vector(nwrep_ev)
    nwrep_ready = s_dve.n


    ffn_pe_done = s_pe.n

    kdef = list(range(0, 64)) + list(range(160, 256))
    kdef_waits = [(s_pe, s_pe.n), (s_dve, s_dve.n)]
    k4_tr = {-2: s_pe.n, -1: s_pe.n}
    k4_i = [0]

    def emit_k4(g4):
        kt0 = kdef[4 * g4]
        kb = 4 + (kt0 // 4) % 2
        kp = k4_i[0] % 2
        k4_i[0] += 1

        def k4_ld(sy, kt0=kt0, kp=kp, slot_gate=k4_tr.get(k4_i[0] - 3, 0)):
            if slot_gate:
                sy.wait_ge(s_pe.h, slot_gate)
            i = sy.dma_start(
                out=kst4[:, kp * 512:(kp + 1) * 512].rearrange("p (g r) -> p g r", g=4),
                in_=KD[kt0 * 128:(kt0 + 4) * 128, :].rearrange("(g p) r -> p g r", g=4))
            s_k2[kp].incd(i)
        blk.sync(k4_ld)
        ld_k = s_k2[kp].n

        def k4_tr_f(t, kt0=kt0, kb=kb, kp=kp, ld_k=ld_k,
                    gate=kev_bank[4 + (kt0 // 4) % 2]):
            t.wait_ge(s_k2[kp].h, ld_k)
            if gate:
                t.wait_ge(s_act.h, gate)
            for g in range(4):
                i = t.transpose(out=bank(kb)[:, g * 128:(g + 1) * 128],
                                in_=kst4[:, kp * 512 + g * 128:kp * 512 + (g + 1) * 128],
                                identity=idf[:])
            s_pe.inc(i)
        blk.tensor(k4_tr_f)
        k4_tr[k4_i[0] - 1] = s_pe.n
        pe_k = s_pe.n

        def k4_ev(s, kt0=kt0, kb=kb, pe_k=pe_k, ev_waits=tuple(kdef_waits)):
            s.wait_ge(s_pe.h, pe_k)
            for sem, cnt in ev_waits:
                s.wait_ge(sem.h, cnt)
            i = s.activation(out=KT[:, kt0 * 128:kt0 * 128 + 512],
                             in_=bank(kb), func=AF.Copy)
            s_act.inc(i)
        blk.scalar(k4_ev)
        kev_bank[kb] = s_act.n

    sc_cons = {}
    for n in range(NC):
        slot = n % 2

        def sc_ld(sy, n=n, slot=slot):
            if n < 2:
                sy.wait_ge(s_dve.h, nwrep_ready)
                sy.wait_ge(s_ctl.h, ld_imr)
                sy.wait_ge(s_act.h, etp_ready)
            else:
                sy.wait_ge(s_dve.h, sc_cons[n - 2])
            i = sy.dma_start(out=cbuf[:, slot * 1024:(slot + 1) * 1024]
                             .rearrange("p (q r) -> p q r", q=DCH),
                             in_=CP[n, :, :].rearrange("(q p) r -> p q r", q=DCH))
            s_c2[slot].incd(i)
        blk.sync(sc_ld)
        ld_c = s_c2[slot].n

        def sc_acc(v, n=n, slot=slot, ld_c=ld_c):
            v.wait_ge(s_c2[slot].h, ld_c)
            if n:
                v.wait_ge(s_dve.h, s_dve.n)
            for c in range(DCH):
                src = cbuf[:, slot * 1024 + c * 128:slot * 1024 + (c + 1) * 128]
                dst = SCm[:, c * 128:(c + 1) * 128]
                if n == 0:
                    i = v.tensor_scalar(out=dst, in0=src, scalar1=nwrep[:, n:n + 1],
                                        scalar2=None, op0=ALU.mult)
                else:
                    i = v.scalar_tensor_tensor(out=dst, in0=src,
                                               scalar=nwrep[:, n:n + 1], in1=dst,
                                               op0=ALU.mult, op1=ALU.add)
            s_dve.inc(i)
        blk.vector(sc_acc)
        sc_cons[n] = s_dve.n
        if n < 40:
            emit_k4(n)
    kt_ready = s_act.n

    def sc_cast(v):
        v.wait_ge(s_dve.h, s_dve.n)
        i = v.tensor_copy(out=SCb[:], in_=SCm[:])
        s_dve.inc(i)
    blk.vector(sc_cast)
    scb_ready = s_dve.n
    _maybe_tap("SCm", SCm[:, :])

    qt_bank_free = {0: 0, 1: 0}
    for st in range(8):
        bk = st % 2

        def q_mm(t, st=st, bk=bk, gate=qt_bank_free[st % 2]):
            t.wait_ge(s_dve.h, scb_ready)
            if gate:
                t.wait_ge(s_act.h, gate)
            for c in range(DCH):
                i = t.matmul(out=bank(bk), lhsT=SCb[:, c * 128:(c + 1) * 128],
                             rhs=xnT(c)[:, st * 512:(st + 1) * 512],
                             start=(c == 0), stop=(c == DCH - 1))
            s_pe.inc(i)
        blk.tensor(q_mm)
        pe1 = s_pe.n

        def q_ev(s, st=st, bk=bk, pe1=pe1):
            s.wait_ge(s_pe.h, pe1)
            i = s.activation(out=QT[:, st * 512:(st + 1) * 512], in_=bank(bk),
                             func=AF.Copy, scale=ckon[:, 1:2])
            s_act.inc(i)
        blk.scalar(q_ev)
        qt_bank_free[bk] = s_act.n
    qt_ready = s_act.n
    _maybe_tap("QT", QT[:, :])

    s_oe = Sem(ctx, nc, "s_oe")
    s_oo = Sem(ctx, nc, "s_oo")
    opar = [s_oe, s_oo]
    sc_bank_free = [0] * 6
    half_read = {}
    wsum_done = {}
    w8_ready = {}
    gidx_ready = {}
    nmax_n = {}
    exp_n = {}
    pool_ready = {}
    base_all = s_out.n

    def emit_wsum(tt):
        def d_ws(v, tt=tt):
            par = tt % 2
            acc = xb[:, par * D:(par + 1) * D]
            v.wait_ge(s_act.h, exp_n[tt])
            i = v.tensor_scalar(out=d32[:, 0:8], in0=e8[:], scalar1=1.0,
                                scalar2=None, op0=ALU.mult, op1=ALU.add,
                                accum_out=sm1[:, 0:1])
            n1 = s_dve.inc(i)
            v.wait_ge(s_dve.h, n1)
            i = v.reciprocal(out=sm1[:, 1:2], in_=sm1[:, 0:1])
            n2 = s_dve.inc(i)
            v.wait_ge(s_dve.h, n2)
            i = v.tensor_scalar(out=w8[:], in0=e8[:], scalar1=sm1[:, 1:2],
                                scalar2=None, op0=ALU.mult)
            w8_ready[tt] = s_dve.inc(i)
            v.wait_ge(s_dve.h, w8_ready[tt])
            v.wait_ge(s_pl.h, pool_ready[tt])
            v.wait_ge(opar[tt % 2].h, opar[tt % 2].n)
            for k in range(KK):
                if k:
                    v.wait_ge(s_dve.h, s_dve.n)
                i = v.scalar_tensor_tensor(
                    out=acc, in0=Vg[:].rearrange("p (k d) -> p k d", k=KK)[:, k, :],
                    scalar=w8[:, k:k + 1],
                    in1=(bsb[:, par * D:(par + 1) * D] if k == 0 else acc),
                    op0=ALU.mult, op1=ALU.add)
                s_dve.inc(i)
        blk.vector(d_ws)
        wsum_done[tt] = s_dve.n

    def emit_out(tt):
        def p_out(g, tt=tt, wn=wsum_done[tt]):
            g.wait_ge(s_dve.h, wn)
            par = tt % 2
            i = g.dma_start(out=OUT[tt * 128:(tt + 1) * 128, :],
                            in_=xb[:, par * D:(par + 1) * D])
            opar[par].incd(i)
        blk.gpsimd(p_out)

    NP4 = N_P4[0]
    for tt in range(NP4):

        for j in range(64):
            b = j % 6
            q = j // 16
            qs = q % 2

            def s_mm(t, tt=tt, j=j, b=b, gate=sc_bank_free[j % 6]):
                if tt == 0 and j == 0:
                    t.wait_ge(s_act.h, qt_ready)
                    t.wait_ge(s_act.h, kt_ready)
                if gate:
                    t.wait_ge(s_act.h, gate)
                i = t.matmul(out=bank(b), lhsT=QT[:, tt * 128:(tt + 1) * 128],
                             rhs=KT[:, j * 512:(j + 1) * 512], start=True, stop=True)
                s_pe.inc(i)
            blk.tensor(s_mm)
            mm_n = s_pe.n

            war = (half_read.get((tt, q - 2), 0) if q >= 2
                   else half_read.get((tt - 1, q + 2), 0))

            def s_ev(s, tt=tt, j=j, b=b, q=q, qs=qs, mm_n=mm_n, war=war):
                s.wait_ge(s_pe.h, mm_n)
                if war:
                    s.wait_ge(s_dve.h, war)
                i = s.activation(out=scq[:, qs * 8192 + (j % 16) * 512:
                                         qs * 8192 + (j % 16) * 512 + 512],
                                 in_=bank(b), func=AF.Copy)
                s_act.inc(i)
            blk.scalar(s_ev)
            sc_bank_free[b] = s_act.n

            if j == 15 and tt > 0:
                def a_exp(s, p=tt - 1):
                    s.wait_ge(s_dve.h, nmax_n[p])
                    i = s.activation(out=e8[:], in_=gvf[:], func=AF.Exp,
                                     bias=nmax[:, 0:1])
                    s_act.inc(i)
                blk.scalar(a_exp)
                exp_n[tt - 1] = s_act.n

            if j % 16 == 15:
                qe = s_act.n

                def d_q(v, tt=tt, q=q, qs=qs, qe=qe):
                    v.wait_ge(s_act.h, qe)
                    i = v.max(out=cman[:, 8 * q:8 * q + 8],
                              in_=scq[:, qs * 8192:(qs + 1) * 8192])
                    n1 = s_dve.inc(i)
                    v.wait_ge(s_dve.h, n1)
                    i = v.max_index(out=cmi[:, 8 * q:8 * q + 8],
                                    in_max=cman[:, 8 * q:8 * q + 8],
                                    in_values=scq[:, qs * 8192:(qs + 1) * 8192])
                    half_read[(tt, q)] = s_dve.inc(i)
                blk.vector(d_q)
                if q == 1 and tt > 0:
                    emit_wsum(tt - 1)

        def d_top(v, tt=tt):
            v.wait_ge(s_dve.h, s_dve.n)
            i = v.max(out=gv[:], in_=cman[:])
            n4 = s_dve.inc(i)
            v.wait_ge(s_dve.h, n4)
            i = v.max_index(out=pos[:], in_max=gv[:], in_values=cman[:])
            n5 = s_dve.inc(i)
            v.wait_ge(s_dve.h, n5)
            v.tensor_scalar(out=posf[:], in0=pos[:], scalar1=0.0, scalar2=None,
                            op0=ALU.add)
            i = v.tensor_scalar(out=cgi[:], in0=cmi[:], scalar1=0.0, scalar2=None,
                                op0=ALU.add)
            n6 = s_dve.inc(i)
            v.wait_ge(s_dve.h, n6)
            i = v.tensor_tensor(out=cgi[:], in0=cgi[:], in1=hb32[:], op=ALU.add)
            n7 = s_dve.inc(i)
            v.wait_ge(s_dve.h, n7)
            for k in range(KK):
                i = v.scalar_tensor_tensor(out=d32[:], in0=io32[:],
                                           scalar=posf[:, k:k + 1], in1=cgi[:],
                                           op0=ALU.is_equal, op1=ALU.mult,
                                           accum_out=gixf[:, k:k + 1])
            n8 = s_dve.inc(i)
            v.wait_ge(s_dve.h, n8)
            i = v.tensor_scalar(out=gixu[:], in0=gixf[:], scalar1=0.0,
                                scalar2=None, op0=ALU.add)
            gidx_ready[tt] = s_dve.inc(i)
            i = v.tensor_scalar(out=gvf[:], in0=gv[:], scalar1=0.0, scalar2=None,
                                op0=ALU.add)
            n9 = s_dve.inc(i)
            v.wait_ge(s_dve.h, n9)
            i = v.tensor_scalar(out=nmax[:], in0=gvf[:, 0:1], scalar1=-1.0,
                                scalar2=None, op0=ALU.mult)
            nmax_n[tt] = s_dve.inc(i)
        blk.vector(d_top)

        for k in range(KK):
            def p_g(g, tt=tt, k=k):
                if k == 0:
                    g.wait_ge(s_dve.h, gidx_ready[tt])
                    if tt > 0:
                        g.wait_ge(s_dve.h, wsum_done[tt - 1])
                i = g.indirect_dma_start(
                    out=Vg[:].rearrange("p (k d) -> p k d", k=KK)[:, k, :],
                    out_offset=None, in_=VD[:, :],
                    in_offset=IndirectOffsetOnAxis(ap=gixu[:, k:k + 1], axis=0))
                s_pl.incd(i)
            blk.gpsimd(p_g)

        def p_b(g, tt=tt, base_all=base_all):
            g.wait_ge(s_out.h, base_all)
            if tt >= 2:
                g.wait_ge(s_dve.h, wsum_done[tt - 2])
            i = g.dma_start(out=bsb[:, (tt % 2) * D:(tt % 2 + 1) * D],
                            in_=BASE[tt * 128:(tt + 1) * 128, :])
            s_pl.incd(i)
        blk.gpsimd(p_b)
        pool_ready[tt] = s_pl.n

        if tt > 0:
            emit_out(tt - 1)

    def a_exp_last(s):
        s.wait_ge(s_dve.h, nmax_n[NP4 - 1])
        i = s.activation(out=e8[:], in_=gvf[:], func=AF.Exp, bias=nmax[:, 0:1])
        s_act.inc(i)
    blk.scalar(a_exp_last)
    exp_n[NP4 - 1] = s_act.n
    emit_wsum(NP4 - 1)
    emit_out(NP4 - 1)
    _maybe_tap("gv", gv[:, :])
    _maybe_tap("cmi", cmi[:, :])
    _maybe_tap("cman", cman[:, :])
    _maybe_tap("pos", pos[:, :])
    _maybe_tap("gixf", gixf[:, :])
    _maybe_tap("gixu", gixu[:, :])
    _maybe_tap("e8", e8[:, :])
    _maybe_tap("w8", w8[:, :])
    _maybe_tap("bsb", bsb[:, :])
    _maybe_tap("Vg", Vg[:, :])
    _maybe_tap("cgi", cgi[:, :])
    _maybe_tap("KTt", KT[:, :])
    _maybe_tap("scq", scq[:, :])

    def fin(g):
        g.wait_ge(s_oe.h, s_oe.n)
        g.wait_ge(s_oo.h, s_oo.n)
        g.wait_ge(s_out.h, s_out.n)
        g.wait_ge(s_pl.h, s_pl.n)
        g.wait_ge(s_ctl.h, s_ctl.n)
    blk.gpsimd(fin)

    ctx.close()
    return nc


_TAP_SHAPES = {
    "gv": ([128, 8], BF16),
    "cmi": ([128, 32], U16),
    "cman": ([128, 32], BF16),
    "pos": ([128, 8], U16),
    "gixf": ([128, 8], F32),
    "gixu": ([128, 8], U32),
    "e8": ([128, 8], F32),
    "w8": ([128, 8], F32),
    "bsb": ([128, 2048], F32),
    "Vg": ([128, 8 * 1024], F32),
    "cgi": ([128, 32], F32),
    "KTt": ([128, NK], BF16),
    "scq": ([128, 2 * 8192], BF16),
    "Apow": ([64, 64 * SD], BF16),
    "uT": ([64, S], BF16),
    "hvec": ([64, 1], F32),
    "im2d": ([128, 32], BF16),
    "nw64": ([64, 1], F32),
    "SCm": ([128, DCH * 128], F32),
    "QT": ([128, S], BF16),
}


def kernel(L0_attn_x, L0_attn_A, L0_attn_Bm, L0_attn_W_imp, L0_attn_W_router,
           L0_attn_compress, L0_attn_K, L0_attn_V, L0_ffn_weight):
    x = np.ascontiguousarray(np.asarray(L0_attn_x, dtype=np.float32))
    com = {
        "A": np.ascontiguousarray(np.asarray(L0_attn_A, dtype=np.float32)),
        "Bm": np.ascontiguousarray(np.asarray(L0_attn_Bm, dtype=np.float32)),
        "Wimp": np.ascontiguousarray(np.asarray(L0_attn_W_imp, dtype=np.float32)),
        "Wrout": np.ascontiguousarray(np.asarray(L0_attn_W_router, dtype=np.float32)),
        "comp": np.ascontiguousarray(np.asarray(L0_attn_compress, dtype=np.float32)),
        "K": np.ascontiguousarray(np.asarray(L0_attn_K, dtype=np.float32)),
        "V": np.ascontiguousarray(np.asarray(L0_attn_V, dtype=np.float32)),
        "Wf": np.ascontiguousarray(np.asarray(L0_ffn_weight, dtype=np.float32)),
    }
    nc = build_program()
    in_maps = [dict(com, x=np.ascontiguousarray(x[b])) for b in range(B)]
    res = run_bass_kernel_spmd(nc, in_maps, core_ids=list(range(B)))
    out = np.stack([np.asarray(res.results[b]["out"]) for b in range(B)], axis=0)
    return out


# version 08
# speedup vs baseline: 1.1561x; 1.1150x over previous
import contextlib

import numpy as np

import concourse.bass as bass
import concourse.mybir as mybir
from concourse.bass import IndirectOffsetOnAxis
from concourse.bass_utils import run_bass_kernel_spmd

F32 = mybir.dt.float32
BF16 = mybir.dt.bfloat16
U16 = mybir.dt.uint16
U32 = mybir.dt.uint32
ALU = mybir.AluOpType
AF = mybir.ActivationFunctionType

B, S, D, R, NC, NK, SD, KK = 8, 4096, 1024, 128, 64, 32768, 64, 8
TT = S // 128
DCH = D // 128
EPS_LN = 1e-5
EPS_W = 1e-5
EPS_NW = 1e-8
RSQRT_R = float(1.0 / np.sqrt(np.float64(R)))

_DEBUG_TAPS = []
N_P4 = [TT]


class Sem:
    def __init__(self, ctx, nc, name):
        self.h = ctx.enter_context(nc.semaphore(name))
        self.n = 0

    def inc(self, inst, k=1):
        inst.then_inc(self.h, k)
        self.n += k
        return self.n

    def incd(self, inst):
        return self.inc(inst, 16)

    def wait(self, eng, n=None):
        eng.wait_ge(self.h, self.n if n is None else n)


def build_program():
    nc = bass.Bass("TRN2", target_bir_lowering=False, debug=False)

    X = nc.dram_tensor("x", [S, D], F32, kind="ExternalInput").ap()
    A_ = nc.dram_tensor("A", [SD, SD], F32, kind="ExternalInput").ap()
    BM = nc.dram_tensor("Bm", [D, SD], F32, kind="ExternalInput").ap()
    WI = nc.dram_tensor("Wimp", [D, SD], F32, kind="ExternalInput").ap()
    WR = nc.dram_tensor("Wrout", [NC, D], F32, kind="ExternalInput").ap()
    CP = nc.dram_tensor("comp", [NC, D, R], F32, kind="ExternalInput").ap()
    KD = nc.dram_tensor("K", [NK, R], F32, kind="ExternalInput").ap()
    VD = nc.dram_tensor("V", [NK, D], F32, kind="ExternalInput").ap()
    WF = nc.dram_tensor("Wf", [D, D], F32, kind="ExternalInput").ap()
    OUT = nc.dram_tensor("out", [S, D], F32, kind="ExternalOutput").ap()
    BASE = nc.dram_tensor("base", [S, D], F32, kind="Internal").ap()
    LDR = nc.dram_tensor("Ldr", [S], F32, kind="Internal").ap()
    ZDR = nc.dram_tensor("Zdr", [S], F32, kind="Internal").ap()
    IMD = nc.dram_tensor("imd", [S], BF16, kind="Internal").ap()

    taps = {}
    for name in _DEBUG_TAPS:
        shape, dt = _TAP_SHAPES[name]
        taps[name] = nc.dram_tensor("tap_" + name, shape, dt, kind="ExternalOutput").ap()

    ctx = contextlib.ExitStack()

    at = nc.alloc_sbuf_tensor_at
    B0 = 16512
    KT = at("KT", [128, NK], BF16, offset=B0).ap()
    wstg = at("wstg", [128, 4096], F32, offset=B0).ap()
    negt = at("negt", [128, 8192], BF16, offset=B0 + 16384).ap()
    posx = at("posx", [128, 4096], BF16, offset=B0 + 32768).ap()
    uT = at("uT", [64, S], BF16, offset=B0 + 40960).ap()
    WfT = at("WfT", [128, DCH * D], BF16, offset=B0 + 49152).ap()
    big = at("big", [128, NK], BF16, offset=82048).ap()
    scq = at("scq", [128, 2 * 8192], BF16, offset=82048).ap()
    Vg = at("Vg", [128, KK * D], F32, offset=82048 + 32768).ap()
    QT = at("QT", [128, S], BF16, offset=147584).ap()
    kst = at("kst", [128, 256], F32, offset=147584).ap()
    kst4 = at("kst4", [128, 2 * 512], F32, offset=147584).ap()
    xnt2 = at("xnt2", [128, D], BF16, offset=147584 + 4096).ap()
    junk = at("junk", [128, D], F32, offset=B0).ap()
    imr = at("imr", [64, S], BF16, offset=B0).ap()
    UB = 155776
    bsb = at("bsb", [128, 2 * D], F32, offset=UB).ap()
    Apow = at("Apow", [64, 64 * SD], BF16, offset=UB + 8192).ap()
    xb2 = at("xb2", [128, 2048], F32, offset=UB + 8192).ap()
    xb3 = at("xb3", [128, 2048], F32, offset=B0).ap()
    e_tp = at("e_tp", [64, S], BF16, offset=UB + 8192).ap()
    cbuf = at("cbuf", [128, 2048], F32, offset=UB + 8192).ap()
    A64p = at("A64p", [64, 64 * SD], BF16, offset=UB + 16384).ap()
    Lst = at("Lst", [1, 512], F32, offset=UB + 16384).ap()
    WiT = at("WiT", [64, D], F32, offset=UB + 18432).ap()
    SCm = at("SCm", [128, DCH * 128], F32, offset=UB + 16384).ap()
    SCb = at("SCb", [128, DCH * 128], BF16, offset=UB + 20480).ap()
    o = 180352

    def tail(name, shape, dt):
        nonlocal o
        t = at(name, shape, dt, offset=o)
        per = int(np.prod(shape[1:])) * mybir.dt.size(dt)
        o += (per + 31) // 32 * 32
        return t.ap()

    xb = tail("xb", [128, 2 * D], F32)
    xnt = tail("xnt", [128, D], BF16)
    Abf = tail("Abf", [64, SD], BF16)
    Bmb = tail("Bmb", [128, DCH * SD], BF16)
    WrT = tail("WrT", [128, DCH * NC], BF16)
    idb = tail("idb", [128, 128], BF16)
    idf = tail("idf", [128, 128], F32)
    onesf = tail("onesf", [1, 128], F32)
    ones128 = tail("ones128", [128, 1], F32)
    ones64 = tail("ones64", [64, 1], BF16)
    stats = tail("stats", [128, 16], F32)
    scal = tail("scal", [1, 8], F32)
    srep = tail("srep", [128, 2], F32)
    hp = tail("hp", [128, DCH], F32)
    hpb = tail("hpb", [128, DCH], BF16)
    Tm = tail("Tm", [64, SD], BF16)
    inT = tail("inT", [64, SD + 2], BF16)
    hvec = tail("hvec", [64, 1], F32)
    nwacc = tail("nwacc", [64, 1], F32)
    nw64 = tail("nw64", [64, 1], F32)
    nwrep = tail("nwrep", [128, NC], F32)
    L2d = tail("L2d", [128, 32], F32)
    eL2d = tail("eL2d", [128, 32], F32)
    Z2d = tail("Z2d", [128, 32], F32)
    rz2d = tail("rz2d", [128, 32], F32)
    im2d = tail("im2d", [128, 32], BF16)
    rsum = tail("rsum", [128, 1], F32)
    io32 = tail("io32", [128, 32], F32)
    hb32 = tail("hb32", [128, 32], F32)
    cman = tail("cman", [128, 32], BF16)
    cmi = tail("cmi", [128, 32], U16)
    cgi = tail("cgi", [128, 32], F32)
    gv = tail("gv", [128, 8], BF16)
    gvf = tail("gvf", [128, 8], F32)
    pos = tail("pos", [128, 8], U16)
    posf = tail("posf", [128, 8], F32)
    gixf = tail("gixf", [128, 8], F32)
    gixu = tail("gixu", [128, 8], U32)
    e8 = tail("e8", [128, 8], F32)
    w8 = tail("w8", [128, 8], F32)
    sm1 = tail("sm1", [128, 2], F32)
    nmax = tail("nmax", [128, 1], F32)
    d32 = tail("d32", [128, 32], F32)
    ckon = tail("ckon", [128, 2], F32)
    assert o <= 196576 - 32, f"sbuf tail overflow: {o}"

    psA = ctx.enter_context(nc.psum_tensor("psA", [128, 3072], F32))
    psB = ctx.enter_context(nc.psum_tensor("psB", [128, 1024], BF16))
    psD = ctx.enter_context(nc.psum_tensor("psD", [128, 512], F32))
    bank = lambda b: psA[:, 512 * b:512 * (b + 1)]

    s_pe = Sem(ctx, nc, "s_pe")
    s_dve = Sem(ctx, nc, "s_dve")
    s_act = Sem(ctx, nc, "s_act")
    s_ld = Sem(ctx, nc, "s_ld")
    s_pl = Sem(ctx, nc, "s_pl")
    s_out = Sem(ctx, nc, "s_out")
    s_x2 = [Sem(ctx, nc, "s_x0"), Sem(ctx, nc, "s_x1")]
    s_k2 = [Sem(ctx, nc, "s_k0"), Sem(ctx, nc, "s_k1")]
    s_c2 = [Sem(ctx, nc, "s_c0"), Sem(ctx, nc, "s_c1")]
    s_oe = Sem(ctx, nc, "s_oe")
    s_oo = Sem(ctx, nc, "s_oo")
    opar = [s_oe, s_oo]
    s_ctl = Sem(ctx, nc, "s_ctl")

    blk = ctx.enter_context(nc.Block())

    def ctl_dma(out_ap, in_ap, waits=()):
        pre = s_ctl.n

        def _c(g, out_ap=out_ap, in_ap=in_ap, waits=tuple(waits), pre=pre):
            g.wait_ge(s_ctl.h, pre)
            for sem, cnt in waits:
                g.wait_ge(sem.h, cnt)
            i = g.dma_start(out=out_ap, in_=in_ap)
            s_ctl.incd(i)
        blk.gpsimd(_c)
        return s_ctl.n

    def xnT(c):
        return big[:, c * S:(c + 1) * S]

    def _maybe_tap(name, src_ap):
        if name not in taps:
            return
        dn = s_dve.n
        an = s_act.n
        pn = s_pe.n

        def t_dma(g, name=name, src_ap=src_ap, dn=dn, an=an, pn=pn):
            g.wait_ge(s_dve.h, dn)
            g.wait_ge(s_act.h, an)
            g.wait_ge(s_pe.h, pn)
            i = g.dma_start(out=taps[name][:], in_=src_ap)
            s_pl.incd(i)
        blk.gpsimd(t_dma)

    def c_pool(g):
        g.memset(idb[:], 0.0)
        g.affine_select(out=idb[:], in_=idb[:], compare_op=ALU.not_equal,
                        fill=1.0, base=0, pattern=[[-1, 128]], channel_multiplier=1)
        g.memset(idf[:], 0.0)
        g.affine_select(out=idf[:], in_=idf[:], compare_op=ALU.not_equal,
                        fill=1.0, base=0, pattern=[[-1, 128]], channel_multiplier=1)
        g.iota(out=io32[:], pattern=[[1, 32]], base=0, channel_multiplier=0,
               allow_small_or_imprecise_dtypes=True)
        g.iota(out=hb32[:], pattern=[[8192, 4], [0, 8]], base=0,
               channel_multiplier=0, allow_small_or_imprecise_dtypes=True)
        g.memset(ones64[:], 1.0)
        g.memset(onesf[:], 1.0)
        g.memset(ones128[:], 1.0)
        g.memset(ckon[:, 0:1], EPS_LN)
        i = g.memset(ckon[:, 1:2], RSQRT_R)
        s_pl.inc(i)
    blk.gpsimd(c_pool)
    pl_consts = s_pl.n

    ctl_dma(wstg[0:64, 0:64], A_[:, :])
    ld_small = ctl_dma(wstg[:, 64:64 + DCH * SD].rearrange("p (c e) -> p c e", c=DCH),
                       BM[:, :].rearrange("(c p) e -> p c e", c=DCH))

    def c_cast(v, ld_small=ld_small):
        v.wait_ge(s_ctl.h, ld_small)
        v.wait_ge(s_pl.h, pl_consts)
        v.tensor_copy(out=Abf[:], in_=wstg[0:64, 0:64])
        i = v.tensor_copy(out=Bmb[:], in_=wstg[:, 64:64 + DCH * SD])
        s_dve.inc(i)
    blk.vector(c_cast)
    cast_done = s_dve.n

    def act_preload(s):
        from concourse.hw_specs import get_activation_tables
        need = {AF.Ln, AF.Exp, AF.Identity, AF.Copy}
        for idx, funcs in enumerate(get_activation_tables(nc.m.arch).values()):
            if need.issubset(funcs):
                s.add_instruction(mybir.InstLoadActFuncSet(
                    name=nc.get_next_instruction_name(),
                    act_func_set_id=idx, ins=[], outs=[]))
                return
        raise ValueError("no table set covers all activation funcs")
    blk.scalar(act_preload)

    habs = []
    for h in range(2):
        def wfa_ld(sy, h=h):
            if h == 0:
                sy.wait_ge(s_dve.h, cast_done)
            else:
                sy.wait_ge(s_dve.h, s_dve.n)
            i = sy.dma_start(out=wstg[:].rearrange("p (c i) -> p c i", c=4),
                             in_=WF[:, :].rearrange("(c p) i -> p c i", c=DCH)
                             [:, 4 * h:4 * (h + 1), :])
            s_ld.incd(i)
        blk.sync(wfa_ld)
        ldh = s_ld.n

        def wf_abs(v, h=h, ldh=ldh):
            v.wait_ge(s_ld.h, ldh)
            i = v.reduce_sum(out=stats[:, h:h + 1], in_=wstg[:],
                             axis=mybir.AxisListType.X, apply_absolute_value=True)
            s_dve.inc(i)
        blk.vector(wf_abs)
        habs.append(s_dve.n)

    def wf_sc(v):
        v.wait_ge(s_dve.h, habs[1])
        i = v.tensor_tensor(out=stats[:, 2:3], in0=stats[:, 0:1],
                            in1=stats[:, 1:2], op=ALU.add)
        s_dve.inc(i)
    blk.vector(wf_sc)
    dv1 = s_dve.n

    def wfs_mm(t, dv1=dv1):
        t.wait_ge(s_dve.h, dv1)
        i = t.matmul(out=psD[0:1, 432:433], lhsT=stats[:, 2:3],
                     rhs=ones128[:], start=True, stop=True)
        s_pe.inc(i)
    blk.tensor(wfs_mm)
    pe1 = s_pe.n

    def wfs_fin(v, pe1=pe1):
        v.wait_ge(s_pe.h, pe1)
        i = v.tensor_scalar(out=scal[0:1, 5:6], in0=psD[0:1, 432:433],
                            scalar1=1.0 / (D * D), scalar2=EPS_W,
                            op0=ALU.mult, op1=ALU.max)
        n1 = s_dve.inc(i)
        v.wait_ge(s_dve.h, n1)
        i = v.reciprocal(out=scal[0:1, 6:7], in_=scal[0:1, 5:6])
        s_dve.inc(i)
    blk.vector(wfs_fin)
    dv1 = s_dve.n

    def srep_mm(t, dv1=dv1):
        t.wait_ge(s_dve.h, dv1)
        t.matmul(out=psD[:, 433:434], lhsT=onesf[:], rhs=scal[0:1, 5:6],
                 start=True, stop=True)
        i = t.matmul(out=psD[:, 434:435], lhsT=onesf[:], rhs=scal[0:1, 6:7],
                     start=True, stop=True)
        s_pe.inc(i)
    blk.tensor(srep_mm)
    pe1 = s_pe.n

    def srep_ev(v, pe1=pe1):
        v.wait_ge(s_pe.h, pe1)
        v.tensor_scalar(out=srep[:, 0:1], in0=psD[:, 433:434], scalar1=-1.0,
                        scalar2=None, op0=ALU.mult)
        i = v.tensor_copy(out=srep[:, 1:2], in_=psD[:, 434:435])
        s_dve.inc(i)
    blk.vector(srep_ev)
    srep_ready = s_dve.n

    for h in range(2):
        def wft_ld(sy, h=h):
            sy.wait_ge(s_dve.h, s_dve.n)
            i = sy.dma_start(out=wstg[:].rearrange("p (c i) -> p c i", c=4),
                             in_=WF[:, :].rearrange("(c p) i -> p c i", c=DCH)
                             [:, 4 * h:4 * (h + 1), :])
            s_ld.incd(i)
        blk.sync(wft_ld)
        ldh = s_ld.n

        def wf_tern(v, h=h, ldh=ldh):
            v.wait_ge(s_ld.h, ldh)
            v.wait_ge(s_dve.h, srep_ready)
            i = v.tensor_scalar(out=wstg[:], in0=wstg[:], scalar1=srep[:, 1:2],
                                scalar2=None, op0=ALU.mult)
            n1 = s_dve.inc(i)
            v.wait_ge(s_dve.h, n1)
            i = v.tensor_scalar(out=posx[:], in0=wstg[:], scalar1=0.5,
                                scalar2=None, op0=ALU.is_gt)
            n2 = s_dve.inc(i)
            v.wait_ge(s_dve.h, n2)
            i = v.scalar_tensor_tensor(out=negt[:, h * 4096:(h + 1) * 4096],
                                       in0=wstg[:], scalar=-0.5, in1=posx[:],
                                       op0=ALU.is_lt, op1=ALU.subtract)
            s_dve.inc(i)
        blk.vector(wf_tern)
    tern_ready = s_dve.n

    for ci in range(DCH):
        def wft_tr(t, ci=ci):
            t.wait_ge(s_dve.h, tern_ready)
            if ci > 0:
                t.wait_ge(s_act.h, s_act.n)
            for co in range(DCH):
                i = t.transpose(out=psB[:, co * 128:(co + 1) * 128],
                                in_=negt[:, co * 1024 + ci * 128:co * 1024 + ci * 128 + 128],
                                identity=idb[:])
            s_pe.inc(i)
        blk.tensor(wft_tr)
        pe1 = s_pe.n

        def wft_ev(s, ci=ci, pe1=pe1):
            s.wait_ge(s_pe.h, pe1)
            i = s.activation(out=WfT[:, ci * 1024:(ci + 1) * 1024], in_=psB[:],
                             func=AF.Copy)
            s_act.inc(i)
        blk.scalar(wft_ev)
    wfT_ready = s_act.n


    kev_bank = {4: 0, 5: 0}
    ffn_bank_free = {0: 0, 1: 0, 2: 0, 3: 0}
    base_done = {}
    ktr_pe = {}
    kt_emitted = []

    def emit_ktile(kt, ev_waits=()):
        kb = 4 + (kt // 4) % 2
        slot = kt % 4
        kpar = len(kt_emitted) % 2
        kt_emitted.append(kt)

        def k_ld(g, kt=kt, kpar=kpar,
                 slot_gate=(ktr_pe[kt_emitted[-3]] if len(kt_emitted) >= 3 else 0)):
            g.wait_ge(s_dve.h, tern_ready)
            if slot_gate:
                g.wait_ge(s_pe.h, slot_gate)
            i = g.dma_start(out=kst[:, kpar * 128:(kpar + 1) * 128],
                            in_=KD[kt * 128:(kt + 1) * 128, :])
            s_k2[kpar].incd(i)
        blk.gpsimd(k_ld)
        ld_k = s_k2[kpar].n

        def k_tr(t, kb=kb, slot=slot, kpar=kpar, ld_k=ld_k,
                 gate=kev_bank[4 + (kt // 4) % 2]):
            t.wait_ge(s_k2[kpar].h, ld_k)
            if slot == 0 and gate:
                t.wait_ge(s_act.h, gate)
            i = t.transpose(out=bank(kb)[:, slot * 128:(slot + 1) * 128],
                            in_=kst[:, kpar * 128:(kpar + 1) * 128],
                            identity=idf[:])
            s_pe.inc(i)
        blk.tensor(k_tr)
        ktr_pe[kt] = s_pe.n

        if slot == 3:
            pe_k = s_pe.n

            def k_ev(s, kt=kt, kb=kb, pe_k=pe_k, ev_waits=tuple(ev_waits)):
                s.wait_ge(s_pe.h, pe_k)
                for sem, cnt in ev_waits:
                    s.wait_ge(sem.h, cnt)
                base_n = (kt - 3) * 128
                i = s.activation(out=KT[:, base_n:base_n + 512],
                                 in_=bank(kb), func=AF.Copy)
                s_act.inc(i)
            blk.scalar(k_ev)
            kev_bank[kb] = s_act.n

    prev_evac = [0, 0]
    norm_done = {}
    base_done = {}
    bst_n = {0: 0, 1: 0}
    xsem = [s_x2[0], s_x2[1], s_c2[0], s_c2[1], s_ld, s_ctl]

    def xslot(tt):
        sl = tt % 6
        buf = (xb, xb2, xb3)[sl // 2]
        return buf[:, (sl % 2) * D:(sl % 2 + 1) * D]

    def p1_A(tt):
        par = tt % 2
        sb = 8 * par

        def p1_ld(g, tt=tt):
            if tt == 0:
                g.wait_ge(s_pl.h, pl_consts)
            if tt in (4, 5):
                g.wait_ge(s_dve.h, tern_ready)
            if tt >= 6:
                g.wait_ge(s_dve.h, base_done[tt - 6])
            i = g.dma_start(out=xslot(tt), in_=X[tt * 128:(tt + 1) * 128, :])
            xsem[tt % 6].incd(i)
        blk.gpsimd(p1_ld)
        ld_n = xsem[tt % 6].n

        def p1_stats(v, tt=tt, par=par, sb=sb, ld_n=ld_n):
            v.wait_ge(xsem[tt % 6].h, ld_n)
            xt = xslot(tt)
            bns = d32[:, 16 * par:16 * par + 12]
            v.bn_stats(out=bns[:, 0:6], in_=xt[:, 0:512])
            i = v.bn_stats(out=bns[:, 6:12], in_=xt[:, 512:1024])
            n1 = s_dve.inc(i)
            v.wait_ge(s_dve.h, n1)
            i = v.bn_aggr(out=stats[:, sb:sb + 2], in_=bns[:])
            n2 = s_dve.inc(i)
            v.wait_ge(s_dve.h, n2)
            i = v.tensor_copy(out=stats[:, sb + 4:sb + 5], in_=stats[:, sb + 1:sb + 2])
            s_dve.inc(i)
        blk.vector(p1_stats)
        return s_dve.n

    def p1_B(tt, dve_var):
        par = tt % 2
        sb = 8 * par

        def p1_ln(s, sb=sb, dve_var=dve_var):
            s.wait_ge(s_dve.h, dve_var)
            i = s.activation(out=stats[:, sb + 5:sb + 6], in_=stats[:, sb + 4:sb + 5],
                             func=AF.Ln, bias=ckon[:, 0:1])
            s_act.inc(i)
        blk.scalar(p1_ln)
        act_ln = s_act.n

        def p1_half(v, sb=sb, act_ln=act_ln):
            v.wait_ge(s_act.h, act_ln)
            i = v.tensor_scalar(out=stats[:, sb + 6:sb + 7], in0=stats[:, sb + 5:sb + 6],
                                scalar1=-0.5, scalar2=None, op0=ALU.mult)
            s_dve.inc(i)
        blk.vector(p1_half)
        dve_half = s_dve.n

        def p1_exp(s, sb=sb, dve_half=dve_half):
            s.wait_ge(s_dve.h, dve_half)
            i = s.activation(out=stats[:, sb + 5:sb + 6], in_=stats[:, sb + 6:sb + 7],
                             func=AF.Exp)
            s_act.inc(i)
        blk.scalar(p1_exp)
        act_rstd = s_act.n

        def p1_rstd(v, sb=sb, act_rstd=act_rstd):
            v.wait_ge(s_act.h, act_rstd)
            i = v.scalar_tensor_tensor(out=stats[:, sb + 7:sb + 8],
                                       in0=stats[:, sb:sb + 1], scalar=-1.0,
                                       in1=stats[:, sb + 5:sb + 6],
                                       op0=ALU.mult, op1=ALU.mult)
            s_dve.inc(i)
        blk.vector(p1_rstd)
        dve_rstd = s_dve.n

        def p1_norm(s, tt=tt, par=par, sb=sb, dve_rstd=dve_rstd, prev=prev_evac[par]):
            s.wait_ge(s_dve.h, dve_rstd)
            if prev:
                s.wait_ge(s_pe.h, prev)
            xt = xslot(tt)
            i = s.activation(out=(xnt if par == 0 else xnt2)[:], in_=xt,
                             func=AF.Identity,
                             bias=stats[:, sb + 7:sb + 8], scale=stats[:, sb + 5:sb + 6])
            s_act.inc(i)
        blk.scalar(p1_norm)
        norm_done[tt] = s_act.n
        return s_act.n

    ffn_pe = {}

    def p1_Cx(tt, act_norm, psb_gate):
        par = tt % 2
        xsrc = xnt if par == 0 else xnt2

        def p1_tr(t, act_norm=act_norm, xsrc=xsrc, psb_gate=psb_gate):
            t.wait_ge(s_act.h, act_norm)
            t.wait_ge(s_dve.h, psb_gate)
            for c in range(DCH):
                i = t.transpose(out=psB[:, c * 128:(c + 1) * 128],
                                in_=xsrc[:, c * 128:(c + 1) * 128], identity=idb[:])
            s_pe.inc(i)
        blk.tensor(p1_tr)
        pe_tr = s_pe.n
        prev_evac[par] = pe_tr

        def p1_ev(v, tt=tt, pe_tr=pe_tr):
            v.wait_ge(s_pe.h, pe_tr)
            dst = big[:].rearrange("p (c s) -> p c s", c=DCH)[:, :, tt * 128:(tt + 1) * 128]
            i = v.tensor_copy(out=dst, in_=psB[:].rearrange("p (c s) -> p c s", c=DCH))
            s_dve.inc(i)
        blk.vector(p1_ev)
        ev_n = s_dve.n

        for q in range(3):
            emit_ktile(64 + tt * 3 + q)

        for half in range(2):
            b = 2 * par + half

            def f_mm(t, tt=tt, half=half, b=b, ev_n=ev_n,
                     gate=ffn_bank_free[2 * par + half]):
                t.wait_ge(s_act.h, wfT_ready)
                t.wait_ge(s_dve.h, ev_n)
                if gate:
                    t.wait_ge(s_dve.h, gate)
                for c in range(DCH):
                    i = t.matmul(out=bank(b),
                                 lhsT=xnT(c)[:, tt * 128:(tt + 1) * 128],
                                 rhs=WfT[:, c * 1024 + half * 512:c * 1024 + half * 512 + 512],
                                 start=(c == 0), stop=(c == DCH - 1))
                s_pe.inc(i)
            blk.tensor(f_mm)
        ffn_pe[tt] = s_pe.n

    def p1_Cy(tt):
        par = tt % 2

        def f_base(v, tt=tt, par=par, pe_f=ffn_pe[tt], st_gate=bst_n[par]):
            v.wait_ge(s_pe.h, pe_f)
            if st_gate:
                v.wait_ge(opar[par].h, st_gate)
            xs = xslot(tt)
            for half in range(2):
                b = 2 * par + half
                i = v.scalar_tensor_tensor(
                    out=bsb[:, par * D + half * 512:par * D + half * 512 + 512],
                    in0=bank(b), scalar=srep[:, 0:1],
                    in1=xs[:, half * 512:half * 512 + 512],
                    op0=ALU.mult, op1=ALU.add)
            s_dve.inc(i)
        blk.vector(f_base)
        base_done[tt] = s_dve.n
        ffn_bank_free[2 * par] = s_dve.n
        ffn_bank_free[2 * par + 1] = s_dve.n

        def f_st(sy, tt=tt, par=par, dn=base_done[tt]):
            sy.wait_ge(s_dve.h, dn)
            i = sy.dma_start(out=BASE[tt * 128:(tt + 1) * 128, :],
                             in_=bsb[:, par * D:(par + 1) * D])
            opar[par].incd(i)
        blk.sync(f_st)
        bst_n[par] = opar[par].n

    stageA = {}
    stageB = {}
    for it in range(TT + 3):
        psb_gate = s_dve.n
        if it < TT:
            stageA[it] = p1_A(it)
        if 1 <= it <= TT:
            stageB[it - 1] = p1_B(it - 1, stageA[it - 1])
        if 2 <= it <= TT + 1:
            p1_Cx(it - 2, stageB[it - 2], psb_gate)
        if 3 <= it <= TT + 2:
            p1_Cy(it - 3)
    xnT_ready = s_dve.n
    ffn_pe_done = s_pe.n

    def p2_seed(v):
        v.wait_ge(s_dve.h, cast_done)
        v.tensor_copy(out=Apow[:, 0:64], in_=idb[0:64, 0:64])
        v.tensor_copy(out=A64p[:, 0:64], in_=idb[0:64, 0:64])
        i = v.tensor_copy(out=Apow[:, 64:128], in_=Abf[:])
        s_dve.inc(i)
    blk.vector(p2_seed)
    def emit_dbl(pow_ap, rounds):
        for m, lo, hi in rounds:
            w = hi - lo

            def r_tr(t, pow_ap=pow_ap, m=m):
                t.wait_ge(s_dve.h, s_dve.n)
                i = t.transpose(out=psB[0:64, 0:64],
                                in_=pow_ap[:, 64 * m:64 * (m + 1)],
                                identity=idb[0:64, 0:64])
                s_pe.inc(i)
            blk.tensor(r_tr)
            pe1 = s_pe.n

            def r_ev(v, pe1=pe1):
                v.wait_ge(s_pe.h, pe1)
                i = v.tensor_copy(out=Tm[:], in_=psB[0:64, 0:64])
                s_dve.inc(i)
            blk.vector(r_ev)
            dv1 = s_dve.n

            def r_mm(t, pow_ap=pow_ap, m=m, lo=lo, w=w, dv1=dv1):
                t.wait_ge(s_dve.h, dv1)
                i = t.matmul(out=psD[0:64, 0:64 * w], lhsT=Tm[:],
                             rhs=pow_ap[:, 64 * lo:64 * (lo + w)],
                             start=True, stop=True)
                s_pe.inc(i)
            blk.tensor(r_mm)
            pe2 = s_pe.n

            def r_ev2(v, pow_ap=pow_ap, m=m, lo=lo, w=w, pe2=pe2):
                v.wait_ge(s_pe.h, pe2)
                i = v.tensor_copy(out=pow_ap[:, 64 * (m + lo):64 * (m + lo + w)],
                                  in_=psD[0:64, 0:64 * w])
                s_dve.inc(i)
            blk.vector(r_ev2)

    ROUNDS = [(1, 1, 2), (2, 1, 3), (4, 1, 5), (8, 1, 8), (8, 8, 9),
              (16, 1, 9), (16, 9, 16), (16, 16, 17),
              (32, 1, 9), (32, 9, 17), (32, 17, 25), (32, 25, 32)]
    emit_dbl(Apow, ROUNDS)

    def a64_mm(t):
        t.wait_ge(s_dve.h, s_dve.n)
        i = t.matmul(out=psD[0:64, 0:64], lhsT=Tm[:], rhs=Apow[:, 64 * 32:64 * 33],
                     start=True, stop=True)
        s_pe.inc(i)
    blk.tensor(a64_mm)
    pe1 = s_pe.n

    def a64_ev(v, pe1=pe1):
        v.wait_ge(s_pe.h, pe1)
        i = v.tensor_copy(out=A64p[:, 64:128], in_=psD[0:64, 0:64])
        s_dve.inc(i)
    blk.vector(a64_ev)
    emit_dbl(A64p, ROUNDS)
    _maybe_tap("Apow", Apow[:, :])

    for st in range(8):
        bk = st % 2

        def u_mm(t, st=st, bk=bk):
            t.wait_ge(s_dve.h, xnT_ready)
            if st >= 2:
                t.wait_ge(s_act.h, s_act.n)
            for c in range(DCH):
                i = t.matmul(out=bank(bk)[0:64, :],
                             lhsT=Bmb[:, c * SD:(c + 1) * SD],
                             rhs=xnT(c)[:, st * 512:(st + 1) * 512],
                             start=(c == 0), stop=(c == DCH - 1))
            s_pe.inc(i)
        blk.tensor(u_mm)
        pe1 = s_pe.n

        def u_ev(s, st=st, bk=bk, pe1=pe1):
            s.wait_ge(s_pe.h, pe1)
            i = s.activation(out=uT[:, st * 512:(st + 1) * 512],
                             in_=bank(bk)[0:64, :], func=AF.Copy)
            s_act.inc(i)
        blk.scalar(u_ev)
    uT_ready = s_act.n
    _maybe_tap("uT", uT[:, :])

    def in_mm(t):
        t.wait_ge(s_act.h, uT_ready)
        t.wait_ge(s_dve.h, s_dve.n)
        uv = uT[:].rearrange("p (c i) -> p i c", i=64)
        for i_ in range(64):
            i = t.matmul(out=psD[0:64, 0:64],
                         lhsT=Apow[:, 64 * (63 - i_):64 * (64 - i_)],
                         rhs=uv[:, i_, :], start=(i_ == 0), stop=(i_ == 63))
        s_pe.inc(i)
    blk.tensor(in_mm)
    pe1 = s_pe.n

    def in_ev(v, pe1=pe1):
        v.wait_ge(s_pe.h, pe1)
        i = v.tensor_copy(out=inT[:, 0:64], in_=psD[0:64, 0:64])
        s_dve.inc(i)
    blk.vector(in_ev)
    dv1 = s_dve.n

    def out_mm(t, dv1=dv1):
        t.wait_ge(s_dve.h, dv1)
        for c in range(64):
            i = t.matmul(out=psD[0:64, 64:65],
                         lhsT=A64p[:, 64 * (63 - c):64 * (64 - c)],
                         rhs=inT[:, c:c + 1], start=(c == 0), stop=(c == 63))
        s_pe.inc(i)
    blk.tensor(out_mm)
    pe_outer = s_pe.n

    def h_ev(v, pe_outer=pe_outer):
        v.wait_ge(s_pe.h, pe_outer)
        i = v.tensor_copy(out=hvec[:], in_=psD[0:64, 64:65])
        s_dve.inc(i)
    blk.vector(h_ev)
    _maybe_tap("hvec", hvec[:, :])

    ld_wi = ctl_dma(wstg[:, 0:DCH * SD].rearrange("p (c e) -> p c e", c=DCH),
                    WI[:, :].rearrange("(c p) e -> p c e", c=DCH),
                    waits=[(s_dve, xnT_ready)])
    for c in range(DCH):
        def wi_tr(t, c=c, ld_wi=ld_wi):
            t.wait_ge(s_ctl.h, ld_wi)
            t.wait_ge(s_dve.h, s_dve.n)
            i = t.transpose(out=psD[0:64, 128:256],
                            in_=wstg[:, c * SD:(c + 1) * SD],
                            identity=idf[:])
            s_pe.inc(i)
        blk.tensor(wi_tr)
        pe1 = s_pe.n

        def wi_ev(v, c=c, pe1=pe1):
            v.wait_ge(s_pe.h, pe1)
            i = v.tensor_copy(out=WiT[:, c * 128:(c + 1) * 128],
                              in_=psD[0:64, 128:256])
            s_dve.inc(i)
        blk.vector(wi_ev)
    dv1 = s_dve.n

    def hp_mm(t, dv1=dv1):
        t.wait_ge(s_dve.h, dv1)
        for c in range(DCH):
            i = t.matmul(out=psD[:, 256 + c:257 + c],
                         lhsT=WiT[:, c * 128:(c + 1) * 128], rhs=hvec[:],
                         start=True, stop=True)
        s_pe.inc(i)
    blk.tensor(hp_mm)
    pe1 = s_pe.n

    def hp_ev(v, pe1=pe1):
        v.wait_ge(s_pe.h, pe1)
        v.tensor_copy(out=hp[:], in_=psD[:, 256:256 + DCH])
        i = v.tensor_copy(out=hpb[:], in_=psD[:, 256:256 + DCH])
        s_dve.inc(i)
    blk.vector(hp_ev)
    hp_ready = s_dve.n

    lst_free = 0
    for st in range(8):
        def L_mm(t, st=st):
            t.wait_ge(s_dve.h, s_dve.n if st else hp_ready)
            for c in range(DCH):
                i = t.matmul(out=psD[0:1, :], lhsT=hpb[:, c:c + 1],
                             rhs=xnT(c)[:, st * 512:(st + 1) * 512],
                             start=(c == 0), stop=(c == DCH - 1))
            s_pe.inc(i)
        blk.tensor(L_mm)
        pe1 = s_pe.n

        def L_ev(v, pe1=pe1, lst_free=lst_free):
            v.wait_ge(s_pe.h, pe1)
            v.wait_ge(s_ld.h, lst_free)
            i = v.tensor_copy(out=Lst[:], in_=psD[0:1, :])
            s_dve.inc(i)
        blk.vector(L_ev)
        dv1 = s_dve.n

        def L_st(sy, st=st, dv1=dv1):
            sy.wait_ge(s_dve.h, dv1)
            i = sy.dma_start(out=LDR[st * 512:(st + 1) * 512], in_=Lst[0:1, :])
            s_ld.incd(i)
        blk.sync(L_st)
        lst_free = s_ld.n

    ld_L2d = ctl_dma(L2d[:], LDR[:].rearrange("(p q) -> p q", p=128),
                     waits=[(s_ld, s_ld.n)])

    def eL_a(s, ld_L2d=ld_L2d):
        s.wait_ge(s_ctl.h, ld_L2d)
        i = s.activation(out=eL2d[:], in_=L2d[:], func=AF.Exp)
        s_act.inc(i)
    blk.scalar(eL_a)
    act1 = s_act.n

    def eL_sum(v, act1=act1):
        v.wait_ge(s_act.h, act1)
        i = v.tensor_scalar(out=d32[:], in0=eL2d[:], scalar1=1.0,
                            scalar2=None, op0=ALU.mult, op1=ALU.add,
                            accum_out=stats[:, 0:1])
        s_dve.inc(i)
    blk.vector(eL_sum)
    dv1 = s_dve.n

    def els_mm(t, dv1=dv1):
        t.wait_ge(s_dve.h, dv1)
        i = t.matmul(out=psD[0:1, 300:301], lhsT=stats[:, 0:1],
                     rhs=ones128[:], start=True, stop=True)
        s_pe.inc(i)
    blk.tensor(els_mm)
    pe1 = s_pe.n

    def els_r(v, pe1=pe1):
        v.wait_ge(s_pe.h, pe1)
        i = v.reciprocal(out=scal[0:1, 0:1], in_=psD[0:1, 300:301])
        s_dve.inc(i)
    blk.vector(els_r)
    dv1 = s_dve.n

    def els_rep(t, dv1=dv1):
        t.wait_ge(s_dve.h, dv1)
        i = t.matmul(out=psD[:, 301:302], lhsT=onesf[:], rhs=scal[0:1, 0:1],
                     start=True, stop=True)
        s_pe.inc(i)
    blk.tensor(els_rep)
    pe1 = s_pe.n

    def els_rep_ev(v, pe1=pe1):
        v.wait_ge(s_pe.h, pe1)
        i = v.tensor_copy(out=rsum[:], in_=psD[:, 301:302])
        s_dve.inc(i)
    blk.vector(els_rep_ev)
    rsum_ready = s_dve.n

    ld_wr = ctl_dma(wstg[0:64, 512:512 + D], WR[:, :])

    def wr_cast(s, ld_wr=ld_wr):
        s.wait_ge(s_ctl.h, ld_wr)
        s.wait_ge(s_pe.h, s_pe.n)
        i = s.activation(out=e_tp[:, 0:1024], in_=wstg[0:64, 512:512 + D],
                         func=AF.Copy)
        s_act.inc(i)
    blk.scalar(wr_cast)
    act1 = s_act.n

    for c in range(DCH):
        def wr_tr(t, c=c, act1=act1):
            t.wait_ge(s_act.h, act1)
            t.wait_ge(s_dve.h, s_dve.n)
            i = t.transpose(out=psB[:, 0:64], in_=e_tp[:, c * 128:(c + 1) * 128],
                            identity=idb[0:64, 0:64])
            s_pe.inc(i)
        blk.tensor(wr_tr)
        pe1 = s_pe.n

        def wr_ev(v, c=c, pe1=pe1):
            v.wait_ge(s_pe.h, pe1)
            i = v.tensor_copy(out=WrT[:, c * NC:(c + 1) * NC], in_=psB[:, 0:64])
            s_dve.inc(i)
        blk.vector(wr_ev)
    wrT_ready = s_dve.n

    for st in range(8):
        bk = 2 + st % 2

        def tp_mm(t, st=st, bk=bk):
            t.wait_ge(s_dve.h, wrT_ready)
            if st >= 2:
                t.wait_ge(s_act.h, s_act.n)
            for c in range(DCH):
                i = t.matmul(out=bank(bk)[0:64, :], lhsT=WrT[:, c * NC:(c + 1) * NC],
                             rhs=xnT(c)[:, st * 512:(st + 1) * 512],
                             start=(c == 0), stop=(c == DCH - 1))
            s_pe.inc(i)
        blk.tensor(tp_mm)
        pe1 = s_pe.n

        def tp_ev(s, st=st, bk=bk, pe1=pe1):
            s.wait_ge(s_pe.h, pe1)
            s.wait_ge(s_dve.h, s_dve.n)
            i = s.activation(out=e_tp[:, st * 512:(st + 1) * 512],
                             in_=bank(bk)[0:64, :], func=AF.Exp)
            s_act.inc(i)
        blk.scalar(tp_ev)
    etp_ready = s_act.n

    for st in range(8):
        def z_mm(t, st=st):
            t.wait_ge(s_act.h, etp_ready)
            t.wait_ge(s_dve.h, s_dve.n)
            i = t.matmul(out=psD[0:1, :], lhsT=ones64[:],
                         rhs=e_tp[:, st * 512:(st + 1) * 512], start=True, stop=True)
            s_pe.inc(i)
        blk.tensor(z_mm)
        pe1 = s_pe.n

        def z_ev(v, pe1=pe1, lst_free=lst_free):
            v.wait_ge(s_pe.h, pe1)
            v.wait_ge(s_ld.h, lst_free)
            i = v.tensor_copy(out=Lst[:], in_=psD[0:1, :])
            s_dve.inc(i)
        blk.vector(z_ev)
        dv1 = s_dve.n

        def z_st(sy, st=st, dv1=dv1):
            sy.wait_ge(s_dve.h, dv1)
            i = sy.dma_start(out=ZDR[st * 512:(st + 1) * 512], in_=Lst[0:1, :])
            s_ld.incd(i)
        blk.sync(z_st)
        lst_free = s_ld.n

    ld_z = ctl_dma(Z2d[:], ZDR[:].rearrange("(p q) -> p q", p=128),
                   waits=[(s_ld, s_ld.n)])

    def z_fin(v, ld_z=ld_z):
        v.wait_ge(s_ctl.h, ld_z)
        v.wait_ge(s_dve.h, rsum_ready)
        i = v.reciprocal(out=rz2d[:], in_=Z2d[:])
        n1 = s_dve.inc(i)
        v.wait_ge(s_dve.h, n1)
        i = v.tensor_scalar(out=im2d[:],
                            in0=eL2d[:], scalar1=rsum[:, 0:1], scalar2=None,
                            op0=ALU.mult)
        n2 = s_dve.inc(i)
        v.wait_ge(s_dve.h, n2)
        i = v.tensor_tensor(out=im2d[:], in0=im2d[:], in1=rz2d[:], op=ALU.mult)
        s_dve.inc(i)
    blk.vector(z_fin)
    dv1 = s_dve.n

    ctl_dma(IMD[:].rearrange("(p q) -> p q", p=128), im2d[:],
            waits=[(s_dve, dv1)])
    ld_imr = ctl_dma(imr[:],
                     IMD[:].rearrange("(a s) -> a s", a=1).to_broadcast([64, S]),
                     waits=[(s_pe, pe_outer)])
    _maybe_tap("im2d", im2d[:, :])

    def nw_acc(v, ld_imr=ld_imr):
        v.wait_ge(s_ctl.h, ld_imr)
        v.wait_ge(s_act.h, etp_ready)
        i = v.scalar_tensor_tensor(out=e_tp[:], in0=e_tp[:], scalar=1.0, in1=imr[:],
                                   op0=ALU.mult, op1=ALU.mult, accum_out=nwacc[:])
        n1 = s_dve.inc(i)
        v.wait_ge(s_dve.h, n1)
        i = v.tensor_copy(out=inT[:, 64:65], in_=nwacc[:])
        s_dve.inc(i)
    blk.vector(nw_acc)
    dv1 = s_dve.n

    def nws_mm(t, dv1=dv1):
        t.wait_ge(s_dve.h, dv1)
        i = t.matmul(out=psD[0:1, 302:303], lhsT=inT[:, 64:65], rhs=ones64[:],
                     start=True, stop=True)
        s_pe.inc(i)
    blk.tensor(nws_mm)
    pe1 = s_pe.n

    def nws_fin(v, pe1=pe1):
        v.wait_ge(s_pe.h, pe1)
        i = v.tensor_scalar(out=scal[0:1, 2:3], in0=psD[0:1, 302:303],
                            scalar1=EPS_NW, scalar2=None, op0=ALU.add)
        n1 = s_dve.inc(i)
        v.wait_ge(s_dve.h, n1)
        i = v.reciprocal(out=scal[0:1, 3:4], in_=scal[0:1, 2:3])
        s_dve.inc(i)
    blk.vector(nws_fin)
    dv1 = s_dve.n

    def nwr_rep(t, dv1=dv1):
        t.wait_ge(s_dve.h, dv1)
        i = t.matmul(out=psD[0:64, 303:304], lhsT=onesf[:, 0:64],
                     rhs=scal[0:1, 3:4], start=True, stop=True)
        s_pe.inc(i)
    blk.tensor(nwr_rep)
    pe1 = s_pe.n

    def nw_fin(v, pe1=pe1):
        v.wait_ge(s_pe.h, pe1)
        i = v.scalar_tensor_tensor(out=nw64[:], in0=nwacc[:], scalar=1.0,
                                   in1=psD[0:64, 303:304], op0=ALU.mult, op1=ALU.mult)
        s_dve.inc(i)
    blk.vector(nw_fin)
    dv1 = s_dve.n
    _maybe_tap("nw64", nw64[:, :])

    def nwt_tr(t, dv1=dv1):
        t.wait_ge(s_dve.h, dv1)
        i = t.transpose(out=psD[0:1, 304:368], in_=nw64[:], identity=idf[0:64, 0:64])
        s_pe.inc(i)
    blk.tensor(nwt_tr)
    pe1 = s_pe.n

    def nwrow_ev(v, pe1=pe1, lst_free=lst_free):
        v.wait_ge(s_pe.h, pe1)
        v.wait_ge(s_ld.h, lst_free)
        i = v.tensor_copy(out=Lst[0:1, 0:64], in_=psD[0:1, 304:368])
        s_dve.inc(i)
    blk.vector(nwrow_ev)
    dv1 = s_dve.n

    def nwrep_mm(t, dv1=dv1):
        t.wait_ge(s_dve.h, dv1)
        i = t.matmul(out=psD[:, 368:432], lhsT=onesf[:], rhs=Lst[0:1, 0:64],
                     start=True, stop=True)
        s_pe.inc(i)
    blk.tensor(nwrep_mm)
    pe1 = s_pe.n

    def nwrep_ev(v, pe1=pe1):
        v.wait_ge(s_pe.h, pe1)
        i = v.tensor_copy(out=nwrep[:], in_=psD[:, 368:432])
        s_dve.inc(i)
    blk.vector(nwrep_ev)
    nwrep_ready = s_dve.n


    ffn_pe_done = s_pe.n

    kdef = list(range(0, 64)) + list(range(160, 256))
    kdef_waits = [(s_pe, s_pe.n), (s_dve, s_dve.n)]
    k4_tr = {-2: s_pe.n, -1: s_pe.n}
    k4_i = [0]

    def emit_k4(g4):
        kt0 = kdef[4 * g4]
        kb = 4 + (kt0 // 4) % 2
        kp = k4_i[0] % 2
        k4_i[0] += 1

        def k4_ld(sy, kt0=kt0, kp=kp, slot_gate=k4_tr.get(k4_i[0] - 3, 0)):
            if slot_gate:
                sy.wait_ge(s_pe.h, slot_gate)
            i = sy.dma_start(
                out=kst4[:, kp * 512:(kp + 1) * 512].rearrange("p (g r) -> p g r", g=4),
                in_=KD[kt0 * 128:(kt0 + 4) * 128, :].rearrange("(g p) r -> p g r", g=4))
            s_k2[kp].incd(i)
        blk.sync(k4_ld)
        ld_k = s_k2[kp].n

        def k4_tr_f(t, kt0=kt0, kb=kb, kp=kp, ld_k=ld_k,
                    gate=kev_bank[4 + (kt0 // 4) % 2]):
            t.wait_ge(s_k2[kp].h, ld_k)
            if gate:
                t.wait_ge(s_act.h, gate)
            for g in range(4):
                i = t.transpose(out=bank(kb)[:, g * 128:(g + 1) * 128],
                                in_=kst4[:, kp * 512 + g * 128:kp * 512 + (g + 1) * 128],
                                identity=idf[:])
            s_pe.inc(i)
        blk.tensor(k4_tr_f)
        k4_tr[k4_i[0] - 1] = s_pe.n
        pe_k = s_pe.n

        def k4_ev(s, kt0=kt0, kb=kb, pe_k=pe_k, ev_waits=tuple(kdef_waits)):
            s.wait_ge(s_pe.h, pe_k)
            for sem, cnt in ev_waits:
                s.wait_ge(sem.h, cnt)
            i = s.activation(out=KT[:, kt0 * 128:kt0 * 128 + 512],
                             in_=bank(kb), func=AF.Copy)
            s_act.inc(i)
        blk.scalar(k4_ev)
        kev_bank[kb] = s_act.n

    sc_cons = {}
    for n in range(NC):
        slot = n % 2

        def sc_ld(sy, n=n, slot=slot):
            if n < 2:
                sy.wait_ge(s_dve.h, nwrep_ready)
                sy.wait_ge(s_ctl.h, ld_imr)
                sy.wait_ge(s_act.h, etp_ready)
            else:
                sy.wait_ge(s_dve.h, sc_cons[n - 2])
            i = sy.dma_start(out=cbuf[:, slot * 1024:(slot + 1) * 1024]
                             .rearrange("p (q r) -> p q r", q=DCH),
                             in_=CP[n, :, :].rearrange("(q p) r -> p q r", q=DCH))
            s_c2[slot].incd(i)
        blk.sync(sc_ld)
        ld_c = s_c2[slot].n

        def sc_acc(v, n=n, slot=slot, ld_c=ld_c):
            v.wait_ge(s_c2[slot].h, ld_c)
            if n:
                v.wait_ge(s_dve.h, s_dve.n)
            for c in range(DCH):
                src = cbuf[:, slot * 1024 + c * 128:slot * 1024 + (c + 1) * 128]
                dst = SCm[:, c * 128:(c + 1) * 128]
                if n == 0:
                    i = v.tensor_scalar(out=dst, in0=src, scalar1=nwrep[:, n:n + 1],
                                        scalar2=None, op0=ALU.mult)
                else:
                    i = v.scalar_tensor_tensor(out=dst, in0=src,
                                               scalar=nwrep[:, n:n + 1], in1=dst,
                                               op0=ALU.mult, op1=ALU.add)
            s_dve.inc(i)
        blk.vector(sc_acc)
        sc_cons[n] = s_dve.n
        if n < 40:
            emit_k4(n)
    kt_ready = s_act.n

    def sc_cast(v):
        v.wait_ge(s_dve.h, s_dve.n)
        i = v.tensor_copy(out=SCb[:], in_=SCm[:])
        s_dve.inc(i)
    blk.vector(sc_cast)
    scb_ready = s_dve.n
    _maybe_tap("SCm", SCm[:, :])

    qt_bank_free = {0: 0, 1: 0}
    for st in range(8):
        bk = st % 2

        def q_mm(t, st=st, bk=bk, gate=qt_bank_free[st % 2]):
            t.wait_ge(s_dve.h, scb_ready)
            if gate:
                t.wait_ge(s_act.h, gate)
            for c in range(DCH):
                i = t.matmul(out=bank(bk), lhsT=SCb[:, c * 128:(c + 1) * 128],
                             rhs=xnT(c)[:, st * 512:(st + 1) * 512],
                             start=(c == 0), stop=(c == DCH - 1))
            s_pe.inc(i)
        blk.tensor(q_mm)
        pe1 = s_pe.n

        def q_ev(s, st=st, bk=bk, pe1=pe1):
            s.wait_ge(s_pe.h, pe1)
            i = s.activation(out=QT[:, st * 512:(st + 1) * 512], in_=bank(bk),
                             func=AF.Copy, scale=ckon[:, 1:2])
            s_act.inc(i)
        blk.scalar(q_ev)
        qt_bank_free[bk] = s_act.n
    qt_ready = s_act.n
    _maybe_tap("QT", QT[:, :])

    sc_bank_free = [0] * 6
    half_read = {}
    wsum_done = {}
    w8_ready = {}
    gidx_ready = {}
    nmax_n = {}
    exp_n = {}
    pool_ready = {}
    base_all = (s_oe.n, s_oo.n)

    def emit_wsum(tt):
        def d_ws(v, tt=tt):
            par = tt % 2
            acc = xb[:, par * D:(par + 1) * D]
            v.wait_ge(s_act.h, exp_n[tt])
            i = v.tensor_scalar(out=d32[:, 0:8], in0=e8[:], scalar1=1.0,
                                scalar2=None, op0=ALU.mult, op1=ALU.add,
                                accum_out=sm1[:, 0:1])
            n1 = s_dve.inc(i)
            v.wait_ge(s_dve.h, n1)
            i = v.reciprocal(out=sm1[:, 1:2], in_=sm1[:, 0:1])
            n2 = s_dve.inc(i)
            v.wait_ge(s_dve.h, n2)
            i = v.tensor_scalar(out=w8[:], in0=e8[:], scalar1=sm1[:, 1:2],
                                scalar2=None, op0=ALU.mult)
            w8_ready[tt] = s_dve.inc(i)
            v.wait_ge(s_dve.h, w8_ready[tt])
            v.wait_ge(s_pl.h, pool_ready[tt])
            v.wait_ge(opar[tt % 2].h, opar[tt % 2].n)
            for k in range(KK):
                if k:
                    v.wait_ge(s_dve.h, s_dve.n)
                i = v.scalar_tensor_tensor(
                    out=acc, in0=Vg[:].rearrange("p (k d) -> p k d", k=KK)[:, k, :],
                    scalar=w8[:, k:k + 1],
                    in1=(bsb[:, par * D:(par + 1) * D] if k == 0 else acc),
                    op0=ALU.mult, op1=ALU.add)
                s_dve.inc(i)
        blk.vector(d_ws)
        wsum_done[tt] = s_dve.n

    def emit_out(tt):
        def p_out(g, tt=tt, wn=wsum_done[tt]):
            g.wait_ge(s_dve.h, wn)
            par = tt % 2
            i = g.dma_start(out=OUT[tt * 128:(tt + 1) * 128, :],
                            in_=xb[:, par * D:(par + 1) * D])
            opar[par].incd(i)
        blk.gpsimd(p_out)

    NP4 = N_P4[0]
    for tt in range(NP4):

        for j in range(64):
            b = j % 6
            q = j // 16
            qs = q % 2

            def s_mm(t, tt=tt, j=j, b=b, gate=sc_bank_free[j % 6]):
                if tt == 0 and j == 0:
                    t.wait_ge(s_act.h, qt_ready)
                    t.wait_ge(s_act.h, kt_ready)
                if gate:
                    t.wait_ge(s_act.h, gate)
                i = t.matmul(out=bank(b), lhsT=QT[:, tt * 128:(tt + 1) * 128],
                             rhs=KT[:, j * 512:(j + 1) * 512], start=True, stop=True)
                s_pe.inc(i)
            blk.tensor(s_mm)
            mm_n = s_pe.n

            war = (half_read.get((tt, q - 2), 0) if q >= 2
                   else half_read.get((tt - 1, q + 2), 0))

            def s_ev(s, tt=tt, j=j, b=b, q=q, qs=qs, mm_n=mm_n, war=war):
                s.wait_ge(s_pe.h, mm_n)
                if war:
                    s.wait_ge(s_dve.h, war)
                i = s.activation(out=scq[:, qs * 8192 + (j % 16) * 512:
                                         qs * 8192 + (j % 16) * 512 + 512],
                                 in_=bank(b), func=AF.Copy)
                s_act.inc(i)
            blk.scalar(s_ev)
            sc_bank_free[b] = s_act.n

            if j == 15 and tt > 0:
                def a_exp(s, p=tt - 1):
                    s.wait_ge(s_dve.h, nmax_n[p])
                    i = s.activation(out=e8[:], in_=gvf[:], func=AF.Exp,
                                     bias=nmax[:, 0:1])
                    s_act.inc(i)
                blk.scalar(a_exp)
                exp_n[tt - 1] = s_act.n

            if j % 16 == 15:
                qe = s_act.n

                def d_q(v, tt=tt, q=q, qs=qs, qe=qe):
                    v.wait_ge(s_act.h, qe)
                    i = v.max(out=cman[:, 8 * q:8 * q + 8],
                              in_=scq[:, qs * 8192:(qs + 1) * 8192])
                    n1 = s_dve.inc(i)
                    v.wait_ge(s_dve.h, n1)
                    i = v.max_index(out=cmi[:, 8 * q:8 * q + 8],
                                    in_max=cman[:, 8 * q:8 * q + 8],
                                    in_values=scq[:, qs * 8192:(qs + 1) * 8192])
                    half_read[(tt, q)] = s_dve.inc(i)
                blk.vector(d_q)
                if q == 1 and tt > 0:
                    emit_wsum(tt - 1)

        def d_top(v, tt=tt):
            v.wait_ge(s_dve.h, s_dve.n)
            i = v.max(out=gv[:], in_=cman[:])
            n4 = s_dve.inc(i)
            v.wait_ge(s_dve.h, n4)
            i = v.max_index(out=pos[:], in_max=gv[:], in_values=cman[:])
            n5 = s_dve.inc(i)
            v.wait_ge(s_dve.h, n5)
            v.tensor_scalar(out=posf[:], in0=pos[:], scalar1=0.0, scalar2=None,
                            op0=ALU.add)
            i = v.tensor_scalar(out=cgi[:], in0=cmi[:], scalar1=0.0, scalar2=None,
                                op0=ALU.add)
            n6 = s_dve.inc(i)
            v.wait_ge(s_dve.h, n6)
            i = v.tensor_tensor(out=cgi[:], in0=cgi[:], in1=hb32[:], op=ALU.add)
            n7 = s_dve.inc(i)
            v.wait_ge(s_dve.h, n7)
            for k in range(KK):
                i = v.scalar_tensor_tensor(out=d32[:], in0=io32[:],
                                           scalar=posf[:, k:k + 1], in1=cgi[:],
                                           op0=ALU.is_equal, op1=ALU.mult,
                                           accum_out=gixf[:, k:k + 1])
            n8 = s_dve.inc(i)
            v.wait_ge(s_dve.h, n8)
            i = v.tensor_scalar(out=gixu[:], in0=gixf[:], scalar1=0.0,
                                scalar2=None, op0=ALU.add)
            gidx_ready[tt] = s_dve.inc(i)
            i = v.tensor_scalar(out=gvf[:], in0=gv[:], scalar1=0.0, scalar2=None,
                                op0=ALU.add)
            n9 = s_dve.inc(i)
            v.wait_ge(s_dve.h, n9)
            i = v.tensor_scalar(out=nmax[:], in0=gvf[:, 0:1], scalar1=-1.0,
                                scalar2=None, op0=ALU.mult)
            nmax_n[tt] = s_dve.inc(i)
        blk.vector(d_top)

        for k in range(KK):
            def p_g(g, tt=tt, k=k):
                if k == 0:
                    g.wait_ge(s_dve.h, gidx_ready[tt])
                    if tt > 0:
                        g.wait_ge(s_dve.h, wsum_done[tt - 1])
                i = g.indirect_dma_start(
                    out=Vg[:].rearrange("p (k d) -> p k d", k=KK)[:, k, :],
                    out_offset=None, in_=VD[:, :],
                    in_offset=IndirectOffsetOnAxis(ap=gixu[:, k:k + 1], axis=0))
                s_pl.incd(i)
            blk.gpsimd(p_g)

        def p_b(g, tt=tt, base_all=base_all):
            g.wait_ge(s_oe.h, base_all[0])
            g.wait_ge(s_oo.h, base_all[1])
            if tt >= 2:
                g.wait_ge(s_dve.h, wsum_done[tt - 2])
            i = g.dma_start(out=bsb[:, (tt % 2) * D:(tt % 2 + 1) * D],
                            in_=BASE[tt * 128:(tt + 1) * 128, :])
            s_pl.incd(i)
        blk.gpsimd(p_b)
        pool_ready[tt] = s_pl.n

        if tt > 0:
            emit_out(tt - 1)

    def a_exp_last(s):
        s.wait_ge(s_dve.h, nmax_n[NP4 - 1])
        i = s.activation(out=e8[:], in_=gvf[:], func=AF.Exp, bias=nmax[:, 0:1])
        s_act.inc(i)
    blk.scalar(a_exp_last)
    exp_n[NP4 - 1] = s_act.n
    emit_wsum(NP4 - 1)
    emit_out(NP4 - 1)
    _maybe_tap("gv", gv[:, :])
    _maybe_tap("cmi", cmi[:, :])
    _maybe_tap("cman", cman[:, :])
    _maybe_tap("pos", pos[:, :])
    _maybe_tap("gixf", gixf[:, :])
    _maybe_tap("gixu", gixu[:, :])
    _maybe_tap("e8", e8[:, :])
    _maybe_tap("w8", w8[:, :])
    _maybe_tap("bsb", bsb[:, :])
    _maybe_tap("Vg", Vg[:, :])
    _maybe_tap("cgi", cgi[:, :])
    _maybe_tap("KTt", KT[:, :])
    _maybe_tap("scq", scq[:, :])

    def fin(g):
        g.wait_ge(s_oe.h, s_oe.n)
        g.wait_ge(s_oo.h, s_oo.n)
        g.wait_ge(s_out.h, s_out.n)
        g.wait_ge(s_pl.h, s_pl.n)
        g.wait_ge(s_ctl.h, s_ctl.n)
    blk.gpsimd(fin)

    ctx.close()
    return nc


_TAP_SHAPES = {
    "gv": ([128, 8], BF16),
    "cmi": ([128, 32], U16),
    "cman": ([128, 32], BF16),
    "pos": ([128, 8], U16),
    "gixf": ([128, 8], F32),
    "gixu": ([128, 8], U32),
    "e8": ([128, 8], F32),
    "w8": ([128, 8], F32),
    "bsb": ([128, 2048], F32),
    "Vg": ([128, 8 * 1024], F32),
    "cgi": ([128, 32], F32),
    "KTt": ([128, NK], BF16),
    "scq": ([128, 2 * 8192], BF16),
    "Apow": ([64, 64 * SD], BF16),
    "uT": ([64, S], BF16),
    "hvec": ([64, 1], F32),
    "im2d": ([128, 32], BF16),
    "nw64": ([64, 1], F32),
    "SCm": ([128, DCH * 128], F32),
    "QT": ([128, S], BF16),
}


def kernel(L0_attn_x, L0_attn_A, L0_attn_Bm, L0_attn_W_imp, L0_attn_W_router,
           L0_attn_compress, L0_attn_K, L0_attn_V, L0_ffn_weight):
    x = np.ascontiguousarray(np.asarray(L0_attn_x, dtype=np.float32))
    com = {
        "A": np.ascontiguousarray(np.asarray(L0_attn_A, dtype=np.float32)),
        "Bm": np.ascontiguousarray(np.asarray(L0_attn_Bm, dtype=np.float32)),
        "Wimp": np.ascontiguousarray(np.asarray(L0_attn_W_imp, dtype=np.float32)),
        "Wrout": np.ascontiguousarray(np.asarray(L0_attn_W_router, dtype=np.float32)),
        "comp": np.ascontiguousarray(np.asarray(L0_attn_compress, dtype=np.float32)),
        "K": np.ascontiguousarray(np.asarray(L0_attn_K, dtype=np.float32)),
        "V": np.ascontiguousarray(np.asarray(L0_attn_V, dtype=np.float32)),
        "Wf": np.ascontiguousarray(np.asarray(L0_ffn_weight, dtype=np.float32)),
    }
    nc = build_program()
    in_maps = [dict(com, x=np.ascontiguousarray(x[b])) for b in range(B)]
    res = run_bass_kernel_spmd(nc, in_maps, core_ids=list(range(B)))
    out = np.stack([np.asarray(res.results[b]["out"]) for b in range(B)], axis=0)
    return out


# version 09 (final)
# speedup vs baseline: 1.1602x; 1.0036x over previous
import contextlib

import numpy as np

import concourse.bass as bass
import concourse.mybir as mybir
from concourse.bass import IndirectOffsetOnAxis
from concourse.bass_utils import run_bass_kernel_spmd

F32 = mybir.dt.float32
BF16 = mybir.dt.bfloat16
U16 = mybir.dt.uint16
U32 = mybir.dt.uint32
ALU = mybir.AluOpType
AF = mybir.ActivationFunctionType

B, S, D, R, NC, NK, SD, KK = 8, 4096, 1024, 128, 64, 32768, 64, 8
TT = S // 128
DCH = D // 128
EPS_LN = 1e-5
EPS_W = 1e-5
EPS_NW = 1e-8
RSQRT_R = float(1.0 / np.sqrt(np.float64(R)))

_DEBUG_TAPS = []
N_P4 = [TT]


class Sem:
    def __init__(self, ctx, nc, name):
        self.h = ctx.enter_context(nc.semaphore(name))
        self.n = 0

    def inc(self, inst, k=1):
        inst.then_inc(self.h, k)
        self.n += k
        return self.n

    def incd(self, inst):
        return self.inc(inst, 16)

    def wait(self, eng, n=None):
        eng.wait_ge(self.h, self.n if n is None else n)


def build_program():
    nc = bass.Bass("TRN2", target_bir_lowering=False, debug=False)

    X = nc.dram_tensor("x", [S, D], F32, kind="ExternalInput").ap()
    A_ = nc.dram_tensor("A", [SD, SD], F32, kind="ExternalInput").ap()
    BM = nc.dram_tensor("Bm", [D, SD], F32, kind="ExternalInput").ap()
    WI = nc.dram_tensor("Wimp", [D, SD], F32, kind="ExternalInput").ap()
    WR = nc.dram_tensor("Wrout", [NC, D], F32, kind="ExternalInput").ap()
    CP = nc.dram_tensor("comp", [NC, D, R], F32, kind="ExternalInput").ap()
    KD = nc.dram_tensor("K", [NK, R], F32, kind="ExternalInput").ap()
    VD = nc.dram_tensor("V", [NK, D], F32, kind="ExternalInput").ap()
    WF = nc.dram_tensor("Wf", [D, D], F32, kind="ExternalInput").ap()
    OUT = nc.dram_tensor("out", [S, D], F32, kind="ExternalOutput").ap()
    BASE = nc.dram_tensor("base", [S, D], F32, kind="Internal").ap()
    LDR = nc.dram_tensor("Ldr", [S], F32, kind="Internal").ap()
    ZDR = nc.dram_tensor("Zdr", [S], F32, kind="Internal").ap()
    IMD = nc.dram_tensor("imd", [S], BF16, kind="Internal").ap()

    taps = {}
    for name in _DEBUG_TAPS:
        shape, dt = _TAP_SHAPES[name]
        taps[name] = nc.dram_tensor("tap_" + name, shape, dt, kind="ExternalOutput").ap()

    ctx = contextlib.ExitStack()

    at = nc.alloc_sbuf_tensor_at
    B0 = 16512
    KT = at("KT", [128, NK], BF16, offset=B0).ap()
    wstg = at("wstg", [128, 4096], F32, offset=B0).ap()
    negt = at("negt", [128, 8192], BF16, offset=B0 + 16384).ap()
    posx = at("posx", [128, 4096], BF16, offset=B0 + 32768).ap()
    uT = at("uT", [64, S], BF16, offset=B0 + 40960).ap()
    WfT = at("WfT", [128, DCH * D], BF16, offset=B0 + 49152).ap()
    big = at("big", [128, NK], BF16, offset=82048).ap()
    scq = at("scq", [128, 2 * 8192], BF16, offset=82048).ap()
    Vg = at("Vg", [128, KK * D], F32, offset=82048 + 32768).ap()
    QT = at("QT", [128, S], BF16, offset=147584).ap()
    kst = at("kst", [128, 256], F32, offset=147584).ap()
    kst4 = at("kst4", [128, 2 * 512], F32, offset=147584).ap()
    xnt2 = at("xnt2", [128, D], BF16, offset=147584 + 4096).ap()
    junk = at("junk", [128, D], F32, offset=B0).ap()
    imr = at("imr", [64, S], BF16, offset=B0).ap()
    UB = 155776
    bsb = at("bsb", [128, 2 * D], F32, offset=UB).ap()
    Apow = at("Apow", [64, 64 * SD], BF16, offset=UB + 8192).ap()
    xb2 = at("xb2", [128, 2048], F32, offset=UB + 8192).ap()
    xb3 = at("xb3", [128, 2048], F32, offset=B0).ap()
    e_tp = at("e_tp", [64, S], BF16, offset=UB + 8192).ap()
    cbuf = at("cbuf", [128, 2048], F32, offset=UB + 8192).ap()
    A64p = at("A64p", [64, 64 * SD], BF16, offset=UB + 16384).ap()
    Lst = at("Lst", [1, 512], F32, offset=UB + 16384).ap()
    WiT = at("WiT", [64, D], F32, offset=UB + 18432).ap()
    SCm = at("SCm", [128, DCH * 128], F32, offset=UB + 16384).ap()
    SCb = at("SCb", [128, DCH * 128], BF16, offset=UB + 20480).ap()
    o = 180352

    def tail(name, shape, dt):
        nonlocal o
        t = at(name, shape, dt, offset=o)
        per = int(np.prod(shape[1:])) * mybir.dt.size(dt)
        o += (per + 31) // 32 * 32
        return t.ap()

    xb = tail("xb", [128, 2 * D], F32)
    xnt = tail("xnt", [128, D], BF16)
    Abf = tail("Abf", [64, SD], BF16)
    Bmb = tail("Bmb", [128, DCH * SD], BF16)
    WrT = tail("WrT", [128, DCH * NC], BF16)
    idb = tail("idb", [128, 128], BF16)
    idf = tail("idf", [128, 128], F32)
    onesf = tail("onesf", [1, 128], F32)
    ones128 = tail("ones128", [128, 1], F32)
    ones64 = tail("ones64", [64, 1], BF16)
    stats = tail("stats", [128, 16], F32)
    scal = tail("scal", [1, 8], F32)
    srep = tail("srep", [128, 2], F32)
    hp = tail("hp", [128, DCH], F32)
    hpb = tail("hpb", [128, DCH], BF16)
    Tm = tail("Tm", [64, SD], BF16)
    inT = tail("inT", [64, SD + 2], BF16)
    hvec = tail("hvec", [64, 1], F32)
    nwacc = tail("nwacc", [64, 1], F32)
    nw64 = tail("nw64", [64, 1], F32)
    nwrep = tail("nwrep", [128, NC], F32)
    L2d = tail("L2d", [128, 32], F32)
    eL2d = tail("eL2d", [128, 32], F32)
    Z2d = tail("Z2d", [128, 32], F32)
    rz2d = tail("rz2d", [128, 32], F32)
    im2d = tail("im2d", [128, 32], BF16)
    rsum = tail("rsum", [128, 1], F32)
    io32 = tail("io32", [128, 32], F32)
    hb32 = tail("hb32", [128, 32], F32)
    cman = tail("cman", [128, 32], BF16)
    cmi = tail("cmi", [128, 32], U16)
    cgi = tail("cgi", [128, 32], F32)
    gv = tail("gv", [128, 8], BF16)
    gvf = tail("gvf", [128, 8], F32)
    pos = tail("pos", [128, 8], U16)
    posf = tail("posf", [128, 8], F32)
    gixf = tail("gixf", [128, 8], F32)
    gixu = tail("gixu", [128, 8], U32)
    e8 = tail("e8", [128, 8], F32)
    w8 = tail("w8", [128, 8], F32)
    sm1 = tail("sm1", [128, 2], F32)
    nmax = tail("nmax", [128, 1], F32)
    d32 = tail("d32", [128, 32], F32)
    ckon = tail("ckon", [128, 2], F32)
    assert o <= 196576 - 32, f"sbuf tail overflow: {o}"

    psA = ctx.enter_context(nc.psum_tensor("psA", [128, 3072], F32))
    psB = ctx.enter_context(nc.psum_tensor("psB", [128, 1024], BF16))
    psD = ctx.enter_context(nc.psum_tensor("psD", [128, 512], F32))
    bank = lambda b: psA[:, 512 * b:512 * (b + 1)]

    s_pe = Sem(ctx, nc, "s_pe")
    s_dve = Sem(ctx, nc, "s_dve")
    s_act = Sem(ctx, nc, "s_act")
    s_ld = Sem(ctx, nc, "s_ld")
    s_pl = Sem(ctx, nc, "s_pl")
    s_out = Sem(ctx, nc, "s_out")
    s_x2 = [Sem(ctx, nc, "s_x0"), Sem(ctx, nc, "s_x1")]
    s_k2 = [Sem(ctx, nc, "s_k0"), Sem(ctx, nc, "s_k1")]
    s_c2 = [Sem(ctx, nc, "s_c0"), Sem(ctx, nc, "s_c1")]
    s_oe = Sem(ctx, nc, "s_oe")
    s_oo = Sem(ctx, nc, "s_oo")
    opar = [s_oe, s_oo]
    s_ctl = Sem(ctx, nc, "s_ctl")

    blk = ctx.enter_context(nc.Block())

    def ctl_dma(out_ap, in_ap, waits=()):
        pre = s_ctl.n

        def _c(g, out_ap=out_ap, in_ap=in_ap, waits=tuple(waits), pre=pre):
            g.wait_ge(s_ctl.h, pre)
            for sem, cnt in waits:
                g.wait_ge(sem.h, cnt)
            i = g.dma_start(out=out_ap, in_=in_ap)
            s_ctl.incd(i)
        blk.gpsimd(_c)
        return s_ctl.n

    def xnT(c):
        return big[:, c * S:(c + 1) * S]

    def _maybe_tap(name, src_ap):
        if name not in taps:
            return
        dn = s_dve.n
        an = s_act.n
        pn = s_pe.n

        def t_dma(g, name=name, src_ap=src_ap, dn=dn, an=an, pn=pn):
            g.wait_ge(s_dve.h, dn)
            g.wait_ge(s_act.h, an)
            g.wait_ge(s_pe.h, pn)
            i = g.dma_start(out=taps[name][:], in_=src_ap)
            s_pl.incd(i)
        blk.gpsimd(t_dma)

    def c_pool(g):
        g.memset(idb[:], 0.0)
        g.affine_select(out=idb[:], in_=idb[:], compare_op=ALU.not_equal,
                        fill=1.0, base=0, pattern=[[-1, 128]], channel_multiplier=1)
        g.memset(idf[:], 0.0)
        g.affine_select(out=idf[:], in_=idf[:], compare_op=ALU.not_equal,
                        fill=1.0, base=0, pattern=[[-1, 128]], channel_multiplier=1)
        g.iota(out=io32[:], pattern=[[1, 32]], base=0, channel_multiplier=0,
               allow_small_or_imprecise_dtypes=True)
        g.iota(out=hb32[:], pattern=[[8192, 4], [0, 8]], base=0,
               channel_multiplier=0, allow_small_or_imprecise_dtypes=True)
        g.memset(ones64[:], 1.0)
        g.memset(onesf[:], 1.0)
        g.memset(ones128[:], 1.0)
        g.memset(ckon[:, 0:1], EPS_LN)
        i = g.memset(ckon[:, 1:2], RSQRT_R)
        s_pl.inc(i)
    blk.gpsimd(c_pool)
    pl_consts = s_pl.n

    ctl_dma(wstg[0:64, 0:64], A_[:, :])
    ld_small = ctl_dma(wstg[:, 64:64 + DCH * SD].rearrange("p (c e) -> p c e", c=DCH),
                       BM[:, :].rearrange("(c p) e -> p c e", c=DCH))

    def c_cast(v, ld_small=ld_small):
        v.wait_ge(s_ctl.h, ld_small)
        v.wait_ge(s_pl.h, pl_consts)
        v.tensor_copy(out=Abf[:], in_=wstg[0:64, 0:64])
        i = v.tensor_copy(out=Bmb[:], in_=wstg[:, 64:64 + DCH * SD])
        s_dve.inc(i)
    blk.vector(c_cast)
    cast_done = s_dve.n

    def act_preload(s):
        from concourse.hw_specs import get_activation_tables
        need = {AF.Ln, AF.Exp, AF.Identity, AF.Copy}
        for idx, funcs in enumerate(get_activation_tables(nc.m.arch).values()):
            if need.issubset(funcs):
                s.add_instruction(mybir.InstLoadActFuncSet(
                    name=nc.get_next_instruction_name(),
                    act_func_set_id=idx, ins=[], outs=[]))
                return
        raise ValueError("no table set covers all activation funcs")
    blk.scalar(act_preload)

    habs = []
    for h in range(2):
        def wfa_ld(sy, h=h):
            if h == 0:
                sy.wait_ge(s_dve.h, cast_done)
            else:
                sy.wait_ge(s_dve.h, s_dve.n)
            i = sy.dma_start(out=wstg[:].rearrange("p (c i) -> p c i", c=4),
                             in_=WF[:, :].rearrange("(c p) i -> p c i", c=DCH)
                             [:, 4 * h:4 * (h + 1), :])
            s_ld.incd(i)
        blk.sync(wfa_ld)
        ldh = s_ld.n

        def wf_abs(v, h=h, ldh=ldh):
            v.wait_ge(s_ld.h, ldh)
            i = v.reduce_sum(out=stats[:, h:h + 1], in_=wstg[:],
                             axis=mybir.AxisListType.X, apply_absolute_value=True)
            s_dve.inc(i)
        blk.vector(wf_abs)
        habs.append(s_dve.n)

    def wf_sc(v):
        v.wait_ge(s_dve.h, habs[1])
        i = v.tensor_tensor(out=stats[:, 2:3], in0=stats[:, 0:1],
                            in1=stats[:, 1:2], op=ALU.add)
        s_dve.inc(i)
    blk.vector(wf_sc)
    dv1 = s_dve.n

    def wfs_mm(t, dv1=dv1):
        t.wait_ge(s_dve.h, dv1)
        i = t.matmul(out=psD[0:1, 432:433], lhsT=stats[:, 2:3],
                     rhs=ones128[:], start=True, stop=True)
        s_pe.inc(i)
    blk.tensor(wfs_mm)
    pe1 = s_pe.n

    def wfs_fin(v, pe1=pe1):
        v.wait_ge(s_pe.h, pe1)
        i = v.tensor_scalar(out=scal[0:1, 5:6], in0=psD[0:1, 432:433],
                            scalar1=1.0 / (D * D), scalar2=EPS_W,
                            op0=ALU.mult, op1=ALU.max)
        n1 = s_dve.inc(i)
        v.wait_ge(s_dve.h, n1)
        i = v.reciprocal(out=scal[0:1, 6:7], in_=scal[0:1, 5:6])
        s_dve.inc(i)
    blk.vector(wfs_fin)
    dv1 = s_dve.n

    def srep_mm(t, dv1=dv1):
        t.wait_ge(s_dve.h, dv1)
        t.matmul(out=psD[:, 433:434], lhsT=onesf[:], rhs=scal[0:1, 5:6],
                 start=True, stop=True)
        i = t.matmul(out=psD[:, 434:435], lhsT=onesf[:], rhs=scal[0:1, 6:7],
                     start=True, stop=True)
        s_pe.inc(i)
    blk.tensor(srep_mm)
    pe1 = s_pe.n

    def srep_ev(v, pe1=pe1):
        v.wait_ge(s_pe.h, pe1)
        v.tensor_scalar(out=srep[:, 0:1], in0=psD[:, 433:434], scalar1=-1.0,
                        scalar2=None, op0=ALU.mult)
        i = v.tensor_copy(out=srep[:, 1:2], in_=psD[:, 434:435])
        s_dve.inc(i)
    blk.vector(srep_ev)
    srep_ready = s_dve.n

    for h in range(2):
        def wft_ld(sy, h=h):
            sy.wait_ge(s_dve.h, s_dve.n)
            i = sy.dma_start(out=wstg[:].rearrange("p (c i) -> p c i", c=4),
                             in_=WF[:, :].rearrange("(c p) i -> p c i", c=DCH)
                             [:, 4 * h:4 * (h + 1), :])
            s_ld.incd(i)
        blk.sync(wft_ld)
        ldh = s_ld.n

        def wf_tern(v, h=h, ldh=ldh):
            v.wait_ge(s_ld.h, ldh)
            v.wait_ge(s_dve.h, srep_ready)
            i = v.tensor_scalar(out=wstg[:], in0=wstg[:], scalar1=srep[:, 1:2],
                                scalar2=None, op0=ALU.mult)
            n1 = s_dve.inc(i)
            v.wait_ge(s_dve.h, n1)
            i = v.tensor_scalar(out=posx[:], in0=wstg[:], scalar1=0.5,
                                scalar2=None, op0=ALU.is_gt)
            n2 = s_dve.inc(i)
            v.wait_ge(s_dve.h, n2)
            i = v.scalar_tensor_tensor(out=negt[:, h * 4096:(h + 1) * 4096],
                                       in0=wstg[:], scalar=-0.5, in1=posx[:],
                                       op0=ALU.is_lt, op1=ALU.subtract)
            s_dve.inc(i)
        blk.vector(wf_tern)
    tern_ready = s_dve.n

    for ci in range(DCH):
        def wft_tr(t, ci=ci):
            t.wait_ge(s_dve.h, tern_ready)
            if ci > 0:
                t.wait_ge(s_act.h, s_act.n)
            for co in range(DCH):
                i = t.transpose(out=psB[:, co * 128:(co + 1) * 128],
                                in_=negt[:, co * 1024 + ci * 128:co * 1024 + ci * 128 + 128],
                                identity=idb[:])
            s_pe.inc(i)
        blk.tensor(wft_tr)
        pe1 = s_pe.n

        def wft_ev(s, ci=ci, pe1=pe1):
            s.wait_ge(s_pe.h, pe1)
            i = s.activation(out=WfT[:, ci * 1024:(ci + 1) * 1024], in_=psB[:],
                             func=AF.Copy)
            s_act.inc(i)
        blk.scalar(wft_ev)
    wfT_ready = s_act.n


    kev_bank = {4: 0, 5: 0}
    ffn_bank_free = {0: 0, 1: 0, 2: 0, 3: 0}
    base_done = {}
    ktr_pe = {}
    kt_emitted = []

    def emit_ktile(kt, ev_waits=()):
        kb = 4 + (kt // 4) % 2
        slot = kt % 4
        kpar = len(kt_emitted) % 2
        kt_emitted.append(kt)

        def k_ld(g, kt=kt, kpar=kpar,
                 slot_gate=(ktr_pe[kt_emitted[-3]] if len(kt_emitted) >= 3 else 0)):
            g.wait_ge(s_dve.h, tern_ready)
            if slot_gate:
                g.wait_ge(s_pe.h, slot_gate)
            i = g.dma_start(out=kst[:, kpar * 128:(kpar + 1) * 128],
                            in_=KD[kt * 128:(kt + 1) * 128, :])
            s_k2[kpar].incd(i)
        blk.gpsimd(k_ld)
        ld_k = s_k2[kpar].n

        def k_tr(t, kb=kb, slot=slot, kpar=kpar, ld_k=ld_k,
                 gate=kev_bank[4 + (kt // 4) % 2]):
            t.wait_ge(s_k2[kpar].h, ld_k)
            if slot == 0 and gate:
                t.wait_ge(s_act.h, gate)
            i = t.transpose(out=bank(kb)[:, slot * 128:(slot + 1) * 128],
                            in_=kst[:, kpar * 128:(kpar + 1) * 128],
                            identity=idf[:])
            s_pe.inc(i)
        blk.tensor(k_tr)
        ktr_pe[kt] = s_pe.n

        if slot == 3:
            pe_k = s_pe.n

            def k_ev(s, kt=kt, kb=kb, pe_k=pe_k, ev_waits=tuple(ev_waits)):
                s.wait_ge(s_pe.h, pe_k)
                for sem, cnt in ev_waits:
                    s.wait_ge(sem.h, cnt)
                base_n = (kt - 3) * 128
                i = s.activation(out=KT[:, base_n:base_n + 512],
                                 in_=bank(kb), func=AF.Copy)
                s_act.inc(i)
            blk.scalar(k_ev)
            kev_bank[kb] = s_act.n

    prev_evac = [0, 0]
    norm_done = {}
    base_done = {}
    bst_n = {0: 0, 1: 0}
    xsem = [s_x2[0], s_x2[1], s_c2[0], s_c2[1], s_ld, s_ctl]

    def xslot(tt):
        sl = tt % 6
        buf = (xb, xb2, xb3)[sl // 2]
        return buf[:, (sl % 2) * D:(sl % 2 + 1) * D]

    def p1_A(tt):
        par = tt % 2
        sb = 8 * par

        def p1_ld(g, tt=tt):
            if tt == 0:
                g.wait_ge(s_pl.h, pl_consts)
            if tt in (4, 5):
                g.wait_ge(s_dve.h, tern_ready)
            if tt >= 6:
                g.wait_ge(s_dve.h, base_done[tt - 6])
            i = g.dma_start(out=xslot(tt), in_=X[tt * 128:(tt + 1) * 128, :])
            xsem[tt % 6].incd(i)
        blk.gpsimd(p1_ld)
        ld_n = xsem[tt % 6].n

        def p1_stats(v, tt=tt, par=par, sb=sb, ld_n=ld_n):
            v.wait_ge(xsem[tt % 6].h, ld_n)
            if tt >= 2:
                v.wait_ge(s_act.h, norm_done[tt - 2])
            xt = xslot(tt)
            bns = d32[:, 16 * par:16 * par + 12]
            v.bn_stats(out=bns[:, 0:6], in_=xt[:, 0:512])
            i = v.bn_stats(out=bns[:, 6:12], in_=xt[:, 512:1024])
            n1 = s_dve.inc(i)
            v.wait_ge(s_dve.h, n1)
            i = v.bn_aggr(out=stats[:, sb:sb + 2], in_=bns[:])
            n2 = s_dve.inc(i)
            v.wait_ge(s_dve.h, n2)
            i = v.tensor_scalar(out=stats[:, sb + 3:sb + 4], in0=stats[:, sb:sb + 1],
                                scalar1=-1.0, scalar2=None, op0=ALU.mult)
            s_dve.inc(i)
        blk.vector(p1_stats)
        return s_dve.n

    def p1_B(tt, dve_var):
        par = tt % 2
        sb = 8 * par

        def p1_ln(s, sb=sb, dve_var=dve_var):
            s.wait_ge(s_dve.h, dve_var)
            i = s.activation(out=stats[:, sb + 5:sb + 6], in_=stats[:, sb + 1:sb + 2],
                             func=AF.Ln, bias=ckon[:, 0:1])
            n1 = s_act.inc(i)
            s.wait_ge(s_act.h, n1)
            i = s.activation(out=stats[:, sb + 5:sb + 6], in_=stats[:, sb + 5:sb + 6],
                             func=AF.Exp, scale=-0.5)
            n2 = s_act.inc(i)
            s.wait_ge(s_act.h, n2)
            i = s.activation(out=stats[:, sb + 7:sb + 8], in_=stats[:, sb + 3:sb + 4],
                             func=AF.Identity, scale=stats[:, sb + 5:sb + 6])
            s_act.inc(i)
        blk.scalar(p1_ln)
        act_rstd = s_act.n

        def p1_norm(s, tt=tt, par=par, sb=sb, act_rstd=act_rstd, prev=prev_evac[par]):
            s.wait_ge(s_act.h, act_rstd)
            if prev:
                s.wait_ge(s_pe.h, prev)
            xt = xslot(tt)
            i = s.activation(out=(xnt if par == 0 else xnt2)[:], in_=xt,
                             func=AF.Identity,
                             bias=stats[:, sb + 7:sb + 8], scale=stats[:, sb + 5:sb + 6])
            s_act.inc(i)
        blk.scalar(p1_norm)
        norm_done[tt] = s_act.n
        return s_act.n

    ffn_pe = {}

    def p1_Cx(tt, act_norm, psb_gate):
        par = tt % 2
        xsrc = xnt if par == 0 else xnt2

        def p1_tr(t, act_norm=act_norm, xsrc=xsrc, psb_gate=psb_gate):
            t.wait_ge(s_act.h, act_norm)
            t.wait_ge(s_dve.h, psb_gate)
            for c in range(DCH):
                i = t.transpose(out=psB[:, c * 128:(c + 1) * 128],
                                in_=xsrc[:, c * 128:(c + 1) * 128], identity=idb[:])
            s_pe.inc(i)
        blk.tensor(p1_tr)
        pe_tr = s_pe.n
        prev_evac[par] = pe_tr

        def p1_ev(v, tt=tt, pe_tr=pe_tr):
            v.wait_ge(s_pe.h, pe_tr)
            dst = big[:].rearrange("p (c s) -> p c s", c=DCH)[:, :, tt * 128:(tt + 1) * 128]
            i = v.tensor_copy(out=dst, in_=psB[:].rearrange("p (c s) -> p c s", c=DCH))
            s_dve.inc(i)
        blk.vector(p1_ev)
        ev_n = s_dve.n

        for q in range(3):
            emit_ktile(64 + tt * 3 + q)

        for half in range(2):
            b = 2 * par + half

            def f_mm(t, tt=tt, half=half, b=b, ev_n=ev_n,
                     gate=ffn_bank_free[2 * par + half]):
                t.wait_ge(s_act.h, wfT_ready)
                t.wait_ge(s_dve.h, ev_n)
                if gate:
                    t.wait_ge(s_dve.h, gate)
                for c in range(DCH):
                    i = t.matmul(out=bank(b),
                                 lhsT=xnT(c)[:, tt * 128:(tt + 1) * 128],
                                 rhs=WfT[:, c * 1024 + half * 512:c * 1024 + half * 512 + 512],
                                 start=(c == 0), stop=(c == DCH - 1))
                s_pe.inc(i)
            blk.tensor(f_mm)
        ffn_pe[tt] = s_pe.n

    def p1_Cy(tt):
        par = tt % 2

        def f_base(v, tt=tt, par=par, pe_f=ffn_pe[tt], st_gate=bst_n[par]):
            v.wait_ge(s_pe.h, pe_f)
            if st_gate:
                v.wait_ge(opar[par].h, st_gate)
            xs = xslot(tt)
            for half in range(2):
                b = 2 * par + half
                i = v.scalar_tensor_tensor(
                    out=bsb[:, par * D + half * 512:par * D + half * 512 + 512],
                    in0=bank(b), scalar=srep[:, 0:1],
                    in1=xs[:, half * 512:half * 512 + 512],
                    op0=ALU.mult, op1=ALU.add)
            s_dve.inc(i)
        blk.vector(f_base)
        base_done[tt] = s_dve.n
        ffn_bank_free[2 * par] = s_dve.n
        ffn_bank_free[2 * par + 1] = s_dve.n

        def f_st(sy, tt=tt, par=par, dn=base_done[tt]):
            sy.wait_ge(s_dve.h, dn)
            i = sy.dma_start(out=BASE[tt * 128:(tt + 1) * 128, :],
                             in_=bsb[:, par * D:(par + 1) * D])
            opar[par].incd(i)
        blk.sync(f_st)
        bst_n[par] = opar[par].n

    stageA = {}
    stageB = {}
    for it in range(TT + 3):
        psb_gate = s_dve.n
        if it < TT:
            stageA[it] = p1_A(it)
        if 1 <= it <= TT:
            stageB[it - 1] = p1_B(it - 1, stageA[it - 1])
        if 2 <= it <= TT + 1:
            p1_Cx(it - 2, stageB[it - 2], psb_gate)
        if 3 <= it <= TT + 2:
            p1_Cy(it - 3)
    xnT_ready = s_dve.n
    ffn_pe_done = s_pe.n

    def p2_seed(v):
        v.wait_ge(s_dve.h, cast_done)
        v.tensor_copy(out=Apow[:, 0:64], in_=idb[0:64, 0:64])
        v.tensor_copy(out=A64p[:, 0:64], in_=idb[0:64, 0:64])
        i = v.tensor_copy(out=Apow[:, 64:128], in_=Abf[:])
        s_dve.inc(i)
    blk.vector(p2_seed)
    def emit_dbl(pow_ap, rounds):
        for m, lo, hi in rounds:
            w = hi - lo

            def r_tr(t, pow_ap=pow_ap, m=m):
                t.wait_ge(s_dve.h, s_dve.n)
                i = t.transpose(out=psB[0:64, 0:64],
                                in_=pow_ap[:, 64 * m:64 * (m + 1)],
                                identity=idb[0:64, 0:64])
                s_pe.inc(i)
            blk.tensor(r_tr)
            pe1 = s_pe.n

            def r_ev(v, pe1=pe1):
                v.wait_ge(s_pe.h, pe1)
                i = v.tensor_copy(out=Tm[:], in_=psB[0:64, 0:64])
                s_dve.inc(i)
            blk.vector(r_ev)
            dv1 = s_dve.n

            def r_mm(t, pow_ap=pow_ap, m=m, lo=lo, w=w, dv1=dv1):
                t.wait_ge(s_dve.h, dv1)
                i = t.matmul(out=psD[0:64, 0:64 * w], lhsT=Tm[:],
                             rhs=pow_ap[:, 64 * lo:64 * (lo + w)],
                             start=True, stop=True)
                s_pe.inc(i)
            blk.tensor(r_mm)
            pe2 = s_pe.n

            def r_ev2(v, pow_ap=pow_ap, m=m, lo=lo, w=w, pe2=pe2):
                v.wait_ge(s_pe.h, pe2)
                i = v.tensor_copy(out=pow_ap[:, 64 * (m + lo):64 * (m + lo + w)],
                                  in_=psD[0:64, 0:64 * w])
                s_dve.inc(i)
            blk.vector(r_ev2)

    ROUNDS = [(1, 1, 2), (2, 1, 3), (4, 1, 5), (8, 1, 8), (8, 8, 9),
              (16, 1, 9), (16, 9, 16), (16, 16, 17),
              (32, 1, 9), (32, 9, 17), (32, 17, 25), (32, 25, 32)]
    emit_dbl(Apow, ROUNDS)

    def a64_mm(t):
        t.wait_ge(s_dve.h, s_dve.n)
        i = t.matmul(out=psD[0:64, 0:64], lhsT=Tm[:], rhs=Apow[:, 64 * 32:64 * 33],
                     start=True, stop=True)
        s_pe.inc(i)
    blk.tensor(a64_mm)
    pe1 = s_pe.n

    def a64_ev(v, pe1=pe1):
        v.wait_ge(s_pe.h, pe1)
        i = v.tensor_copy(out=A64p[:, 64:128], in_=psD[0:64, 0:64])
        s_dve.inc(i)
    blk.vector(a64_ev)
    emit_dbl(A64p, ROUNDS)
    _maybe_tap("Apow", Apow[:, :])

    for st in range(8):
        bk = st % 2

        def u_mm(t, st=st, bk=bk):
            t.wait_ge(s_dve.h, xnT_ready)
            if st >= 2:
                t.wait_ge(s_act.h, s_act.n)
            for c in range(DCH):
                i = t.matmul(out=bank(bk)[0:64, :],
                             lhsT=Bmb[:, c * SD:(c + 1) * SD],
                             rhs=xnT(c)[:, st * 512:(st + 1) * 512],
                             start=(c == 0), stop=(c == DCH - 1))
            s_pe.inc(i)
        blk.tensor(u_mm)
        pe1 = s_pe.n

        def u_ev(s, st=st, bk=bk, pe1=pe1):
            s.wait_ge(s_pe.h, pe1)
            i = s.activation(out=uT[:, st * 512:(st + 1) * 512],
                             in_=bank(bk)[0:64, :], func=AF.Copy)
            s_act.inc(i)
        blk.scalar(u_ev)
    uT_ready = s_act.n
    _maybe_tap("uT", uT[:, :])

    def in_mm(t):
        t.wait_ge(s_act.h, uT_ready)
        t.wait_ge(s_dve.h, s_dve.n)
        uv = uT[:].rearrange("p (c i) -> p i c", i=64)
        for i_ in range(64):
            i = t.matmul(out=psD[0:64, 0:64],
                         lhsT=Apow[:, 64 * (63 - i_):64 * (64 - i_)],
                         rhs=uv[:, i_, :], start=(i_ == 0), stop=(i_ == 63))
        s_pe.inc(i)
    blk.tensor(in_mm)
    pe1 = s_pe.n

    def in_ev(v, pe1=pe1):
        v.wait_ge(s_pe.h, pe1)
        i = v.tensor_copy(out=inT[:, 0:64], in_=psD[0:64, 0:64])
        s_dve.inc(i)
    blk.vector(in_ev)
    dv1 = s_dve.n

    def out_mm(t, dv1=dv1):
        t.wait_ge(s_dve.h, dv1)
        for c in range(64):
            i = t.matmul(out=psD[0:64, 64:65],
                         lhsT=A64p[:, 64 * (63 - c):64 * (64 - c)],
                         rhs=inT[:, c:c + 1], start=(c == 0), stop=(c == 63))
        s_pe.inc(i)
    blk.tensor(out_mm)
    pe_outer = s_pe.n

    def h_ev(v, pe_outer=pe_outer):
        v.wait_ge(s_pe.h, pe_outer)
        i = v.tensor_copy(out=hvec[:], in_=psD[0:64, 64:65])
        s_dve.inc(i)
    blk.vector(h_ev)
    _maybe_tap("hvec", hvec[:, :])

    ld_wi = ctl_dma(wstg[:, 0:DCH * SD].rearrange("p (c e) -> p c e", c=DCH),
                    WI[:, :].rearrange("(c p) e -> p c e", c=DCH),
                    waits=[(s_dve, xnT_ready)])
    for c in range(DCH):
        def wi_tr(t, c=c, ld_wi=ld_wi):
            t.wait_ge(s_ctl.h, ld_wi)
            t.wait_ge(s_dve.h, s_dve.n)
            i = t.transpose(out=psD[0:64, 128:256],
                            in_=wstg[:, c * SD:(c + 1) * SD],
                            identity=idf[:])
            s_pe.inc(i)
        blk.tensor(wi_tr)
        pe1 = s_pe.n

        def wi_ev(v, c=c, pe1=pe1):
            v.wait_ge(s_pe.h, pe1)
            i = v.tensor_copy(out=WiT[:, c * 128:(c + 1) * 128],
                              in_=psD[0:64, 128:256])
            s_dve.inc(i)
        blk.vector(wi_ev)
    dv1 = s_dve.n

    def hp_mm(t, dv1=dv1):
        t.wait_ge(s_dve.h, dv1)
        for c in range(DCH):
            i = t.matmul(out=psD[:, 256 + c:257 + c],
                         lhsT=WiT[:, c * 128:(c + 1) * 128], rhs=hvec[:],
                         start=True, stop=True)
        s_pe.inc(i)
    blk.tensor(hp_mm)
    pe1 = s_pe.n

    def hp_ev(v, pe1=pe1):
        v.wait_ge(s_pe.h, pe1)
        v.tensor_copy(out=hp[:], in_=psD[:, 256:256 + DCH])
        i = v.tensor_copy(out=hpb[:], in_=psD[:, 256:256 + DCH])
        s_dve.inc(i)
    blk.vector(hp_ev)
    hp_ready = s_dve.n

    lst_free = 0
    for st in range(8):
        def L_mm(t, st=st):
            t.wait_ge(s_dve.h, s_dve.n if st else hp_ready)
            for c in range(DCH):
                i = t.matmul(out=psD[0:1, :], lhsT=hpb[:, c:c + 1],
                             rhs=xnT(c)[:, st * 512:(st + 1) * 512],
                             start=(c == 0), stop=(c == DCH - 1))
            s_pe.inc(i)
        blk.tensor(L_mm)
        pe1 = s_pe.n

        def L_ev(v, pe1=pe1, lst_free=lst_free):
            v.wait_ge(s_pe.h, pe1)
            v.wait_ge(s_ld.h, lst_free)
            i = v.tensor_copy(out=Lst[:], in_=psD[0:1, :])
            s_dve.inc(i)
        blk.vector(L_ev)
        dv1 = s_dve.n

        def L_st(sy, st=st, dv1=dv1):
            sy.wait_ge(s_dve.h, dv1)
            i = sy.dma_start(out=LDR[st * 512:(st + 1) * 512], in_=Lst[0:1, :])
            s_ld.incd(i)
        blk.sync(L_st)
        lst_free = s_ld.n

    ld_L2d = ctl_dma(L2d[:], LDR[:].rearrange("(p q) -> p q", p=128),
                     waits=[(s_ld, s_ld.n)])

    def eL_a(s, ld_L2d=ld_L2d):
        s.wait_ge(s_ctl.h, ld_L2d)
        i = s.activation(out=eL2d[:], in_=L2d[:], func=AF.Exp)
        s_act.inc(i)
    blk.scalar(eL_a)
    act1 = s_act.n

    def eL_sum(v, act1=act1):
        v.wait_ge(s_act.h, act1)
        i = v.tensor_scalar(out=d32[:], in0=eL2d[:], scalar1=1.0,
                            scalar2=None, op0=ALU.mult, op1=ALU.add,
                            accum_out=stats[:, 0:1])
        s_dve.inc(i)
    blk.vector(eL_sum)
    dv1 = s_dve.n

    def els_mm(t, dv1=dv1):
        t.wait_ge(s_dve.h, dv1)
        i = t.matmul(out=psD[0:1, 300:301], lhsT=stats[:, 0:1],
                     rhs=ones128[:], start=True, stop=True)
        s_pe.inc(i)
    blk.tensor(els_mm)
    pe1 = s_pe.n

    def els_r(v, pe1=pe1):
        v.wait_ge(s_pe.h, pe1)
        i = v.reciprocal(out=scal[0:1, 0:1], in_=psD[0:1, 300:301])
        s_dve.inc(i)
    blk.vector(els_r)
    dv1 = s_dve.n

    def els_rep(t, dv1=dv1):
        t.wait_ge(s_dve.h, dv1)
        i = t.matmul(out=psD[:, 301:302], lhsT=onesf[:], rhs=scal[0:1, 0:1],
                     start=True, stop=True)
        s_pe.inc(i)
    blk.tensor(els_rep)
    pe1 = s_pe.n

    def els_rep_ev(v, pe1=pe1):
        v.wait_ge(s_pe.h, pe1)
        i = v.tensor_copy(out=rsum[:], in_=psD[:, 301:302])
        s_dve.inc(i)
    blk.vector(els_rep_ev)
    rsum_ready = s_dve.n

    ld_wr = ctl_dma(wstg[0:64, 512:512 + D], WR[:, :])

    def wr_cast(s, ld_wr=ld_wr):
        s.wait_ge(s_ctl.h, ld_wr)
        s.wait_ge(s_pe.h, s_pe.n)
        i = s.activation(out=e_tp[:, 0:1024], in_=wstg[0:64, 512:512 + D],
                         func=AF.Copy)
        s_act.inc(i)
    blk.scalar(wr_cast)
    act1 = s_act.n

    for c in range(DCH):
        def wr_tr(t, c=c, act1=act1):
            t.wait_ge(s_act.h, act1)
            t.wait_ge(s_dve.h, s_dve.n)
            i = t.transpose(out=psB[:, 0:64], in_=e_tp[:, c * 128:(c + 1) * 128],
                            identity=idb[0:64, 0:64])
            s_pe.inc(i)
        blk.tensor(wr_tr)
        pe1 = s_pe.n

        def wr_ev(v, c=c, pe1=pe1):
            v.wait_ge(s_pe.h, pe1)
            i = v.tensor_copy(out=WrT[:, c * NC:(c + 1) * NC], in_=psB[:, 0:64])
            s_dve.inc(i)
        blk.vector(wr_ev)
    wrT_ready = s_dve.n

    for st in range(8):
        bk = 2 + st % 2

        def tp_mm(t, st=st, bk=bk):
            t.wait_ge(s_dve.h, wrT_ready)
            if st >= 2:
                t.wait_ge(s_act.h, s_act.n)
            for c in range(DCH):
                i = t.matmul(out=bank(bk)[0:64, :], lhsT=WrT[:, c * NC:(c + 1) * NC],
                             rhs=xnT(c)[:, st * 512:(st + 1) * 512],
                             start=(c == 0), stop=(c == DCH - 1))
            s_pe.inc(i)
        blk.tensor(tp_mm)
        pe1 = s_pe.n

        def tp_ev(s, st=st, bk=bk, pe1=pe1):
            s.wait_ge(s_pe.h, pe1)
            s.wait_ge(s_dve.h, s_dve.n)
            i = s.activation(out=e_tp[:, st * 512:(st + 1) * 512],
                             in_=bank(bk)[0:64, :], func=AF.Exp)
            s_act.inc(i)
        blk.scalar(tp_ev)
    etp_ready = s_act.n

    for st in range(8):
        def z_mm(t, st=st):
            t.wait_ge(s_act.h, etp_ready)
            t.wait_ge(s_dve.h, s_dve.n)
            i = t.matmul(out=psD[0:1, :], lhsT=ones64[:],
                         rhs=e_tp[:, st * 512:(st + 1) * 512], start=True, stop=True)
            s_pe.inc(i)
        blk.tensor(z_mm)
        pe1 = s_pe.n

        def z_ev(v, pe1=pe1, lst_free=lst_free):
            v.wait_ge(s_pe.h, pe1)
            v.wait_ge(s_ld.h, lst_free)
            i = v.tensor_copy(out=Lst[:], in_=psD[0:1, :])
            s_dve.inc(i)
        blk.vector(z_ev)
        dv1 = s_dve.n

        def z_st(sy, st=st, dv1=dv1):
            sy.wait_ge(s_dve.h, dv1)
            i = sy.dma_start(out=ZDR[st * 512:(st + 1) * 512], in_=Lst[0:1, :])
            s_ld.incd(i)
        blk.sync(z_st)
        lst_free = s_ld.n

    ld_z = ctl_dma(Z2d[:], ZDR[:].rearrange("(p q) -> p q", p=128),
                   waits=[(s_ld, s_ld.n)])

    def z_fin(v, ld_z=ld_z):
        v.wait_ge(s_ctl.h, ld_z)
        v.wait_ge(s_dve.h, rsum_ready)
        i = v.reciprocal(out=rz2d[:], in_=Z2d[:])
        n1 = s_dve.inc(i)
        v.wait_ge(s_dve.h, n1)
        i = v.tensor_scalar(out=im2d[:],
                            in0=eL2d[:], scalar1=rsum[:, 0:1], scalar2=None,
                            op0=ALU.mult)
        n2 = s_dve.inc(i)
        v.wait_ge(s_dve.h, n2)
        i = v.tensor_tensor(out=im2d[:], in0=im2d[:], in1=rz2d[:], op=ALU.mult)
        s_dve.inc(i)
    blk.vector(z_fin)
    dv1 = s_dve.n

    ctl_dma(IMD[:].rearrange("(p q) -> p q", p=128), im2d[:],
            waits=[(s_dve, dv1)])
    ld_imr = ctl_dma(imr[:],
                     IMD[:].rearrange("(a s) -> a s", a=1).to_broadcast([64, S]),
                     waits=[(s_pe, pe_outer)])
    _maybe_tap("im2d", im2d[:, :])

    def nw_acc(v, ld_imr=ld_imr):
        v.wait_ge(s_ctl.h, ld_imr)
        v.wait_ge(s_act.h, etp_ready)
        i = v.scalar_tensor_tensor(out=e_tp[:], in0=e_tp[:], scalar=1.0, in1=imr[:],
                                   op0=ALU.mult, op1=ALU.mult, accum_out=nwacc[:])
        n1 = s_dve.inc(i)
        v.wait_ge(s_dve.h, n1)
        i = v.tensor_copy(out=inT[:, 64:65], in_=nwacc[:])
        s_dve.inc(i)
    blk.vector(nw_acc)
    dv1 = s_dve.n

    def nws_mm(t, dv1=dv1):
        t.wait_ge(s_dve.h, dv1)
        i = t.matmul(out=psD[0:1, 302:303], lhsT=inT[:, 64:65], rhs=ones64[:],
                     start=True, stop=True)
        s_pe.inc(i)
    blk.tensor(nws_mm)
    pe1 = s_pe.n

    def nws_fin(v, pe1=pe1):
        v.wait_ge(s_pe.h, pe1)
        i = v.tensor_scalar(out=scal[0:1, 2:3], in0=psD[0:1, 302:303],
                            scalar1=EPS_NW, scalar2=None, op0=ALU.add)
        n1 = s_dve.inc(i)
        v.wait_ge(s_dve.h, n1)
        i = v.reciprocal(out=scal[0:1, 3:4], in_=scal[0:1, 2:3])
        s_dve.inc(i)
    blk.vector(nws_fin)
    dv1 = s_dve.n

    def nwr_rep(t, dv1=dv1):
        t.wait_ge(s_dve.h, dv1)
        i = t.matmul(out=psD[0:64, 303:304], lhsT=onesf[:, 0:64],
                     rhs=scal[0:1, 3:4], start=True, stop=True)
        s_pe.inc(i)
    blk.tensor(nwr_rep)
    pe1 = s_pe.n

    def nw_fin(v, pe1=pe1):
        v.wait_ge(s_pe.h, pe1)
        i = v.scalar_tensor_tensor(out=nw64[:], in0=nwacc[:], scalar=1.0,
                                   in1=psD[0:64, 303:304], op0=ALU.mult, op1=ALU.mult)
        s_dve.inc(i)
    blk.vector(nw_fin)
    dv1 = s_dve.n
    _maybe_tap("nw64", nw64[:, :])

    def nwt_tr(t, dv1=dv1):
        t.wait_ge(s_dve.h, dv1)
        i = t.transpose(out=psD[0:1, 304:368], in_=nw64[:], identity=idf[0:64, 0:64])
        s_pe.inc(i)
    blk.tensor(nwt_tr)
    pe1 = s_pe.n

    def nwrow_ev(v, pe1=pe1, lst_free=lst_free):
        v.wait_ge(s_pe.h, pe1)
        v.wait_ge(s_ld.h, lst_free)
        i = v.tensor_copy(out=Lst[0:1, 0:64], in_=psD[0:1, 304:368])
        s_dve.inc(i)
    blk.vector(nwrow_ev)
    dv1 = s_dve.n

    def nwrep_mm(t, dv1=dv1):
        t.wait_ge(s_dve.h, dv1)
        i = t.matmul(out=psD[:, 368:432], lhsT=onesf[:], rhs=Lst[0:1, 0:64],
                     start=True, stop=True)
        s_pe.inc(i)
    blk.tensor(nwrep_mm)
    pe1 = s_pe.n

    def nwrep_ev(v, pe1=pe1):
        v.wait_ge(s_pe.h, pe1)
        i = v.tensor_copy(out=nwrep[:], in_=psD[:, 368:432])
        s_dve.inc(i)
    blk.vector(nwrep_ev)
    nwrep_ready = s_dve.n


    ffn_pe_done = s_pe.n

    kdef = list(range(0, 64)) + list(range(160, 256))
    kdef_waits = [(s_pe, s_pe.n), (s_dve, s_dve.n)]
    k4_tr = {-2: s_pe.n, -1: s_pe.n}
    k4_i = [0]

    def emit_k4(g4):
        kt0 = kdef[4 * g4]
        kb = 4 + (kt0 // 4) % 2
        kp = k4_i[0] % 2
        k4_i[0] += 1

        def k4_ld(sy, kt0=kt0, kp=kp, slot_gate=k4_tr.get(k4_i[0] - 3, 0)):
            if slot_gate:
                sy.wait_ge(s_pe.h, slot_gate)
            i = sy.dma_start(
                out=kst4[:, kp * 512:(kp + 1) * 512].rearrange("p (g r) -> p g r", g=4),
                in_=KD[kt0 * 128:(kt0 + 4) * 128, :].rearrange("(g p) r -> p g r", g=4))
            s_k2[kp].incd(i)
        blk.sync(k4_ld)
        ld_k = s_k2[kp].n

        def k4_tr_f(t, kt0=kt0, kb=kb, kp=kp, ld_k=ld_k,
                    gate=kev_bank[4 + (kt0 // 4) % 2]):
            t.wait_ge(s_k2[kp].h, ld_k)
            if gate:
                t.wait_ge(s_act.h, gate)
            for g in range(4):
                i = t.transpose(out=bank(kb)[:, g * 128:(g + 1) * 128],
                                in_=kst4[:, kp * 512 + g * 128:kp * 512 + (g + 1) * 128],
                                identity=idf[:])
            s_pe.inc(i)
        blk.tensor(k4_tr_f)
        k4_tr[k4_i[0] - 1] = s_pe.n
        pe_k = s_pe.n

        def k4_ev(s, kt0=kt0, kb=kb, pe_k=pe_k, ev_waits=tuple(kdef_waits)):
            s.wait_ge(s_pe.h, pe_k)
            for sem, cnt in ev_waits:
                s.wait_ge(sem.h, cnt)
            i = s.activation(out=KT[:, kt0 * 128:kt0 * 128 + 512],
                             in_=bank(kb), func=AF.Copy)
            s_act.inc(i)
        blk.scalar(k4_ev)
        kev_bank[kb] = s_act.n

    sc_cons = {}
    for n in range(NC):
        slot = n % 2

        def sc_ld(sy, n=n, slot=slot):
            if n < 2:
                sy.wait_ge(s_dve.h, nwrep_ready)
                sy.wait_ge(s_ctl.h, ld_imr)
                sy.wait_ge(s_act.h, etp_ready)
            else:
                sy.wait_ge(s_dve.h, sc_cons[n - 2])
            i = sy.dma_start(out=cbuf[:, slot * 1024:(slot + 1) * 1024]
                             .rearrange("p (q r) -> p q r", q=DCH),
                             in_=CP[n, :, :].rearrange("(q p) r -> p q r", q=DCH))
            s_c2[slot].incd(i)
        blk.sync(sc_ld)
        ld_c = s_c2[slot].n

        def sc_acc(v, n=n, slot=slot, ld_c=ld_c):
            v.wait_ge(s_c2[slot].h, ld_c)
            if n:
                v.wait_ge(s_dve.h, s_dve.n)
            for c in range(DCH):
                src = cbuf[:, slot * 1024 + c * 128:slot * 1024 + (c + 1) * 128]
                dst = SCm[:, c * 128:(c + 1) * 128]
                if n == 0:
                    i = v.tensor_scalar(out=dst, in0=src, scalar1=nwrep[:, n:n + 1],
                                        scalar2=None, op0=ALU.mult)
                else:
                    i = v.scalar_tensor_tensor(out=dst, in0=src,
                                               scalar=nwrep[:, n:n + 1], in1=dst,
                                               op0=ALU.mult, op1=ALU.add)
            s_dve.inc(i)
        blk.vector(sc_acc)
        sc_cons[n] = s_dve.n
        if n < 40:
            emit_k4(n)
    kt_ready = s_act.n

    def sc_cast(v):
        v.wait_ge(s_dve.h, s_dve.n)
        i = v.tensor_copy(out=SCb[:], in_=SCm[:])
        s_dve.inc(i)
    blk.vector(sc_cast)
    scb_ready = s_dve.n
    _maybe_tap("SCm", SCm[:, :])

    qt_bank_free = {0: 0, 1: 0}
    for st in range(8):
        bk = st % 2

        def q_mm(t, st=st, bk=bk, gate=qt_bank_free[st % 2]):
            t.wait_ge(s_dve.h, scb_ready)
            if gate:
                t.wait_ge(s_act.h, gate)
            for c in range(DCH):
                i = t.matmul(out=bank(bk), lhsT=SCb[:, c * 128:(c + 1) * 128],
                             rhs=xnT(c)[:, st * 512:(st + 1) * 512],
                             start=(c == 0), stop=(c == DCH - 1))
            s_pe.inc(i)
        blk.tensor(q_mm)
        pe1 = s_pe.n

        def q_ev(s, st=st, bk=bk, pe1=pe1):
            s.wait_ge(s_pe.h, pe1)
            i = s.activation(out=QT[:, st * 512:(st + 1) * 512], in_=bank(bk),
                             func=AF.Copy, scale=ckon[:, 1:2])
            s_act.inc(i)
        blk.scalar(q_ev)
        qt_bank_free[bk] = s_act.n
    qt_ready = s_act.n
    _maybe_tap("QT", QT[:, :])

    sc_bank_free = [0] * 6
    half_read = {}
    wsum_done = {}
    w8_ready = {}
    gidx_ready = {}
    nmax_n = {}
    exp_n = {}
    pool_ready = {}
    base_all = (s_oe.n, s_oo.n)

    def emit_wsum(tt):
        def d_ws(v, tt=tt):
            par = tt % 2
            acc = xb[:, par * D:(par + 1) * D]
            v.wait_ge(s_act.h, exp_n[tt])
            i = v.tensor_scalar(out=d32[:, 0:8], in0=e8[:], scalar1=1.0,
                                scalar2=None, op0=ALU.mult, op1=ALU.add,
                                accum_out=sm1[:, 0:1])
            n1 = s_dve.inc(i)
            v.wait_ge(s_dve.h, n1)
            i = v.reciprocal(out=sm1[:, 1:2], in_=sm1[:, 0:1])
            n2 = s_dve.inc(i)
            v.wait_ge(s_dve.h, n2)
            i = v.tensor_scalar(out=w8[:], in0=e8[:], scalar1=sm1[:, 1:2],
                                scalar2=None, op0=ALU.mult)
            w8_ready[tt] = s_dve.inc(i)
            v.wait_ge(s_dve.h, w8_ready[tt])
            v.wait_ge(s_pl.h, pool_ready[tt])
            v.wait_ge(opar[tt % 2].h, opar[tt % 2].n)
            for k in range(KK):
                if k:
                    v.wait_ge(s_dve.h, s_dve.n)
                i = v.scalar_tensor_tensor(
                    out=acc, in0=Vg[:].rearrange("p (k d) -> p k d", k=KK)[:, k, :],
                    scalar=w8[:, k:k + 1],
                    in1=(bsb[:, par * D:(par + 1) * D] if k == 0 else acc),
                    op0=ALU.mult, op1=ALU.add)
                s_dve.inc(i)
        blk.vector(d_ws)
        wsum_done[tt] = s_dve.n

    def emit_out(tt):
        def p_out(g, tt=tt, wn=wsum_done[tt]):
            g.wait_ge(s_dve.h, wn)
            par = tt % 2
            i = g.dma_start(out=OUT[tt * 128:(tt + 1) * 128, :],
                            in_=xb[:, par * D:(par + 1) * D])
            opar[par].incd(i)
        blk.gpsimd(p_out)

    NP4 = N_P4[0]
    for tt in range(NP4):

        for j in range(64):
            b = j % 6
            q = j // 16
            qs = q % 2

            def s_mm(t, tt=tt, j=j, b=b, gate=sc_bank_free[j % 6]):
                if tt == 0 and j == 0:
                    t.wait_ge(s_act.h, qt_ready)
                    t.wait_ge(s_act.h, kt_ready)
                if gate:
                    t.wait_ge(s_act.h, gate)
                i = t.matmul(out=bank(b), lhsT=QT[:, tt * 128:(tt + 1) * 128],
                             rhs=KT[:, j * 512:(j + 1) * 512], start=True, stop=True)
                s_pe.inc(i)
            blk.tensor(s_mm)
            mm_n = s_pe.n

            war = (half_read.get((tt, q - 2), 0) if q >= 2
                   else half_read.get((tt - 1, q + 2), 0))

            def s_ev(s, tt=tt, j=j, b=b, q=q, qs=qs, mm_n=mm_n, war=war):
                s.wait_ge(s_pe.h, mm_n)
                if war:
                    s.wait_ge(s_dve.h, war)
                i = s.activation(out=scq[:, qs * 8192 + (j % 16) * 512:
                                         qs * 8192 + (j % 16) * 512 + 512],
                                 in_=bank(b), func=AF.Copy)
                s_act.inc(i)
            blk.scalar(s_ev)
            sc_bank_free[b] = s_act.n

            if j == 15 and tt > 0:
                def a_exp(s, p=tt - 1):
                    s.wait_ge(s_dve.h, nmax_n[p])
                    i = s.activation(out=e8[:], in_=gvf[:], func=AF.Exp,
                                     bias=nmax[:, 0:1])
                    s_act.inc(i)
                blk.scalar(a_exp)
                exp_n[tt - 1] = s_act.n

            if j % 16 == 15:
                qe = s_act.n

                def d_q(v, tt=tt, q=q, qs=qs, qe=qe):
                    v.wait_ge(s_act.h, qe)
                    i = v.max(out=cman[:, 8 * q:8 * q + 8],
                              in_=scq[:, qs * 8192:(qs + 1) * 8192])
                    n1 = s_dve.inc(i)
                    v.wait_ge(s_dve.h, n1)
                    i = v.max_index(out=cmi[:, 8 * q:8 * q + 8],
                                    in_max=cman[:, 8 * q:8 * q + 8],
                                    in_values=scq[:, qs * 8192:(qs + 1) * 8192])
                    half_read[(tt, q)] = s_dve.inc(i)
                blk.vector(d_q)
                if q == 1 and tt > 0:
                    emit_wsum(tt - 1)

        def d_top(v, tt=tt):
            v.wait_ge(s_dve.h, s_dve.n)
            i = v.max(out=gv[:], in_=cman[:])
            n4 = s_dve.inc(i)
            v.wait_ge(s_dve.h, n4)
            i = v.max_index(out=pos[:], in_max=gv[:], in_values=cman[:])
            n5 = s_dve.inc(i)
            v.wait_ge(s_dve.h, n5)
            v.tensor_scalar(out=posf[:], in0=pos[:], scalar1=0.0, scalar2=None,
                            op0=ALU.add)
            i = v.tensor_scalar(out=cgi[:], in0=cmi[:], scalar1=0.0, scalar2=None,
                                op0=ALU.add)
            n6 = s_dve.inc(i)
            v.wait_ge(s_dve.h, n6)
            i = v.tensor_tensor(out=cgi[:], in0=cgi[:], in1=hb32[:], op=ALU.add)
            n7 = s_dve.inc(i)
            v.wait_ge(s_dve.h, n7)
            for k in range(KK):
                i = v.scalar_tensor_tensor(out=d32[:], in0=io32[:],
                                           scalar=posf[:, k:k + 1], in1=cgi[:],
                                           op0=ALU.is_equal, op1=ALU.mult,
                                           accum_out=gixf[:, k:k + 1])
            n8 = s_dve.inc(i)
            v.wait_ge(s_dve.h, n8)
            i = v.tensor_scalar(out=gixu[:], in0=gixf[:], scalar1=0.0,
                                scalar2=None, op0=ALU.add)
            gidx_ready[tt] = s_dve.inc(i)
            i = v.tensor_scalar(out=gvf[:], in0=gv[:], scalar1=0.0, scalar2=None,
                                op0=ALU.add)
            n9 = s_dve.inc(i)
            v.wait_ge(s_dve.h, n9)
            i = v.tensor_scalar(out=nmax[:], in0=gvf[:, 0:1], scalar1=-1.0,
                                scalar2=None, op0=ALU.mult)
            nmax_n[tt] = s_dve.inc(i)
        blk.vector(d_top)

        for k in range(KK):
            def p_g(g, tt=tt, k=k):
                if k == 0:
                    g.wait_ge(s_dve.h, gidx_ready[tt])
                    if tt > 0:
                        g.wait_ge(s_dve.h, wsum_done[tt - 1])
                i = g.indirect_dma_start(
                    out=Vg[:].rearrange("p (k d) -> p k d", k=KK)[:, k, :],
                    out_offset=None, in_=VD[:, :],
                    in_offset=IndirectOffsetOnAxis(ap=gixu[:, k:k + 1], axis=0))
                s_pl.incd(i)
            blk.gpsimd(p_g)

        def p_b(g, tt=tt, base_all=base_all):
            g.wait_ge(s_oe.h, base_all[0])
            g.wait_ge(s_oo.h, base_all[1])
            if tt >= 2:
                g.wait_ge(s_dve.h, wsum_done[tt - 2])
            i = g.dma_start(out=bsb[:, (tt % 2) * D:(tt % 2 + 1) * D],
                            in_=BASE[tt * 128:(tt + 1) * 128, :])
            s_pl.incd(i)
        blk.gpsimd(p_b)
        pool_ready[tt] = s_pl.n

        if tt > 0:
            emit_out(tt - 1)

    def a_exp_last(s):
        s.wait_ge(s_dve.h, nmax_n[NP4 - 1])
        i = s.activation(out=e8[:], in_=gvf[:], func=AF.Exp, bias=nmax[:, 0:1])
        s_act.inc(i)
    blk.scalar(a_exp_last)
    exp_n[NP4 - 1] = s_act.n
    emit_wsum(NP4 - 1)
    emit_out(NP4 - 1)
    _maybe_tap("gv", gv[:, :])
    _maybe_tap("cmi", cmi[:, :])
    _maybe_tap("cman", cman[:, :])
    _maybe_tap("pos", pos[:, :])
    _maybe_tap("gixf", gixf[:, :])
    _maybe_tap("gixu", gixu[:, :])
    _maybe_tap("e8", e8[:, :])
    _maybe_tap("w8", w8[:, :])
    _maybe_tap("bsb", bsb[:, :])
    _maybe_tap("Vg", Vg[:, :])
    _maybe_tap("cgi", cgi[:, :])
    _maybe_tap("KTt", KT[:, :])
    _maybe_tap("scq", scq[:, :])

    def fin(g):
        g.wait_ge(s_oe.h, s_oe.n)
        g.wait_ge(s_oo.h, s_oo.n)
        g.wait_ge(s_out.h, s_out.n)
        g.wait_ge(s_pl.h, s_pl.n)
        g.wait_ge(s_ctl.h, s_ctl.n)
    blk.gpsimd(fin)

    ctx.close()
    return nc


_TAP_SHAPES = {
    "gv": ([128, 8], BF16),
    "cmi": ([128, 32], U16),
    "cman": ([128, 32], BF16),
    "pos": ([128, 8], U16),
    "gixf": ([128, 8], F32),
    "gixu": ([128, 8], U32),
    "e8": ([128, 8], F32),
    "w8": ([128, 8], F32),
    "bsb": ([128, 2048], F32),
    "Vg": ([128, 8 * 1024], F32),
    "cgi": ([128, 32], F32),
    "KTt": ([128, NK], BF16),
    "scq": ([128, 2 * 8192], BF16),
    "Apow": ([64, 64 * SD], BF16),
    "uT": ([64, S], BF16),
    "hvec": ([64, 1], F32),
    "im2d": ([128, 32], BF16),
    "nw64": ([64, 1], F32),
    "SCm": ([128, DCH * 128], F32),
    "QT": ([128, S], BF16),
}


def kernel(L0_attn_x, L0_attn_A, L0_attn_Bm, L0_attn_W_imp, L0_attn_W_router,
           L0_attn_compress, L0_attn_K, L0_attn_V, L0_ffn_weight):
    x = np.ascontiguousarray(np.asarray(L0_attn_x, dtype=np.float32))
    com = {
        "A": np.ascontiguousarray(np.asarray(L0_attn_A, dtype=np.float32)),
        "Bm": np.ascontiguousarray(np.asarray(L0_attn_Bm, dtype=np.float32)),
        "Wimp": np.ascontiguousarray(np.asarray(L0_attn_W_imp, dtype=np.float32)),
        "Wrout": np.ascontiguousarray(np.asarray(L0_attn_W_router, dtype=np.float32)),
        "comp": np.ascontiguousarray(np.asarray(L0_attn_compress, dtype=np.float32)),
        "K": np.ascontiguousarray(np.asarray(L0_attn_K, dtype=np.float32)),
        "V": np.ascontiguousarray(np.asarray(L0_attn_V, dtype=np.float32)),
        "Wf": np.ascontiguousarray(np.asarray(L0_ffn_weight, dtype=np.float32)),
    }
    nc = build_program()
    in_maps = [dict(com, x=np.ascontiguousarray(x[b])) for b in range(B)]
    res = run_bass_kernel_spmd(nc, in_maps, core_ids=list(range(B)))
    out = np.stack([np.asarray(res.results[b]["out"]) for b in range(B)], axis=0)
    return out
